# version 10
# baseline (speedup 1.0000x reference)
import contextlib
import math
import numpy as np
import ml_dtypes
import concourse.bass as bass
import concourse.mybir as mybir
from concourse.bass_utils import run_bass_kernel_spmd
from concourse.ap import AP

F32 = mybir.dt.float32
BF16 = mybir.dt.bfloat16
I32 = mybir.dt.int32
AF = mybir.ActivationFunctionType
ALU = mybir.AluOpType
AX = mybir.AxisListType

S = 2048
D = 1024
TT = 16
FC = 8
NEG = -1.0e30

ENGINES = ("sync", "tensor", "vector", "scalar", "gpsimd")


class Sem:
    def __init__(self, handle):
        self.h = handle
        self.n = 0


class SP:
    def __init__(self):
        self.streams = {e: [] for e in ENGINES}

    def on(self, engine, fn):
        self.streams[engine].append(fn)


class Arena:
    def __init__(self, handle, cols):
        self.h = handle
        self.cols = cols
        self.off = 0
        self.marks = []

    def alloc(self, n, align=32):
        o = (self.off + align - 1) // align * align
        assert o + n <= self.cols, f"arena overflow: {o}+{n} > {self.cols}"
        self.off = o + n
        return self.h[:, o:o + n]

    def push(self):
        self.marks.append(self.off)

    def pop(self):
        self.off = self.marks.pop()


def cdiv(a, b):
    return (a + b - 1) // b


class Ctx:
    def S(self, name):
        if name not in self.sem:
            h = self.exits.enter_context(self.nc.semaphore("s_" + name))
            self.sem[name] = Sem(h)
        return self.sem[name]


def make_ctx(nc, exits):
    c = Ctx()
    c.nc = nc
    c.exits = exits
    c.sem = {}
    ent = exits.enter_context
    c.ln_fm = ent(nc.sbuf_tensor("ln_fm", [128, FC * S], BF16))
    c.ident = ent(nc.sbuf_tensor("c_ident", [128, 128], BF16))
    c.ones_row = ent(nc.sbuf_tensor("c_ones", [1, 512], BF16))
    ab_cols = 57344
    af_cols = 8192
    c.ab = Arena(ent(nc.sbuf_tensor("arena_bf", [128, ab_cols], BF16)), ab_cols)
    c.af = Arena(ent(nc.sbuf_tensor("arena_f32", [128, af_cols], F32)), af_cols)
    c.stat = ent(nc.sbuf_tensor("stat", [128, 96], F32))
    c.sq_scr = ent(nc.sbuf_tensor("sq_scr", [128, D], BF16))
    c.xbuf = ent(nc.sbuf_tensor("xbuf", [128, 2 * D], F32))
    c.xnew = ent(nc.sbuf_tensor("xnew", [128, 2 * D], F32))
    c.lntm = ent(nc.sbuf_tensor("lntm", [128, 2 * D], BF16))
    c.bias_row = ent(nc.sbuf_tensor("bias_row", [1, 4096], BF16))
    c.bias_col = ent(nc.sbuf_tensor("bias_col", [128, 32], F32))
    c.ps = [ent(nc.psum_tensor(f"ps{i}", [128, 512], F32)) for i in range(6)]
    c.pt = [ent(nc.psum_tensor(f"pt{i}", [128, 1024], BF16)) for i in range(2)]
    for name in ("mm", "ev", "ac", "gp", "vch", "ach"):
        c.S(name)
    c.ps_free = [("ev", 0)] * 6
    c.pt_free = [("ev", 0)] * 2
    return c


def wait(eng, c, key, v):
    if v > 0:
        eng.wait_ge(c.sem[key].h, v)


def dma(sp, c, queue, sem_name, pairs, gates=()):
    sm = c.S(sem_name)
    sm.n += 16 * len(pairs)
    tot = sm.n

    eg = tuple(getattr(c, "entry_gates", ()))

    def fn(g, pairs=pairs, gates=tuple(gates), eg=eg):
        for (k, v) in eg:
            wait(g, c, k, v)
        for (k, v) in gates:
            wait(g, c, k, v)
        for (out_ap, in_ap) in pairs:
            g.dma_start(out=out_ap, in_=in_ap).then_inc(sm.h, 16)
    sp.on(queue, fn)
    return tot


def emit_prepass(sp, c, pairs, sem="prep", chunk_rows=512):
    dmas = []
    for (dst, src) in pairs:
        rows = src.shape[0]
        step = min(chunk_rows, rows)
        for r0 in range(0, rows, step):
            r1 = min(rows, r0 + step)
            dmas.append((dst[r0:r1], src[r0:r1]))
    tot = dma(sp, c, "gpsimd", sem, dmas)
    return (sem, tot)


def emit_const_init(sp, c, ident_dram):
    sev = c.S("ev")
    idv = dma(sp, c, "sync", "aux", [(c.ident[:, :], ident_dram)])
    sev.n += 1
    done = sev.n

    def ones(v):
        v.memset(c.ones_row[:, :], 1.0)
        v.memset(c.bias_col[:, 0:1], 1e-5).then_inc(sev.h, 1)
    sp.on("vector", ones)

    def bar(te, idv=idv, done=done):
        te.wait_ge(c.sem["aux"].h, idv)
        te.wait_ge(sev.h, done)
    sp.on("tensor", bar)
    c.const_ready = done


def emit_ln(sp, c, x_dram, x_tile_ready=None):
    sac, sev, smm = c.S("ac"), c.S("ev"), c.S("mm")
    eps = 1e-5
    ln2_done = [0] * TT
    for t in range(TT):
        buf = t % 2
        sb = buf * 16
        xsl = c.xbuf[:, buf * D:(buf + 1) * D]
        lnsl = c.lntm[:, buf * D:(buf + 1) * D]

        gates = []
        if x_tile_ready is not None and x_tile_ready[t] is not None:
            gates.append(x_tile_ready[t])
        if t >= 2:
            gates.append(("ev", ln2_done[t - 2]))
        xv = dma(sp, c, "sync", f"xl{buf}",
                 [(xsl, x_dram[t * 128:(t + 1) * 128, :])], gates=gates)

        stat_war = 0 if t < 2 else ln2_done[t - 2]
        sac.n += 1
        sd = sac.n

        def stats(a, xsl=xsl, xv=xv, sb=sb, war=stat_war, buf=buf, lnsl_=lnsl):
            a.wait_ge(c.sem[f"xl{buf}"].h, xv)
            wait(a, c, "ev", war)
            a.activation(c.sq_scr[:, :], xsl, AF.Square,
                         accum_out=c.stat[:, sb + 0:sb + 1])
            a.activation(lnsl_, xsl, AF.Identity,
                         accum_out=c.stat[:, sb + 1:sb + 2]).then_inc(sac.h, 1)
        sp.on("scalar", stats)

        vch = c.S("vch")
        vb = vch.n
        vch.n += 3
        sev.n += 1
        v1 = sev.n

        def lnop(v, sb=sb, sd=sd, vb=vb):
            v.wait_ge(sac.h, sd)
            st = c.stat
            v.tensor_mul(st[:, sb + 2:sb + 3], st[:, sb + 1:sb + 2],
                         st[:, sb + 1:sb + 2]).then_inc(vch.h, 1)
            v.tensor_scalar_mul(st[:, sb + 3:sb + 4], st[:, sb + 1:sb + 2],
                                1.0 / D).then_inc(vch.h, 1)
            v.wait_ge(vch.h, vb + 1)
            v.scalar_tensor_tensor(st[:, sb + 4:sb + 5], st[:, sb + 2:sb + 3],
                                   1.0 / D, st[:, sb + 0:sb + 1],
                                   ALU.mult, ALU.subtract).then_inc(sev.h, 1)
        sp.on("vector", lnop)

        sac.n += 1
        rs = sac.n

        def rstd(a, sb=sb, v1=v1):
            a.wait_ge(sev.h, v1)
            a.activation(c.stat[:, sb + 5:sb + 6], c.stat[:, sb + 4:sb + 5],
                         AF.Sqrt, scale=-1.0 / D,
                         bias=c.bias_col[:, 0:1]).then_inc(sac.h, 1)
        sp.on("scalar", rstd)

        sev.n += 1
        ln2_done[t] = sev.n

        def lnop2(v, xsl=xsl, lnsl=lnsl, sb=sb, rs=rs, vb=vb):
            v.wait_ge(sac.h, rs)
            v.reciprocal(c.stat[:, sb + 6:sb + 7],
                         c.stat[:, sb + 5:sb + 6]).then_inc(vch.h, 1)
            v.wait_ge(vch.h, vb + 3)
            v.tensor_scalar(lnsl, xsl, c.stat[:, sb + 3:sb + 4],
                            c.stat[:, sb + 6:sb + 7],
                            ALU.subtract, ALU.mult).then_inc(sev.h, 1)
        sp.on("vector", lnop2)

        for cc in range(FC):
            pb = (t * FC + cc) % 2
            smm.n += 1
            trv = smm.n
            ptf = c.pt_free[pb]

            def tr(te, lnsl=lnsl, cc=cc, pb=pb, l2=ln2_done[t], ptf=ptf):
                te.wait_ge(sev.h, l2)
                wait(te, c, ptf[0], ptf[1])
                te.transpose(c.pt[pb][:, 0:128], lnsl[:, cc * 128:(cc + 1) * 128],
                             c.ident[:, :]).then_inc(smm.h, 1)
            sp.on("tensor", tr)

            sev.n += 1
            c.pt_free[pb] = ("ev", sev.n)

            def ev(v, cc=cc, t=t, pb=pb, trv=trv):
                v.wait_ge(smm.h, trv)
                v.tensor_copy(c.ln_fm[:, cc * S + t * 128: cc * S + (t + 1) * 128],
                              c.pt[pb][:, 0:128]).then_inc(sev.h, 1)
            sp.on("vector", ev)
    return sev.n


def emit_store_tile(sp, c, xo_dram, t, xnsl, ready_ev):
    stv = dma(sp, c, "sync", f"st{t % 2}",
              [(xo_dram[t * 128:(t + 1) * 128, :], xnsl)],
              gates=[("ev", ready_ev)])
    return (f"st{t % 2}", stv)


def emit_ffn_l0(sp, c, wgT, wuT, wdT, x_dram, xo_dram, ln_ready, prep_gate=None):
    nc = c.nc
    smm, sev, sac = c.S("mm"), c.S("ev"), c.S("ac")
    HB = 24
    c.ab.push()
    h_fm = c.ab.alloc(HB * 512)
    sg2 = [c.ab.alloc(512), c.ab.alloc(512)]
    wg_b = [c.ab.alloc(FC * 128) for _ in range(2)]
    wu_b = [c.ab.alloc(FC * 128) for _ in range(2)]
    wd_b = c.ab.alloc(HB * 1024)
    x_out_ready = [None] * TT

    wd_done = dma(sp, c, "sync", "fwd",
                  [(wd_b[:, m * 1024:(m + 1) * 1024], wdT[m * 128:(m + 1) * 128, :])
                   for m in range(HB)],
                  gates=[prep_gate] if prep_gate else [])

    mm_done_slot = [0, 0]
    hmul_done = {}
    down_done = 0
    evac_seq = 0
    for n in range(4):
        tok = slice(n * 512, (n + 1) * 512)
        for m in range(HB):
            dbl = m % 2
            war = [("mm", mm_done_slot[dbl])] if mm_done_slot[dbl] else []
            lg = dma(sp, c, "sync", f"fw{dbl}", [
                (wg_b[dbl].rearrange("p (c m) -> p c m", c=FC),
                 wgT[:, m * 128:(m + 1) * 128].rearrange("(c p) m -> p c m", p=128)),
                (wu_b[dbl].rearrange("p (c m) -> p c m", c=FC),
                 wuT[:, m * 128:(m + 1) * 128].rearrange("(c p) m -> p c m", p=128)),
            ], gates=war)
            pg = 2 * dbl
            pu = pg + 1
            gate_free = c.ps_free[pg]
            up_free = c.ps_free[pu]
            smm.n += 2
            mm_v = smm.n
            mm_done_slot[dbl] = mm_v

            def mms(te, dbl=dbl, lg=lg, tok=tok, pg=pg, pu=pu,
                    gf=gate_free, uf=up_free, lr=ln_ready):
                te.wait_ge(c.sem[f"fw{dbl}"].h, lg)
                te.wait_ge(sev.h, lr)
                wait(te, c, gf[0], gf[1])
                wait(te, c, uf[0], uf[1])
                for cc in range(FC):
                    ins = te.matmul(c.ps[pg][:, :], wg_b[dbl][:, cc * 128:(cc + 1) * 128],
                                    c.ln_fm[:, cc * S:(cc + 1) * S][:, tok],
                                    start=(cc == 0), stop=(cc == FC - 1))
                    if cc == FC - 1:
                        ins.then_inc(smm.h, 1)
                for cc in range(FC):
                    ins = te.matmul(c.ps[pu][:, :], wu_b[dbl][:, cc * 128:(cc + 1) * 128],
                                    c.ln_fm[:, cc * S:(cc + 1) * S][:, tok],
                                    start=(cc == 0), stop=(cc == FC - 1))
                    if cc == FC - 1:
                        ins.then_inc(smm.h, 1)
            sp.on("tensor", mms)

            sac.n += 1
            sa = sac.n
            sg = sg2[dbl]
            sg_war = hmul_done.get((n, m - 2), hmul_done.get((n - 1, HB - 2 + dbl), 0))

            def silu(a, pg=pg, mm_v=mm_v, sg=sg, war=sg_war):
                a.wait_ge(smm.h, mm_v - 1)
                wait(a, c, "ev", war)
                a.activation(sg, c.ps[pg][:, :], AF.Silu,
                             scale=1.0 / 127.0).then_inc(sac.h, 1)
            sp.on("scalar", silu)
            c.ps_free[pg] = ("ac", sa)

            sev.n += 1
            hv = sev.n
            hmul_done[(n, m)] = hv

            def hmul(v, m=m, pu=pu, sa=sa, mm_v=mm_v, sg=sg, dd=down_done):
                v.wait_ge(sac.h, sa)
                v.wait_ge(smm.h, mm_v)
                wait(v, c, "mm", dd)
                v.scalar_tensor_tensor(h_fm[:, m * 512:(m + 1) * 512],
                                       c.ps[pu][:, :], 1.0 / 127.0, sg,
                                       ALU.mult, ALU.mult).then_inc(sev.h, 1)
            sp.on("vector", hmul)
            c.ps_free[pu] = ("ev", hv)
        h_done = sev.n

        for tt_i in range(4):
            t = n * 4 + tt_i
            buf = t % 2
            xsl = c.xbuf[:, buf * D:(buf + 1) * D]
            xnsl = c.xnew[:, buf * D:(buf + 1) * D]
            gates = [("ev", ln_ready)]
            if t >= 2:
                gates.append(x_out_ready[t - 2])
            xv = dma(sp, c, "sync", f"xl{buf}",
                     [(xsl, x_dram[t * 128:(t + 1) * 128, :])], gates=gates)

            for e in range(2):
                pb = 4 + (evac_seq % 2)
                evac_seq += 1
                bf = c.ps_free[pb]
                smm.n += 1
                mm_v = smm.n

                def mmd(te, tt_i=tt_i, e=e, pb=pb, bf=bf, hd=h_done, wd=wd_done):
                    te.wait_ge(sev.h, hd)
                    te.wait_ge(c.sem["fwd"].h, wd)
                    wait(te, c, bf[0], bf[1])
                    for m in range(HB):
                        ins = te.matmul(c.ps[pb][:, :],
                                        h_fm[:, m * 512 + tt_i * 128: m * 512 + (tt_i + 1) * 128],
                                        wd_b[:, m * 1024 + e * 512: m * 1024 + (e + 1) * 512],
                                        start=(m == 0), stop=(m == HB - 1))
                    ins.then_inc(smm.h, 1)
                sp.on("tensor", mmd)
                down_done = smm.n

                sev.n += 1
                rv = sev.n
                c.ps_free[pb] = ("ev", rv)

                def resid(v, e=e, pb=pb, mm_v=mm_v, xv=xv, xsl=xsl, xnsl=xnsl, buf=buf):
                    v.wait_ge(smm.h, mm_v)
                    v.wait_ge(c.sem[f"xl{buf}"].h, xv)
                    v.scalar_tensor_tensor(
                        xnsl[:, e * 512:(e + 1) * 512], c.ps[pb][:, :],
                        1.0 / 127.0, xsl[:, e * 512:(e + 1) * 512],
                        ALU.mult, ALU.add).then_inc(sev.h, 1)
                sp.on("vector", resid)
            x_out_ready[t] = emit_store_tile(sp, c, xo_dram, t, xnsl, sev.n)
    c.ab.pop()
    return x_out_ready



def emit_bias_loads(sp, c, items):
    pairs = [(c.bias_row[0:1, off:off + n], ap[None, :]) for (ap, off, n) in items]
    return dma(sp, c, "gpsimd", "aux2", pairs)


def emit_fm_group(sp, c, ps_bank, lhs_tiles, rhs_tiles, bias_lhs=None, bias_rhs=None,
                  waits=(), tag=""):
    smm = c.S("mm")
    bf = c.ps_free[ps_bank]
    smm.n += 1
    mm_v = smm.n
    n_k = len(lhs_tiles)

    M = lhs_tiles[0].shape[-1]

    def fn(te, waits=tuple(waits), bf=bf, M=M):
        for (k, v) in waits:
            wait(te, c, k, v)
        wait(te, c, bf[0], bf[1])
        first = True
        if bias_lhs is not None:
            te.matmul(c.ps[ps_bank][0:M, 0:bias_rhs.shape[-1]], bias_lhs, bias_rhs,
                      start=True, stop=False)
            first = False
        for i in range(n_k):
            ins = te.matmul(c.ps[ps_bank][0:M, 0:rhs_tiles[i].shape[-1]],
                            lhs_tiles[i], rhs_tiles[i],
                            start=first, stop=(i == n_k - 1))
            first = False
        ins.then_inc(smm.h, 1)
    sp.on("tensor", fn)
    return mm_v


def emit_evac_copy(sp, c, ps_bank, width, out_ap, mm_v, engine="vector"):
    if engine == "vector":
        sev = c.S("ev")
        sev.n += 1
        vv = sev.n

        def fn(v, mm_v=mm_v):
            v.wait_ge(c.sem["mm"].h, mm_v)
            v.tensor_copy(out_ap, c.ps[ps_bank][:, 0:width]).then_inc(sev.h, 1)
        sp.on("vector", fn)
        c.ps_free[ps_bank] = ("ev", vv)
        return ("ev", vv)
    else:
        sac = c.S("ac")
        sac.n += 1
        vv = sac.n

        def fn(a, mm_v=mm_v):
            a.wait_ge(c.sem["mm"].h, mm_v)
            a.copy(out_ap, c.ps[ps_bank][:, 0:width]).then_inc(sac.h, 1)
        sp.on("scalar", fn)
        c.ps_free[ps_bank] = ("ac", vv)
        return ("ac", vv)


def emit_proj_residual(sp, c, a_fm_chunks, w_b_getter, bias_off, x_dram, xo_dram,
                       w_ready, a_ready_fn, ln_ready, scale=1.0, bias_gate=None):
    sev = c.S("ev")
    x_out_ready = [None] * TT
    evac_seq = getattr(c, "_proj_seq", 0)
    for t in range(TT):
        buf = t % 2
        xsl = c.xbuf[:, buf * D:(buf + 1) * D]
        xnsl = c.xnew[:, buf * D:(buf + 1) * D]
        gates = [("ev", ln_ready)]
        if t >= 2:
            gates.append(x_out_ready[t - 2])
        xv = dma(sp, c, "sync", f"xl{buf}",
                 [(xsl, x_dram[t * 128:(t + 1) * 128, :])], gates=gates)
        for e in range(2):
            pb = 4 + (evac_seq % 2)
            evac_seq += 1
            waits = [a_ready_fn(t), ("wp", w_ready)]
            if bias_gate is not None:
                waits.append(bias_gate)
            bias_lhs = c.ones_row[0:1, 0:128] if bias_off is not None else None
            bias_rhs = (c.bias_row[0:1, bias_off + e * 512: bias_off + (e + 1) * 512]
                        if bias_off is not None else None)
            mm_v = emit_fm_group(
                sp, c, pb,
                [a_fm_chunks(cc, t) for cc in range(FC)],
                [w_b_getter(cc, e) for cc in range(FC)],
                bias_lhs=bias_lhs, bias_rhs=bias_rhs, waits=waits)
            sev.n += 1
            rv = sev.n
            c.ps_free[pb] = ("ev", rv)

            def resid(v, e=e, pb=pb, mm_v=mm_v, xv=xv, xsl=xsl, xnsl=xnsl,
                      buf=buf, scale=scale):
                v.wait_ge(c.sem["mm"].h, mm_v)
                v.wait_ge(c.sem[f"xl{buf}"].h, xv)
                v.scalar_tensor_tensor(
                    xnsl[:, e * 512:(e + 1) * 512], c.ps[pb][:, :],
                    scale, xsl[:, e * 512:(e + 1) * 512],
                    ALU.mult, ALU.add).then_inc(sev.h, 1)
            sp.on("vector", resid)
        x_out_ready[t] = emit_store_tile(sp, c, xo_dram, t, xnsl, sev.n)
    c._proj_seq = evac_seq
    return x_out_ready


def emit_softmax(sp, c, strips, scale, par, stat_base=32):
    sev, sac, vch = c.S("ev"), c.S("ac"), c.S("vch")
    for h, st in enumerate(strips):
        sb = stat_base + par * 8 + h * 4
        W = st["W"]
        s1 = st["s1"]
        seg_done = st.get("pre_done", 0)
        for (pbk, off, w, mm_v, bias_ap) in st["segments"]:
            sev.n += 1
            ev_v = sev.n
            c.ps_free[pbk] = ("ev", ev_v)

            def evac(v, pbk=pbk, off=off, w=w, mm_v=mm_v, bias_ap=bias_ap,
                     s1=s1, war=st.get("s1_war", 0), sg=st.get("seg_gate")):
                v.wait_ge(c.sem["mm"].h, mm_v)
                wait(v, c, "ac", war)
                if sg is not None:
                    wait(v, c, sg[0], sg[1])
                if bias_ap is not None:
                    v.scalar_tensor_tensor(s1[:, off:off + w], c.ps[pbk][:, 0:w],
                                           scale, bias_ap, ALU.mult, ALU.add)\
                     .then_inc(sev.h, 1)
                else:
                    v.tensor_scalar(s1[:, off:off + w], c.ps[pbk][:, 0:w],
                                    scale, None, ALU.mult).then_inc(sev.h, 1)
            sp.on("vector", evac)
            seg_done = ev_v
        st["seg_evac_done"] = seg_done
        vch.n += 1
        nm_v = vch.n
        sd = seg_done

        def rmax(v, s1=s1, sb=sb, W=W, sd=sd):
            v.wait_ge(sev.h, sd)
            v.reduce_max(c.stat[:, sb:sb + 1], s1[:, 0:W], axis=AX.X,
                         negate=True).then_inc(vch.h, 1)
        sp.on("vector", rmax)
        sac.n += 1
        ex_v = sac.n
        p_war_h = st.get("p_war", 0)
        sum_war = st.get("sum_war", 0)

        def expf(a, s1=s1, p=st["p"], sb=sb, W=W, nm_v=nm_v, pw=p_war_h,
                 sw=sum_war):
            a.wait_ge(c.sem["vch"].h, max(nm_v, sw))
            wait(a, c, "mm", pw)
            a.activation(p[:, 0:W], s1[:, 0:W], AF.Exp,
                         bias=c.stat[:, sb:sb + 1],
                         accum_out=c.stat[:, sb + 1:sb + 2]).then_inc(sac.h, 1)
        sp.on("scalar", expf)
        st["exp_done"] = ex_v
        st["_sb"] = sb
    return strips


def finish_softmax(sp, c, strips):
    sev, sac, vch = c.S("ev"), c.S("ac"), c.S("vch")
    p_ready = []
    for st in strips:
        sb = st["_sb"]
        W = st["W"]
        ex_v = st["exp_done"]
        vch.n += 2
        rc_v = vch.n - 1
        sev.n += 1
        pn_v = sev.n
        st["recip_done"] = rc_v

        def norm(v, p=st["p"], sb=sb, W=W, ex_v=ex_v, rc_v=rc_v):
            v.wait_ge(sac.h, ex_v)
            v.reciprocal(c.stat[:, sb + 2:sb + 3],
                         c.stat[:, sb + 1:sb + 2]).then_inc(vch.h, 1)
            v.wait_ge(c.sem["vch"].h, rc_v)
            v.tensor_scalar_mul(p[:, 0:W], p[:, 0:W],
                                c.stat[:, sb + 2:sb + 3]).then_inc(vch.h, 1)
            v.engine_nop().then_inc(sev.h, 1)
        sp.on("vector", norm)
        p_ready.append(pn_v)
    return p_ready


def emit_pv2(sp, c, p_strips, p_ready, kc_list, v_tm, heads, a_banks, qw,
             pt_slots, slot_war):
    smm, sev = c.S("mm"), c.S("ev")
    nk = len(kc_list)
    af = [c.ps_free[a_banks[0]], c.ps_free[a_banks[1]]]
    for h in (0, 1):
        gi = 0
        for g0 in range(0, nk, 4):
            gn = min(4, nk - g0)
            pb = gi % 2
            gi += 1
            ptf = c.pt_free[pb]
            trv0 = smm.n
            for u in range(gn):
                j = g0 + u
                smm.n += 1

                def tr(te, h=h, j=j, u=u, pb=pb, pr=p_ready[h], ptf=ptf,
                       p=p_strips[h], first=(u == 0)):
                    if first:
                        te.wait_ge(sev.h, pr)
                        wait(te, c, ptf[0], ptf[1])
                    te.transpose(c.pt[pb][:, u * 128:(u + 1) * 128],
                                 p[:, j * 128:(j + 1) * 128],
                                 c.ident[:, :]).then_inc(smm.h, 1)
                sp.on("tensor", tr)
            tr_v = smm.n
            slot = pt_slots[h][(g0 // 4) % 2]
            sw = slot_war.get((h, (g0 // 4) % 2), 0)
            if gi % 2:
                sev2 = c.S("ev")
                sev2.n += 1
                pt_done = ("ev", sev2.n)
                c.pt_free[pb] = pt_done

                def ptevac(v, pb=pb, tr_v=tr_v, slot=slot, sw=sw, gn=gn):
                    v.wait_ge(smm.h, tr_v)
                    wait(v, c, "mm", sw)
                    v.tensor_copy(slot[:, 0:gn * 128],
                                  c.pt[pb][:, 0:gn * 128]).then_inc(c.sem["ev"].h, 1)
                sp.on("vector", ptevac)
            else:
                sac = c.S("ac")
                sac.n += 1
                pt_done = ("ac", sac.n)
                c.pt_free[pb] = pt_done

                def ptevac(a, pb=pb, tr_v=tr_v, slot=slot, sw=sw, gn=gn):
                    a.wait_ge(smm.h, tr_v)
                    wait(a, c, "mm", sw)
                    a.copy(slot[:, 0:gn * 128],
                           c.pt[pb][:, 0:gn * 128]).then_inc(sac.h, 1)
                sp.on("scalar", ptevac)
            pv_last = 0
            for u in range(gn):
                j = g0 + u
                kc = kc_list[j]
                smm.n += 1
                pv_last = smm.n
                afw = af[h] if j == 0 else None

                def pv(te, h=h, kc=kc, u=u, slot=slot, pt_done=pt_done, afw=afw,
                       first=(j == 0), last=(j == nk - 1)):
                    wait(te, c, pt_done[0], pt_done[1])
                    if afw is not None:
                        wait(te, c, afw[0], afw[1])
                    hh = heads[h]
                    te.matmul(c.ps[a_banks[h]][64 * h:64 * h + 64, 0:qw],
                              v_tm[:, kc * 1024 + 64 * hh: kc * 1024 + 64 * hh + 64],
                              slot[:, u * qw:(u + 1) * qw] if qw == 128 else slot[:, 0:qw],
                              start=first, stop=last, skip_group_check=True,
                              tile_position=(0, 64 * h)).then_inc(smm.h, 1)
                sp.on("tensor", pv)
            slot_war[(h, (g0 // 4) % 2)] = pv_last
    return smm.n


def a_evac2(sp, c, a_banks, qw, out_ap_fn, mm_v):
    sac = c.S("ac")
    for h in (0, 1):
        sac.n += 1

        def ev(a, h=h, mm_v=mm_v):
            a.wait_ge(c.sem["mm"].h, mm_v)
            a.copy(out_ap_fn(h),
                   c.ps[a_banks[h]][64 * h:64 * h + 64, 0:qw]).then_inc(sac.h, 1)
        sp.on("scalar", ev)
        c.ps_free[a_banks[h]] = ("ac", sac.n)
    return sac.n


def a_evac(sp, c, a_ps, qw, out_ap, mm_v):
    return emit_evac_copy(sp, c, a_ps, qw, out_ap, mm_v, engine="vector")


def emit_attn_l0a(sp, c, wqkv, bqkv, wout, bout, x_dram, xo_dram, ln_ready):
    nc = c.nc
    smm, sev, sac = c.S("mm"), c.S("ev"), c.S("ac")
    c.ab.push()
    c.af.push()
    v_tm = c.ab.alloc(TT * 1024)
    a_fm = c.ab.alloc(FC * S)
    q_fm = c.ab.alloc(S)
    k_fm = c.ab.alloc(S)
    p_str2 = [[c.ab.alloc(512) for _ in range(3)], [c.ab.alloc(512) for _ in range(3)]]
    pt_sl = [[c.ab.alloc(384), c.ab.alloc(384)], [c.ab.alloc(384), c.ab.alloc(384)]]
    c.ab.push()
    wv_b = c.ab.alloc(FC * 512)
    s1_str = [c.af.alloc(512), c.af.alloc(512)]
    mask_sb = c.af.alloc(384)

    bias_v = emit_bias_loads(sp, c, [(bqkv, 0, 3072), (bout, 3072, 1024)])
    mask_v = dma(sp, c, "sync", "aux", [(mask_sb, c.mask_l0a_dram)])

    v_done_tile = [0] * TT
    for e in range(2):
        wv_v = dma(sp, c, "gpsimd", "wd",
                   [(wv_b.rearrange("p (c m) -> p c m", c=FC),
                     wqkv[:, 2048 + e * 512: 2048 + (e + 1) * 512]
                     .rearrange("(c p) m -> p c m", p=128))],
                   gates=[("mm", c._wv_war)] if c._wv_war else [])
        last_mm = 0
        for kc in range(TT):
            pbk = kc % 4
            mm_v = emit_fm_group(
                sp, c, pbk,
                [c.ln_fm[:, cc * S + kc * 128: cc * S + (kc + 1) * 128]
                 for cc in range(FC)],
                [wv_b[:, cc * 512:(cc + 1) * 512] for cc in range(FC)],
                bias_lhs=c.ones_row[0:1, 0:128],
                bias_rhs=c.bias_row[0:1, 2048 + e * 512: 2048 + (e + 1) * 512],
                waits=[("wd", wv_v), ("ev", ln_ready), ("aux2", bias_v)])
            emit_evac_copy(sp, c, pbk, 512,
                           v_tm[:, kc * 1024 + e * 512: kc * 1024 + (e + 1) * 512],
                           mm_v)
            last_mm = mm_v
            v_done_tile[kc] = max(v_done_tile[kc], c.sem["ev"].n)
        c._wv_war = last_mm
    v_done = sev.n
    v_mm_done = c.S("mm").n
    c.ab.pop()
    wqk_b2 = [c.ab.alloc(FC * 256), c.ab.alloc(FC * 256)]
    wout_b = c.ab.alloc(FC * 1024)

    war = {}
    pt_slot_war = {}
    seq = 0

    def _l0a_fin(pend):
        blk, par, strips, kc_list = pend
        pr = finish_softmax(sp, c, strips)
        for h in (0, 1):
            war[(blk, h, "recip")] = strips[h]["recip_done"]
        return pr

    def _l0a_pv(pend, p_ready):
        blk, par, strips, kc_list = pend
        pp = blk // TT
        nb = blk % TT
        pv_v = emit_pv2(sp, c, [p_str2[0][par], p_str2[1][par]], p_ready,
                        kc_list, v_tm, (2 * pp, 2 * pp + 1), (4, 5), 128,
                        pt_sl, pt_slot_war)
        for h in (0, 1):
            war[(blk, h, "tr")] = pv_v
        a_evac2(sp, c, (4, 5), 128,
                lambda h, pp=pp, nb=nb: a_fm[64 * h:64 * h + 64,
                                             pp * S + nb * 128:pp * S + (nb + 1) * 128],
                pv_v)

    def _l0a_flush(pend):
        _l0a_pv(pend, _l0a_fin(pend))

    pendq = [None, None]
    for p in range(8):
        slot = p % 2
        wqk_b = wqk_b2[slot]
        prev_scores_done = c.S("mm").n
        wqk_v = dma(sp, c, "gpsimd", f"w{slot}", [
            (wqk_b[:, 0:FC * 128].rearrange("p (c m) -> p c m", c=FC),
             wqkv[:, 128 * p:128 * (p + 1)].rearrange("(c p2) m -> p2 c m", p2=128)),
            (wqk_b[:, FC * 128:FC * 256].rearrange("p (c m) -> p c m", c=FC),
             wqkv[:, 1024 + 128 * p: 1024 + 128 * (p + 1)]
             .rearrange("(c p2) m -> p2 c m", p2=128)),
        ], gates=[("mm", c._wqk_war.get(slot, v_mm_done))])
        qk_first = True
        for which, dst, boff in ((0, q_fm, 128 * p), (1, k_fm, 1024 + 128 * p)):
            for n in range(4):
                pbk = seq % 4
                seq += 1
                mm_v = emit_fm_group(
                    sp, c, pbk,
                    [wqk_b[:, which * FC * 128 + cc * 128: which * FC * 128 + (cc + 1) * 128]
                     for cc in range(FC)],
                    [c.ln_fm[:, cc * S + n * 512: cc * S + (n + 1) * 512]
                     for cc in range(FC)],
                    bias_lhs=c.bias_row[0:1, boff:boff + 128],
                    bias_rhs=c.ones_row[0:1, 0:512],
                    waits=[(f"w{slot}", wqk_v), ("ev", ln_ready), ("aux2", bias_v)])
                if qk_first:
                    sev.n += 1
                    vv = sev.n

                    def evq(v, mm_v=mm_v, dst=dst, n=n, psd=prev_scores_done, pbk=pbk):
                        v.wait_ge(c.sem["mm"].h, mm_v)
                        wait(v, c, "mm", psd)
                        v.tensor_copy(dst[:, n * 512:(n + 1) * 512],
                                      c.ps[pbk][:, :]).then_inc(sev.h, 1)
                    sp.on("vector", evq)
                    c.ps_free[pbk] = ("ev", vv)
                    qk_first = False
                else:
                    emit_evac_copy(sp, c, pbk, 512, dst[:, n * 512:(n + 1) * 512], mm_v)
        qk_done = sev.n
        c._wqk_war[slot] = c.S("mm").n

        for nb in range(TT):
            blk = p * TT + nb
            lo = max(0, (nb - 1) * 128)
            hi = min(S, (nb + 2) * 128)
            w = hi - lo
            moff = 128 if nb == 0 else 0
            par = blk % 3
            strips = []
            pend_pr = _l0a_fin(pendq[0]) if pendq[0] is not None else None
            for h in (0, 1):
                pbk = (nb * 2 + h) % 4
                smm.n += 1
                mm_v = smm.n
                bf = c.ps_free[pbk]

                def smat(te, h=h, nb=nb, lo=lo, hi=hi, w=w, pbk=pbk, bf=bf,
                         qd=qk_done, vd=v_done, mv=mask_v):
                    te.wait_ge(sev.h, max(qd, vd))
                    te.wait_ge(c.sem["aux"].h, mv)
                    wait(te, c, bf[0], bf[1])
                    te.matmul(c.ps[pbk][:, 0:w],
                              q_fm[64 * h:64 * h + 64, nb * 128:(nb + 1) * 128],
                              k_fm[64 * h:64 * h + 64, lo:hi],
                              start=True, stop=True,
                              tile_position=(64 * h, 0)).then_inc(smm.h, 1)
                sp.on("tensor", smat)
                strip = dict(
                    W=w, s1=s1_str[h], p=p_str2[h][par],
                    segments=[(pbk, 0, w, mm_v, mask_sb[:, moff:moff + w])],
                    s1_war=war.get((blk - 1, h, "exp"), 0),
                    p_war=war.get((blk - 3, h, "tr"), 0),
                    sum_war=war.get((blk - 3, h, "recip"), 0),
                )
                emit_softmax(sp, c, [strip], 1.0 / 32.0, par, stat_base=32 + 4 * h)
                strips.append(strip)
                war[(blk, h, "exp")] = strip["exp_done"]
            if pendq[0] is not None:
                _l0a_pv(pendq[0], pend_pr)
            pendq = [pendq[1], (blk, par, strips, list(range(lo // 128, hi // 128)))]
    for pe_ in pendq:
        if pe_ is not None:
            _l0a_flush(pe_)
    a_done_ac = c.S("ac").n

    wo_v = dma(sp, c, "gpsimd", "wp",
               [(wout_b.rearrange("p (c m) -> p c m", c=FC),
                 wout.rearrange("(c p) m -> p c m", p=128))],
               gates=[("mm", c.S("mm").n), ("ac", a_done_ac), ("ev", sev.n)])
    xor = emit_proj_residual(
        sp, c,
        lambda cc, t: a_fm[:, cc * S + t * 128: cc * S + (t + 1) * 128],
        lambda cc, e: wout_b[:, cc * 1024 + e * 512: cc * 1024 + (e + 1) * 512],
        3072, x_dram, xo_dram, wo_v,
        lambda t: ("ac", a_done_ac), ln_ready, bias_gate=("aux2", bias_v))
    c.ab.pop()
    c.af.pop()
    return xor



def emit_attn_l0b(sp, c, wqkvT, woutT, b_out, x_dram, xo_dram, ln_ready):
    nc = c.nc
    smm, sev, sac = c.S("mm"), c.S("ev"), c.S("ac")
    c.ab.push()
    c.af.push()
    v_tm = c.ab.alloc(TT * 1024)
    a_fm = c.ab.alloc(FC * S)
    q_fm = c.ab.alloc(S)
    k_fm = c.ab.alloc(S)
    p_str2 = [[c.ab.alloc(512) for _ in range(3)], [c.ab.alloc(512) for _ in range(3)]]
    pg_str2 = [[c.ab.alloc(16) for _ in range(3)], [c.ab.alloc(16) for _ in range(3)]]
    pt_sl = [[c.ab.alloc(256)], [c.ab.alloc(256)]]
    pgt_sl = [c.ab.alloc(128), c.ab.alloc(128)]
    ln_g = c.ab.alloc(FC * 16)
    vg_tm = c.ab.alloc(1024)
    kg_b = c.ab.alloc(16)
    s1_str = [c.af.alloc(512), c.af.alloc(512)]
    s1g_str = [c.af.alloc(16), c.af.alloc(16)]
    mask_sb = c.af.alloc(256)
    c.ab.push()
    wv_b = c.ab.alloc(FC * 512)

    bias_v = emit_bias_loads(sp, c, [(b_out, 0, 1024)])
    mask_v = dma(sp, c, "sync", "aux", [(mask_sb, c.mask_l0b_dram)])

    sev.n += 1
    lng_v = sev.n

    def lngf(v, lr=ln_ready):
        v.wait_ge(sev.h, lr)
        for cc in range(FC):
            src = AP(c.ln_fm, cc * S, [[FC * S, 128], [128, 16]])
            v.tensor_copy(ln_g[:, cc * 16:(cc + 1) * 16], src)
        v.engine_nop().then_inc(sev.h, 1)
    sp.on("vector", lngf)

    for e in range(2):
        wv_v = dma(sp, c, "gpsimd", "wd",
                   [(wv_b.rearrange("p (c m) -> p c m", c=FC),
                     wqkvT[:, 2048 + e * 512: 2048 + (e + 1) * 512]
                     .rearrange("(c p) m -> p c m", p=128))],
                   gates=[("mm", c._wv_war)] if c._wv_war else [])
        last_mm = 0
        for kc in range(TT):
            pbk = kc % 4
            mm_v = emit_fm_group(
                sp, c, pbk,
                [c.ln_fm[:, cc * S + kc * 128: cc * S + (kc + 1) * 128]
                 for cc in range(FC)],
                [wv_b[:, cc * 512:(cc + 1) * 512] for cc in range(FC)],
                waits=[("wd", wv_v), ("ev", ln_ready)])
            emit_evac_copy(sp, c, pbk, 512,
                           v_tm[:, kc * 1024 + e * 512: kc * 1024 + (e + 1) * 512],
                           mm_v)
            last_mm = mm_v
        pbk = 16 % 4
        mm_v = emit_fm_group(
            sp, c, pbk,
            [ln_g[:, cc * 16:(cc + 1) * 16] for cc in range(FC)],
            [wv_b[:, cc * 512:(cc + 1) * 512] for cc in range(FC)],
            waits=[("wd", wv_v), ("ev", lng_v)])
        sev.n += 1
        vv = sev.n

        def evg(v, mm_v=mm_v, e=e, pbk=pbk):
            v.wait_ge(c.sem["mm"].h, mm_v)
            v.tensor_copy(vg_tm[0:16, e * 512:(e + 1) * 512],
                          c.ps[pbk][0:16, :]).then_inc(sev.h, 1)
        sp.on("vector", evg)
        c.ps_free[pbk] = ("ev", vv)
        c._wv_war = c.S("mm").n
    v_done = sev.n
    v_mm_done = c.S("mm").n
    c.ab.pop()
    wqk_b2 = [c.ab.alloc(FC * 256), c.ab.alloc(FC * 256)]
    wout_b = c.ab.alloc(FC * 1024)

    war = {}
    pt_slot_war = {}
    pgt_slot_war = {}
    seq = 0

    def _l0b_fin(pend):
        blk, par, strips, stripsg, kc_list = pend
        pr = finish_softmax(sp, c, strips)
        prg = finish_softmax(sp, c, stripsg)
        for h in (0, 1):
            war[(blk, h, "recip")] = strips[h]["recip_done"]
            war[(blk, h, "grecip")] = stripsg[h]["recip_done"]
        return (pr, prg)

    def _l0b_pv(pend, prs):
        blk, par, strips, stripsg, kc_list = pend
        p_ready, pg_ready = prs
        pp = blk // TT
        nb = blk % TT
        pv_v = emit_pv_l0b(sp, c, [p_str2[0][par], p_str2[1][par]],
                           [pg_str2[0][par], pg_str2[1][par]], p_ready, pg_ready,
                           kc_list, v_tm, vg_tm, (2 * pp, 2 * pp + 1), (4, 5),
                           pt_sl, pgt_sl, pt_slot_war, pgt_slot_war)
        for h in (0, 1):
            war[(blk, h, "tr")] = pv_v
        a_evac2(sp, c, (4, 5), 128,
                lambda h, pp=pp, nb=nb: a_fm[64 * h:64 * h + 64,
                                             pp * S + nb * 128:pp * S + (nb + 1) * 128],
                pv_v)

    def _l0b_flush(pend):
        _l0b_pv(pend, _l0b_fin(pend))

    pendq = [None, None]
    for p in range(8):
        slot = p % 2
        wqk_b = wqk_b2[slot]
        prev_scores_done = c.S("mm").n
        wqk_v = dma(sp, c, "gpsimd", f"w{slot}", [
            (wqk_b[:, 0:FC * 128].rearrange("p (c m) -> p c m", c=FC),
             wqkvT[:, 128 * p:128 * (p + 1)].rearrange("(c p2) m -> p2 c m", p2=128)),
            (wqk_b[:, FC * 128:FC * 256].rearrange("p (c m) -> p c m", c=FC),
             wqkvT[:, 1024 + 128 * p: 1024 + 128 * (p + 1)]
             .rearrange("(c p2) m -> p2 c m", p2=128)),
        ], gates=[("mm", c._wqk_war.get(slot, v_mm_done))])
        qk_first = True
        for which, dst in ((0, q_fm), (1, k_fm)):
            for n in range(4):
                pbk = seq % 4
                seq += 1
                mm_v = emit_fm_group(
                    sp, c, pbk,
                    [wqk_b[:, which * FC * 128 + cc * 128: which * FC * 128 + (cc + 1) * 128]
                     for cc in range(FC)],
                    [c.ln_fm[:, cc * S + n * 512: cc * S + (n + 1) * 512]
                     for cc in range(FC)],
                    waits=[(f"w{slot}", wqk_v), ("ev", ln_ready)])
                if qk_first:
                    sev.n += 1
                    vv = sev.n

                    def evq(v, mm_v=mm_v, dst=dst, n=n, psd=prev_scores_done, pbk=pbk):
                        v.wait_ge(c.sem["mm"].h, mm_v)
                        wait(v, c, "mm", psd)
                        v.tensor_copy(dst[:, n * 512:(n + 1) * 512],
                                      c.ps[pbk][:, :]).then_inc(sev.h, 1)
                    sp.on("vector", evq)
                    c.ps_free[pbk] = ("ev", vv)
                    qk_first = False
                else:
                    emit_evac_copy(sp, c, pbk, 512, dst[:, n * 512:(n + 1) * 512], mm_v)
        pbk = seq % 4
        seq += 1
        mm_v = emit_fm_group(
            sp, c, pbk,
            [wqk_b[:, FC * 128 + cc * 128: FC * 128 + (cc + 1) * 128] for cc in range(FC)],
            [ln_g[:, cc * 16:(cc + 1) * 16] for cc in range(FC)],
            waits=[(f"w{slot}", wqk_v), ("ev", lng_v)])
        sev.n += 1
        kgv = sev.n

        def evkg(v, mm_v=mm_v, pbk=pbk, psd=prev_scores_done):
            v.wait_ge(c.sem["mm"].h, mm_v)
            wait(v, c, "mm", psd)
            v.tensor_copy(kg_b, c.ps[pbk][:, 0:16]).then_inc(sev.h, 1)
        sp.on("vector", evkg)
        c.ps_free[pbk] = ("ev", kgv)
        qk_done = sev.n
        c._wqk_war[slot] = c.S("mm").n

        for nb in range(TT):
            blk = p * TT + nb
            lo = max(0, (nb - 1) * 128)
            hi = (nb + 1) * 128
            w = hi - lo
            moff = 256 - w
            par = blk % 3
            strips = []
            stripsg = []
            pend_pr = _l0b_fin(pendq[0]) if pendq[0] is not None else None
            for h in (0, 1):
                pbk = (nb * 2 + h) % 4
                smm.n += 1
                mm_v = smm.n
                bf = c.ps_free[pbk]

                def smat(te, h=h, nb=nb, lo=lo, hi=hi, w=w, pbk=pbk, bf=bf,
                         qd=qk_done, vd=v_done, mv=mask_v):
                    te.wait_ge(sev.h, max(qd, vd))
                    te.wait_ge(c.sem["aux"].h, mv)
                    wait(te, c, bf[0], bf[1])
                    te.matmul(c.ps[pbk][:, 0:w],
                              q_fm[64 * h:64 * h + 64, nb * 128:(nb + 1) * 128],
                              k_fm[64 * h:64 * h + 64, lo:hi],
                              start=True, stop=True,
                              tile_position=(64 * h, 0)).then_inc(smm.h, 1)
                sp.on("tensor", smat)
                pbk2 = (nb * 2 + h + 2) % 4
                smm.n += 1
                mmg_v = smm.n
                bfg = c.ps_free[pbk2]

                def smatg(te, h=h, nb=nb, pbk2=pbk2, bfg=bfg, qd=qk_done):
                    te.wait_ge(sev.h, qd)
                    wait(te, c, bfg[0], bfg[1])
                    te.matmul(c.ps[pbk2][:, 0:16],
                              q_fm[64 * h:64 * h + 64, nb * 128:(nb + 1) * 128],
                              kg_b[64 * h:64 * h + 64, 0:16],
                              start=True, stop=True,
                              tile_position=(64 * h, 0)).then_inc(smm.h, 1)
                sp.on("tensor", smatg)
                strip = dict(
                    W=w, s1=s1_str[h], p=p_str2[h][par],
                    segments=[(pbk, 0, w, mm_v, mask_sb[:, moff:moff + w])],
                    s1_war=war.get((blk - 1, h, "exp"), 0),
                    p_war=war.get((blk - 3, h, "tr"), 0),
                    sum_war=war.get((blk - 3, h, "recip"), 0),
                )
                emit_softmax(sp, c, [strip], 1.0 / 8.0, par, stat_base=32 + 4 * h)
                strips.append(strip)
                war[(blk, h, "exp")] = strip["exp_done"]
                stripg = dict(
                    W=16, s1=s1g_str[h], p=pg_str2[h][par],
                    segments=[(pbk2, 0, 16, mmg_v, None)],
                    s1_war=war.get((blk - 1, h, "gexp"), 0),
                    p_war=war.get((blk - 3, h, "tr"), 0),
                    sum_war=war.get((blk - 3, h, "grecip"), 0),
                )
                emit_softmax(sp, c, [stripg], 1.0 / 8.0, par, stat_base=56 + 4 * h)
                stripsg.append(stripg)
                war[(blk, h, "gexp")] = stripg["exp_done"]
            if pendq[0] is not None:
                _l0b_pv(pendq[0], pend_pr)
            pendq = [pendq[1],
                     (blk, par, strips, stripsg, list(range(lo // 128, hi // 128)))]
    for pe_ in pendq:
        if pe_ is not None:
            _l0b_flush(pe_)
    a_done_ac = c.S("ac").n

    wo_v = dma(sp, c, "gpsimd", "wp",
               [(wout_b.rearrange("p (c m) -> p c m", c=FC),
                 woutT.rearrange("(c p) m -> p c m", p=128))],
               gates=[("mm", c.S("mm").n), ("ac", a_done_ac), ("ev", sev.n)])
    xor = emit_proj_residual(
        sp, c,
        lambda cc, t: a_fm[:, cc * S + t * 128: cc * S + (t + 1) * 128],
        lambda cc, e: wout_b[:, cc * 1024 + e * 512: cc * 1024 + (e + 1) * 512],
        0, x_dram, xo_dram, wo_v,
        lambda t: ("ac", a_done_ac), ln_ready, bias_gate=("aux2", bias_v))
    c.ab.pop()
    c.af.pop()
    return xor


def emit_pv_l0b(sp, c, p_str, pg_str, p_ready, pg_ready, kc_list, v_tm, vg_tm,
                heads, a_banks, pt_sl, pgt_sl, slot_war, gslot_war):
    smm, sev = c.S("mm"), c.S("ev")
    nk = len(kc_list)
    af = [c.ps_free[a_banks[0]], c.ps_free[a_banks[1]]]
    for h in (0, 1):
        pb = h % 2
        ptf = c.pt_free[pb]
        for u in range(nk):
            smm.n += 1

            def tr(te, h=h, u=u, pb=pb, pr=p_ready[h], ptf=ptf,
                   p=p_str[h], first=(u == 0)):
                if first:
                    te.wait_ge(sev.h, pr)
                    wait(te, c, ptf[0], ptf[1])
                te.transpose(c.pt[pb][:, u * 128:(u + 1) * 128],
                             p[:, u * 128:(u + 1) * 128],
                             c.ident[:, :]).then_inc(smm.h, 1)
            sp.on("tensor", tr)
        smm.n += 1
        tr_v = smm.n

        def trg(te, h=h, pb=pb, pgr=pg_ready[h], nk=nk, p=pg_str[h]):
            te.wait_ge(sev.h, pgr)
            te.transpose(c.pt[pb][0:16, nk * 128:(nk + 1) * 128],
                         p[:, 0:16], c.ident[:, :]).then_inc(smm.h, 1)
        sp.on("tensor", trg)
        slot = pt_sl[h][0]
        gslot = pgt_sl[h]
        sw = max(slot_war.get(h, 0), gslot_war.get(h, 0))
        sev.n += 1
        pt_v = sev.n
        c.pt_free[pb] = ("ev", pt_v)

        def ptevac(v, pb=pb, tr_v=tr_v, slot=slot, gslot=gslot, sw=sw, nk=nk):
            v.wait_ge(smm.h, tr_v)
            wait(v, c, "mm", sw)
            v.tensor_copy(slot[:, 0:nk * 128], c.pt[pb][:, 0:nk * 128])
            v.tensor_copy(gslot[0:16, 0:128],
                          c.pt[pb][0:16, nk * 128:(nk + 1) * 128]).then_inc(sev.h, 1)
        sp.on("vector", ptevac)
        pv_last = 0
        for u in range(nk):
            kc = kc_list[u]
            smm.n += 1
            pv_last = smm.n
            afw = af[h] if u == 0 else None

            def pv(te, h=h, kc=kc, u=u, slot=slot, pt_v=pt_v, afw=afw,
                   first=(u == 0)):
                te.wait_ge(sev.h, pt_v)
                if afw is not None:
                    wait(te, c, afw[0], afw[1])
                hh = heads[h]
                te.matmul(c.ps[a_banks[h]][64 * h:64 * h + 64, 0:128],
                          v_tm[:, kc * 1024 + 64 * hh: kc * 1024 + 64 * hh + 64],
                          slot[:, u * 128:(u + 1) * 128],
                          start=first, stop=False, skip_group_check=True,
                          tile_position=(0, 64 * h)).then_inc(smm.h, 1)
            sp.on("tensor", pv)
        smm.n += 1
        pv_last = smm.n
        slot_war[h] = pv_last
        gslot_war[h] = pv_last

        def pvg(te, h=h, gslot=gslot, pt_v=pt_v):
            te.wait_ge(sev.h, pt_v)
            hh = heads[h]
            te.matmul(c.ps[a_banks[h]][64 * h:64 * h + 64, 0:128],
                      vg_tm[0:16, 64 * hh:64 * hh + 64],
                      gslot[0:16, :],
                      start=False, stop=True, skip_group_check=True,
                      tile_position=(0, 64 * h)).then_inc(smm.h, 1)
        sp.on("tensor", pvg)
    return smm.n


def emit_attn_l1b(sp, c, wqT, bq, wkT, bk, wvT, bv, woT, bo, t_all,
                  x_dram, xo_dram, ln_ready):
    smm, sev, sac = c.S("mm"), c.S("ev"), c.S("ac")
    c.ab.push()
    c.af.push()
    v_tm = c.ab.alloc(TT * 1024)
    a_fm = c.ab.alloc(FC * S)
    q_fm = c.ab.alloc(S)
    k_fm = c.ab.alloc(S)
    p_str2 = [[c.ab.alloc(S), c.ab.alloc(S)], [c.ab.alloc(S), c.ab.alloc(S)]]
    pt_sl = [[c.ab.alloc(512), c.ab.alloc(512)], [c.ab.alloc(512), c.ab.alloc(512)]]
    s1_str = [c.af.alloc(S), c.af.alloc(S)]

    bias_v = emit_bias_loads(sp, c, [(bq, 0, 1024), (bk, 1024, 1024),
                                     (bv, 2048, 1024), (bo, 3072, 1024)])

    c.ab.push()
    wv_b = c.ab.alloc(FC * 512)
    for e in range(2):
        wv_v = dma(sp, c, "gpsimd", "wd",
                   [(wv_b.rearrange("p (c m) -> p c m", c=FC),
                     wvT[:, e * 512:(e + 1) * 512]
                     .rearrange("(c p) m -> p c m", p=128))],
                   gates=[("mm", c._wv_war)] if c._wv_war else [])
        for kc in range(TT):
            pbk = kc % 4
            mm_v = emit_fm_group(
                sp, c, pbk,
                [c.ln_fm[:, cc * S + kc * 128: cc * S + (kc + 1) * 128]
                 for cc in range(FC)],
                [wv_b[:, cc * 512:(cc + 1) * 512] for cc in range(FC)],
                bias_lhs=c.ones_row[0:1, 0:128],
                bias_rhs=c.bias_row[0:1, 2048 + e * 512: 2048 + (e + 1) * 512],
                waits=[("wd", wv_v), ("ev", ln_ready), ("aux2", bias_v)])
            emit_evac_copy(sp, c, pbk, 512,
                           v_tm[:, kc * 1024 + e * 512: kc * 1024 + (e + 1) * 512],
                           mm_v)
        c._wv_war = c.S("mm").n
    v_done = sev.n
    v_mm_done = c.S("mm").n
    c.ab.pop()
    c.ab.push()
    t_sb = [c.ab.alloc(3968), c.ab.alloc(3968)]
    wqk_b2 = [c.ab.alloc(FC * 256)]
    wqk_b2 = [wqk_b2[0], wqk_b2[0]]

    war = {}
    pt_slot_war = {}
    t_war = 0
    seq = 0

    def _l1b_fin(pend):
        blk, par, strips = pend
        pr = finish_softmax(sp, c, strips)
        for h in (0, 1):
            war[(blk, h, "recip")] = strips[h]["recip_done"]
        return pr

    def _l1b_pv(pend, p_ready):
        blk, par, strips = pend
        p = blk // TT
        nb = blk % TT
        pv_v = emit_pv2(sp, c, [p_str2[0][par], p_str2[1][par]], p_ready,
                        list(range(TT)), v_tm, (2 * p, 2 * p + 1), (4, 5), 128,
                        pt_sl, pt_slot_war)
        for h in (0, 1):
            war[(blk, h, "tr")] = pv_v
        a_evac2(sp, c, (4, 5), 128,
                lambda h: a_fm[64 * h:64 * h + 64,
                               p * S + nb * 128:p * S + (nb + 1) * 128], pv_v)

    def _l1b_flush_pv(pend):
        _l1b_pv(pend, _l1b_fin(pend))

    for p in range(8):
        slot = p % 2
        wqk_b = wqk_b2[slot]
        prev_scores_done = c.S("mm").n
        wqk_v = dma(sp, c, "gpsimd", f"w{slot}", [
            (wqk_b[:, 0:FC * 128].rearrange("p (c m) -> p c m", c=FC),
             wqT[:, 128 * p:128 * (p + 1)].rearrange("(c p2) m -> p2 c m", p2=128)),
            (wqk_b[:, FC * 128:FC * 256].rearrange("p (c m) -> p c m", c=FC),
             wkT[:, 128 * p:128 * (p + 1)].rearrange("(c p2) m -> p2 c m", p2=128)),
        ], gates=[("mm", c._wqk_war.get(0, v_mm_done))])
        t_v = dma(sp, c, "gpsimd", "tld",
                  [(t_sb[0], t_all[2 * p][:, 127:4095]),
                   (t_sb[1], t_all[2 * p + 1][:, 127:4095])],
                  gates=[("ev", t_war)] if t_war else [])
        qk_first = True
        for which, dst, boff in ((0, q_fm, 0), (1, k_fm, 1024)):
            for n in range(4):
                pbk = seq % 4
                seq += 1
                mm_v = emit_fm_group(
                    sp, c, pbk,
                    [wqk_b[:, which * FC * 128 + cc * 128: which * FC * 128 + (cc + 1) * 128]
                     for cc in range(FC)],
                    [c.ln_fm[:, cc * S + n * 512: cc * S + (n + 1) * 512]
                     for cc in range(FC)],
                    bias_lhs=c.bias_row[0:1, boff + 128 * p: boff + 128 * p + 128],
                    bias_rhs=c.ones_row[0:1, 0:512],
                    waits=[(f"w{slot}", wqk_v), ("ev", ln_ready), ("aux2", bias_v)])
                if qk_first:
                    sev.n += 1
                    vv = sev.n

                    def evq(v, mm_v=mm_v, dst=dst, n=n, psd=prev_scores_done, pbk=pbk):
                        v.wait_ge(c.sem["mm"].h, mm_v)
                        wait(v, c, "mm", psd)
                        v.tensor_copy(dst[:, n * 512:(n + 1) * 512],
                                      c.ps[pbk][:, :]).then_inc(sev.h, 1)
                    sp.on("vector", evq)
                    c.ps_free[pbk] = ("ev", vv)
                    qk_first = False
                else:
                    emit_evac_copy(sp, c, pbk, 512, dst[:, n * 512:(n + 1) * 512], mm_v)
        qk_done = sev.n
        c._wqk_war[0] = c.S("mm").n

        pend = None
        for nb in range(TT):
            blk = p * TT + nb
            par = nb % 2
            strips = []
            pend_pr = _l1b_fin(pend) if pend is not None else None
            seg_done = {0: 0, 1: 0}
            for kt in range(4):
                mmvs = {}
                for h in (0, 1):
                    pbk = 2 * h + (kt % 2)
                    smm.n += 1
                    mm_v = smm.n
                    mmvs[h] = (pbk, mm_v)
                    bf = c.ps_free[pbk]

                    def smat(te, h=h, nb=nb, kt=kt, pbk=pbk, bf=bf,
                             qd=qk_done, vd=v_done, tv=t_v):
                        te.wait_ge(sev.h, max(qd, vd))
                        te.wait_ge(c.sem["tld"].h, tv)
                        wait(te, c, bf[0], bf[1])
                        te.matmul(c.ps[pbk][:, 0:512],
                                  q_fm[64 * h:64 * h + 64, nb * 128:(nb + 1) * 128],
                                  k_fm[64 * h:64 * h + 64, kt * 512:(kt + 1) * 512],
                                  start=True, stop=True,
                                  tile_position=(64 * h, 0)).then_inc(smm.h, 1)
                    sp.on("tensor", smat)
                for h in (0, 1):
                    pbk, mm_v = mmvs[h]
                    toff = 2047 - 127 - nb * 128 + kt * 512
                    bias_ap = t_sb[h][:, toff:toff + 512]
                    sev.n += 1
                    ev_v = sev.n
                    c.ps_free[pbk] = ("ev", ev_v)
                    s1w = war.get((blk - 1, h, "exp"), 0)

                    def evs(v, pbk=pbk, mm_v=mm_v, kt=kt, bias_ap=bias_ap,
                            h=h, s1w=s1w):
                        v.wait_ge(c.sem["mm"].h, mm_v)
                        wait(v, c, "ac", s1w)
                        v.scalar_tensor_tensor(
                            s1_str[h][:, kt * 512:(kt + 1) * 512],
                            c.ps[pbk][:, 0:512], 1.0 / 8.0, bias_ap,
                            ALU.mult, ALU.add).then_inc(sev.h, 1)
                    sp.on("vector", evs)
                    seg_done[h] = ev_v
            for h in (0, 1):
                strip = dict(
                    W=S, s1=s1_str[h], p=p_str2[h][par], segments=[],
                    pre_done=seg_done[h],
                    p_war=war.get((blk - 2, h, "tr"), 0),
                    sum_war=war.get((blk - 2, h, "recip"), 0),
                )
                emit_softmax(sp, c, [strip], 1.0 / 8.0, par, stat_base=32 + 4 * h)
                strips.append(strip)
                war[(blk, h, "exp")] = strip["exp_done"]
            if pend is not None:
                _l1b_pv(pend, pend_pr)
            pend = (blk, par, strips)
        _l1b_flush_pv(pend)
        pend = None
        t_war = sev.n
    a_done_ac = c.S("ac").n

    c.ab.pop()
    wout_b = c.ab.alloc(FC * 1024)
    wo_v = dma(sp, c, "gpsimd", "wp",
               [(wout_b.rearrange("p (c m) -> p c m", c=FC),
                 woT.rearrange("(c p) m -> p c m", p=128))],
               gates=[("mm", c.S("mm").n), ("ac", a_done_ac), ("ev", sev.n)])
    xor = emit_proj_residual(
        sp, c,
        lambda cc, t: a_fm[:, cc * S + t * 128: cc * S + (t + 1) * 128],
        lambda cc, e: wout_b[:, cc * 1024 + e * 512: cc * 1024 + (e + 1) * 512],
        3072, x_dram, xo_dram, wo_v,
        lambda t: ("ac", a_done_ac), ln_ready, bias_gate=("aux2", bias_v))
    c.ab.pop()
    c.af.pop()
    return xor



def emit_attn_l1a(sp, c, wq, bq, wk, wv, wproj, bproj, wr, br, bnd_rev, e_scr,
                  x_dram, xo_dram, ln_ready):
    smm, sev, sac = c.S("mm"), c.S("ev"), c.S("ac")
    c.ab.push()
    c.af.push()
    v_tm = c.ab.alloc(TT * 1024)
    a_fm = c.ab.alloc(FC * S)
    c.ab.push()
    q_fm = c.ab.alloc(S)
    k_fm = c.ab.alloc(S)
    p_str2 = [[c.ab.alloc(1280), c.ab.alloc(1280)], [c.ab.alloc(1280), c.ab.alloc(1280)]]
    pt_sl = [[c.ab.alloc(512), c.ab.alloc(512)], [c.ab.alloc(512), c.ab.alloc(512)]]
    r_fm_a = c.ab.alloc(S)
    r_fm_b = c.ab.alloc(S)
    r_head = [c.ab.alloc(S), c.ab.alloc(S)]
    bnd_sb = c.ab.alloc(1024)
    s1_str = [c.af.alloc(1152), c.af.alloc(1152)]
    es_sb2 = [[c.af.alloc(1152), c.af.alloc(1152)],
              [c.af.alloc(1152), c.af.alloc(1152)]]
    e_sb = [c.af.alloc(1024)]
    e_sb = [e_sb[0], e_sb[0]]
    neg_sb = c.af.alloc(256)

    bias_v = emit_bias_loads(sp, c, [(bq, 0, 1024), (br, 1024, 160),
                                     (bproj, 2048, 1024)])
    bnd_v = dma(sp, c, "gpsimd", "aux2", [(bnd_sb[0:10, :], bnd_rev)])
    bias_v = bnd_v

    sev.n += 1
    neg_v = sev.n

    def negset(v):
        v.memset(neg_sb, NEG / 2.0).then_inc(sev.h, 1)
    sp.on("vector", negset)
    marg = []
    for b in range(4):
        marg.append((e_scr[b][:, 0:128], neg_sb[:, 0:128]))
        marg.append((e_scr[b][:, 1152:1408], neg_sb[:, 0:256]))
        marg.append((e_scr[b][:, 1408:1664], neg_sb[:, 0:256]))
    marg_v = dma(sp, c, "sync", "aux", marg, gates=[("ev", neg_v)])

    c.ab.push()
    wv_b = c.ab.alloc(FC * 256)
    for e in range(4):
        wv_v = dma(sp, c, "gpsimd", "wd",
                   [(wv_b.rearrange("p (c m) -> p c m", c=FC),
                     wv[:, e * 256:(e + 1) * 256]
                     .rearrange("(c p) m -> p c m", p=128))],
                   gates=[("mm", c._wv_war)] if c._wv_war else [])
        for kc in range(TT):
            pbk = kc % 4
            mm_v = emit_fm_group(
                sp, c, pbk,
                [c.ln_fm[:, cc * S + kc * 128: cc * S + (kc + 1) * 128]
                 for cc in range(FC)],
                [wv_b[:, cc * 256:(cc + 1) * 256] for cc in range(FC)],
                waits=[("wd", wv_v), ("ev", ln_ready)])
            emit_evac_copy(sp, c, pbk, 256,
                           v_tm[:, kc * 1024 + e * 256: kc * 1024 + (e + 1) * 256],
                           mm_v)
        c._wv_war = c.S("mm").n
    wr_v = dma(sp, c, "gpsimd", "wd",
               [(wv_b[:, 0:FC * 160].rearrange("p (c m) -> p c m", c=FC),
                 wr.rearrange("(c p) m -> p c m", p=128))],
               gates=[("mm", c._wv_war)])
    for which, dst, mw, boff in ((0, r_fm_a, 128, 1024), (1, r_fm_b, 32, 1152)):
        for n in range(4):
            pbk = n % 4
            mm_v = emit_fm_group(
                sp, c, pbk,
                [wv_b[:, cc * 160 + which * 128: cc * 160 + which * 128 + mw]
                 for cc in range(FC)],
                [c.ln_fm[:, cc * S + n * 512: cc * S + (n + 1) * 512]
                 for cc in range(FC)],
                bias_lhs=c.bias_row[0:1, boff:boff + mw],
                bias_rhs=c.ones_row[0:1, 0:512],
                waits=[("wd", wr_v), ("ev", ln_ready), ("aux2", max(bias_v, bnd_v))])
            sev.n += 1
            vv = sev.n

            def evr(v, mm_v=mm_v, dst=dst, n=n, pbk=pbk, mw=mw):
                v.wait_ge(c.sem["mm"].h, mm_v)
                v.tensor_copy(dst[0:mw, n * 512:(n + 1) * 512],
                              c.ps[pbk][0:mw, :]).then_inc(sev.h, 1)
            sp.on("vector", evr)
            c.ps_free[pbk] = ("ev", vv)
    r_done = sev.n
    v_done = sev.n
    v_mm_done = c.S("mm").n
    c.ab.pop()
    wqk_b2 = [c.ab.alloc(FC * 256), c.ab.alloc(FC * 256)]

    war = {}
    pt_slot_war = {}
    e_scr_war = [0, 0, 0, 0]
    seq = 0
    eb = 0

    def _l1a_fin(pend):
        blk, par, strips, kc_list = pend
        pr = finish_softmax(sp, c, strips)
        for h in (0, 1):
            war[(blk, h, "recip")] = strips[h]["recip_done"]
        return pr

    def _l1a_pv(pend, p_ready):
        blk, par, strips, kc_list = pend
        pp = blk // TT
        nb = blk % TT
        pv_v = emit_pv2(sp, c, [p_str2[0][par], p_str2[1][par]], p_ready,
                        kc_list, v_tm, (2 * pp, 2 * pp + 1), (4, 5), 128,
                        pt_sl, pt_slot_war)
        for h in (0, 1):
            war[(blk, h, "tr")] = pv_v
        a_evac2(sp, c, (4, 5), 128,
                lambda h, pp=pp, nb=nb: a_fm[64 * h:64 * h + 64,
                                             pp * S + nb * 128:pp * S + (nb + 1) * 128],
                pv_v)

    def _l1a_flush(pend):
        _l1a_pv(pend, _l1a_fin(pend))

    pend = None
    for p in range(8):
        slot = p % 2
        wqk_b = wqk_b2[slot]
        prev_scores_done = c.S("mm").n
        wqk_v = dma(sp, c, "gpsimd", f"w{slot}", [
            (wqk_b[:, 0:FC * 128].rearrange("p (c m) -> p c m", c=FC),
             wq[:, 128 * p:128 * (p + 1)].rearrange("(c p2) m -> p2 c m", p2=128)),
            (wqk_b[:, FC * 128:FC * 256].rearrange("p (c m) -> p c m", c=FC),
             wk[:, 128 * p:128 * (p + 1)].rearrange("(c p2) m -> p2 c m", p2=128)),
        ], gates=[("mm", c._wqk_war.get(slot, v_mm_done))])
        rrep_pairs = []
        for h in (0, 1):
            hh = 2 * p + h
            lo = 10 * hh
            if lo + 10 <= 128:
                rrep_pairs.append((r_head[h][0:10, :], r_fm_a[lo:lo + 10, :]))
            elif lo < 128:
                k1 = 128 - lo
                rrep_pairs.append((r_head[h][0:k1, :], r_fm_a[lo:128, :]))
                rrep_pairs.append((r_head[h][k1:10, :], r_fm_b[0:10 - k1, :]))
            else:
                rrep_pairs.append((r_head[h][0:10, :], r_fm_b[lo - 128:lo - 118, :]))
        rr_v = dma(sp, c, "sync", "rrep", rrep_pairs,
                   gates=[("ev", r_done), ("mm", war.get(("rrep", p - 2), 0))])
        war[("rrep", p)] = 0
        qk_first = True
        for which, dst, boff in ((0, q_fm, 0), (1, k_fm, None)):
            for n in range(4):
                pbk = seq % 4
                seq += 1
                mm_v = emit_fm_group(
                    sp, c, pbk,
                    [wqk_b[:, which * FC * 128 + cc * 128: which * FC * 128 + (cc + 1) * 128]
                     for cc in range(FC)],
                    [c.ln_fm[:, cc * S + n * 512: cc * S + (n + 1) * 512]
                     for cc in range(FC)],
                    bias_lhs=(c.bias_row[0:1, boff + 128 * p: boff + 128 * p + 128]
                              if boff is not None else None),
                    bias_rhs=(c.ones_row[0:1, 0:512] if boff is not None else None),
                    waits=[(f"w{slot}", wqk_v), ("ev", ln_ready), ("aux2", bias_v)])
                if qk_first:
                    sev.n += 1
                    vv = sev.n

                    def evq(v, mm_v=mm_v, dst=dst, n=n, psd=prev_scores_done, pbk=pbk):
                        v.wait_ge(c.sem["mm"].h, mm_v)
                        wait(v, c, "mm", psd)
                        v.tensor_copy(dst[:, n * 512:(n + 1) * 512],
                                      c.ps[pbk][:, :]).then_inc(sev.h, 1)
                    sp.on("vector", evq)
                    c.ps_free[pbk] = ("ev", vv)
                    qk_first = False
                else:
                    emit_evac_copy(sp, c, pbk, 512, dst[:, n * 512:(n + 1) * 512], mm_v)
        qk_done = sev.n
        c._wqk_war[slot] = c.S("mm").n
        ew_prev = {0: 0, 1: 0}
        sk_prev = {}

        def emit_E(h, nb):
            nonlocal eb
            t0 = nb * 128
            p_start = max(0, t0 - 1024)
            blk = p * TT + nb
            ebuf = eb % 4
            eb += 1
            sk_war = e_scr_war[ebuf]
            for e2 in range(2):
                pbk = c._l1a_seq % 4
                c._l1a_seq += 1
                mm_v = emit_fm_group(
                    sp, c, pbk,
                    [r_head[h][0:10, t0:t0 + 128]],
                    [bnd_sb[0:10, e2 * 512:(e2 + 1) * 512]],
                    waits=[("rrep", rr_v), ("aux2", bnd_v)])
                sev.n += 1
                vv = sev.n
                ewp = ew_prev[h]

                def eve(v, mm_v=mm_v, e2=e2, pbk=pbk, h=h, ewp=ewp):
                    v.wait_ge(c.sem["mm"].h, mm_v)
                    wait(v, c, "ewr", ewp)
                    v.tensor_copy(e_sb[h][:, e2 * 512:(e2 + 1) * 512],
                                  c.ps[pbk][:, :]).then_inc(sev.h, 1)
                sp.on("vector", eve)
                c.ps_free[pbk] = ("ev", vv)
            e_ev = sev.n
            ew_v = dma(sp, c, "sync", "ewr",
                       [(e_scr[ebuf][:, 128:1152], e_sb[h])],
                       gates=[("ev", e_ev), ("sk", sk_war)])
            ew_prev[0] = ew_prev[1] = ew_v
            off0 = 128 + 1023 - t0 + p_start
            Wt = t0 + 128 - p_start
            skew_ap = AP(e_scr[ebuf].tensor,
                         e_scr[ebuf].offset + off0, [[1663, 128], [1, Wt]])
            es_war = war.get((blk - 2, h, "segev"), 0)
            sk_chain = c.S("sk").n
            sk_v = dma(sp, c, "sync", "sk",
                       [(es_sb2[h][nb % 2][:, 0:Wt], skew_ap)],
                       gates=[("ewr", ew_v), ("aux", marg_v), ("ev", es_war),
                              ("sk", sk_chain)])
            e_scr_war[ebuf] = sk_v
            sk_prev[(h, nb)] = sk_v

        c._l1a_seq = seq
        for h in (0, 1):
            emit_E(h, 0)

        for nb in range(TT):
            blk = p * TT + nb
            t0 = nb * 128
            p_start = max(0, t0 - 1024)
            p_end = t0 + 128
            Wt = p_end - p_start
            par = nb % 2
            strips = []
            pend_pr = _l1a_fin(pend) if pend is not None else None
            for h in (0, 1):
                if nb < TT - 1:
                    emit_E(h, nb + 1)
                sk_v = sk_prev[(h, nb)]
                segs = []
                nseg = cdiv(Wt, 512)
                for si in range(nseg):
                    soff = si * 512
                    wseg = min(512, Wt - soff)
                    pbk = c._l1a_seq % 4
                    c._l1a_seq += 1
                    smm.n += 1
                    mm_v = smm.n
                    bf = c.ps_free[pbk]

                    def smat(te, h=h, nb=nb, soff=soff, wseg=wseg, pbk=pbk, bf=bf,
                             qd=qk_done, vd=v_done, ps0=p_start):
                        te.wait_ge(sev.h, max(qd, vd))
                        wait(te, c, bf[0], bf[1])
                        te.matmul(c.ps[pbk][:, 0:wseg],
                                  q_fm[64 * h:64 * h + 64, nb * 128:(nb + 1) * 128],
                                  k_fm[64 * h:64 * h + 64,
                                       ps0 + soff: ps0 + soff + wseg],
                                  start=True, stop=True,
                                  tile_position=(64 * h, 0)).then_inc(smm.h, 1)
                    sp.on("tensor", smat)
                    segs.append((pbk, soff, wseg, mm_v,
                                 es_sb2[h][nb % 2][:, soff:soff + wseg]))
                strip = dict(
                    W=Wt, s1=s1_str[h], p=p_str2[h][par], segments=segs,
                    s1_war=war.get((blk - 1, h, "exp"), 0),
                    p_war=war.get((blk - 2, h, "tr"), 0),
                    sum_war=war.get((blk - 2, h, "recip"), 0),
                    seg_gate=("sk", sk_v),
                )
                emit_softmax(sp, c, [strip], 1.0 / 8.0, par,
                             stat_base=32 + 4 * h)
                strips.append(strip)
                war[(blk, h, "segev")] = strip["seg_evac_done"]
            for h in (0, 1):
                war[(blk, h, "exp")] = strips[h]["exp_done"]
            if pend is not None:
                _l1a_pv(pend, pend_pr)
            pend = (blk, par, strips, list(range(p_start // 128, p_end // 128)))
        war[("rrep", p)] = c.S("mm").n
    _l1a_flush(pend)
    a_done_ac = c.S("ac").n

    c.ab.pop()
    wout_b = c.ab.alloc(FC * 1024)
    wo_v = dma(sp, c, "gpsimd", "wp",
               [(wout_b.rearrange("p (c m) -> p c m", c=FC),
                 wproj.rearrange("(c p) m -> p c m", p=128))],
               gates=[("mm", c.S("mm").n), ("ac", a_done_ac), ("ev", sev.n)])
    xor = emit_proj_residual(
        sp, c,
        lambda cc, t: a_fm[:, cc * S + t * 128: cc * S + (t + 1) * 128],
        lambda cc, e: wout_b[:, cc * 1024 + e * 512: cc * 1024 + (e + 1) * 512],
        2048, x_dram, xo_dram, wo_v,
        lambda t: ("ac", a_done_ac), ln_ready, bias_gate=("aux2", bias_v))
    c.ab.pop()
    c.af.pop()
    return xor



def emit_ffn_l1(sp, c, w1T, w2T, h1T, h2T, w3T, s1d, s2d, h1sd, h2sd, w3sd,
                ident_dram, x_dram, xo_dram, ln_ready, prep_gate=None):
    smm, sev, sac, vch = c.S("mm"), c.S("ev"), c.S("ac"), c.S("vch")
    HB = 32
    c.ab.push()
    c.af.push()
    su_str = c.ab.alloc(HB * 512)
    h_str = c.ab.alloc(HB * 512)
    wb1 = [c.ab.alloc(FC * 128) for _ in range(2)]
    wb2 = [c.ab.alloc(FC * 128) for _ in range(2)]
    hb = [c.ab.alloc(HB * 128) for _ in range(2)]
    w3b = c.ab.alloc(HB * 256)
    ident32 = c.af.alloc(128)
    s_cols = c.af.alloc(160)
    sc_rows = c.af.alloc(128)
    s3_bc = c.af.alloc(1024)
    s3row = c.af.alloc(1024)
    onef = c.af.alloc(128)
    y_blk = c.af.alloc(4096)

    id32_v = dma(sp, c, "gpsimd", "aux2", [(ident32, ident_dram)])
    last_tr = 0
    sc_all = 0
    for i, vec in enumerate((s1d, s2d, h1sd, h2sd)):
        ld_v = dma(sp, c, "sync", "aux",
                   [(sc_rows[0:32, :], vec.rearrange("(m p) -> m p", p=128))],
                   gates=[("mm", last_tr)] if last_tr else [])
        bf = c.ps_free[5]
        smm.n += 1
        tr_v = smm.n
        last_tr = tr_v

        def trsc(te, ld_v=ld_v, id32_v=id32_v, bf=bf):
            te.wait_ge(c.sem["aux"].h, ld_v)
            te.wait_ge(c.sem["aux2"].h, id32_v)
            wait(te, c, bf[0], bf[1])
            te.transpose(c.ps[5][:, 0:32], sc_rows[0:32, :],
                         ident32[0:32, 0:32]).then_inc(smm.h, 1)
        sp.on("tensor", trsc)
        sev.n += 1
        ev_v = sev.n

        def evsc(v, tr_v=tr_v, i=i):
            v.wait_ge(smm.h, tr_v)
            v.tensor_copy(s_cols[:, i * 32:(i + 1) * 32],
                          c.ps[5][:, 0:32]).then_inc(sev.h, 1)
        sp.on("vector", evsc)
        c.ps_free[5] = ("ev", ev_v)
        sc_all = ev_v
    s3_v = dma(sp, c, "sync", "aux", [(s3row[0:1, :], w3sd[None, :])],
               gates=[("mm", last_tr)])
    sev.n += 1
    onef_v = sev.n

    def onesf(v):
        v.memset(onef[0:1, 0:128], 1.0).then_inc(sev.h, 1)
    sp.on("vector", onesf)
    s3bc_done = 0
    for e in range(2):
        bf = c.ps_free[4]
        smm.n += 1
        mm_v = smm.n

        def s3mm(te, e=e, s3_v=s3_v, onef_v=onef_v, bf=bf):
            te.wait_ge(c.sem["aux"].h, s3_v)
            te.wait_ge(sev.h, onef_v)
            wait(te, c, bf[0], bf[1])
            te.matmul(c.ps[4][0:128, 0:512], onef[0:1, 0:128],
                      s3row[0:1, e * 512:(e + 1) * 512],
                      start=True, stop=True).then_inc(smm.h, 1)
        sp.on("tensor", s3mm)
        sev.n += 1
        vv = sev.n

        def s3ev(v, mm_v=mm_v, e=e):
            v.wait_ge(smm.h, mm_v)
            v.tensor_copy(s3_bc[:, e * 512:(e + 1) * 512],
                          c.ps[4][:, 0:512]).then_inc(sev.h, 1)
        sp.on("vector", s3ev)
        c.ps_free[4] = ("ev", vv)
        s3bc_done = vv
        prep_done = max(sc_all, s3bc_done)

    war = {}
    mm_slot = {"w1": [0, 0], "h": [0, 0]}
    x_out_ready = [None] * TT
    for n in range(4):
        tok = slice(n * 512, (n + 1) * 512)
        for m in range(HB):
            dbl = m % 2
            g0 = [("mm", mm_slot["w1"][dbl])] if mm_slot["w1"][dbl] else []
            if prep_gate and n == 0 and m == dbl:
                g0 = g0 + [prep_gate]
            lw = dma(sp, c, "sync", f"sw{dbl}", [
                (wb1[dbl].rearrange("p (c m) -> p c m", c=FC),
                 w1T[:, m * 128:(m + 1) * 128].rearrange("(c p) m -> p c m", p=128)),
                (wb2[dbl].rearrange("p (c m) -> p c m", c=FC),
                 w2T[:, m * 128:(m + 1) * 128].rearrange("(c p) m -> p c m", p=128)),
            ], gates=g0)
            pg, pu = 2 * dbl, 2 * dbl + 1
            gf, uf = c.ps_free[pg], c.ps_free[pu]
            smm.n += 2
            mm_v = smm.n
            mm_slot["w1"][dbl] = mm_v

            def mmu(te, dbl=dbl, lw=lw, tok=tok, pg=pg, pu=pu, gf=gf, uf=uf,
                    lr=ln_ready, pd=prep_done):
                te.wait_ge(c.sem[f"sw{dbl}"].h, lw)
                te.wait_ge(sev.h, max(lr, pd))
                wait(te, c, gf[0], gf[1])
                wait(te, c, uf[0], uf[1])
                for cc in range(FC):
                    ins = te.matmul(c.ps[pg][:, :], wb1[dbl][:, cc * 128:(cc + 1) * 128],
                                    c.ln_fm[:, cc * S:(cc + 1) * S][:, tok],
                                    start=(cc == 0), stop=(cc == FC - 1))
                    if cc == FC - 1:
                        ins.then_inc(smm.h, 1)
                for cc in range(FC):
                    ins = te.matmul(c.ps[pu][:, :], wb2[dbl][:, cc * 128:(cc + 1) * 128],
                                    c.ln_fm[:, cc * S:(cc + 1) * S][:, tok],
                                    start=(cc == 0), stop=(cc == FC - 1))
                    if cc == FC - 1:
                        ins.then_inc(smm.h, 1)
            sp.on("tensor", mmu)

            sac.n += 1
            sa = sac.n
            su_war = war.get(("su", n - 1), 0) if m < 2 else 0

            def silu(a, pg=pg, mm_v=mm_v, m=m, sw=su_war):
                a.wait_ge(smm.h, mm_v - 1)
                wait(a, c, "ev", sw)
                a.activation(su_str[:, m * 512:(m + 1) * 512], c.ps[pg][:, :],
                             AF.Silu, scale=s_cols[:, m:m + 1]).then_inc(sac.h, 1)
            sp.on("scalar", silu)
            c.ps_free[pg] = ("ac", sa)

            sev.n += 1
            hv = sev.n
            h_war = war.get(("h", n - 1), 0) if m < 2 else 0

            def hmul(v, m=m, pu=pu, sa=sa, mm_v=mm_v, hw=h_war):
                v.wait_ge(sac.h, sa)
                v.wait_ge(smm.h, mm_v)
                wait(v, c, "mm", hw)
                v.scalar_tensor_tensor(h_str[:, m * 512:(m + 1) * 512],
                                       c.ps[pu][:, :], s_cols[:, 32 + m:33 + m],
                                       su_str[:, m * 512:(m + 1) * 512],
                                       ALU.mult, ALU.mult).then_inc(sev.h, 1)
            sp.on("vector", hmul)
            c.ps_free[pu] = ("ev", hv)
        h_done = sev.n
        war[("su", n)] = sev.n

        def hidden_layer(src, dst, wT, scol_off, h_ready, wslot_key, wait_extra):
            g_done_local = 0
            for m2 in range(HB):
                dbl = m2 % 2
                lw = dma(sp, c, "sync", f"sh{dbl}", [
                    (hb[dbl].rearrange("p (c m) -> p c m", c=HB),
                     wT[:, m2 * 128:(m2 + 1) * 128]
                     .rearrange("(c p) m -> p c m", p=128)),
                ], gates=[("mm", mm_slot["h"][dbl])] if mm_slot["h"][dbl] else [])
                pbk = 2 * dbl
                bf = c.ps_free[pbk]
                smm.n += 1
                mm_v = smm.n
                mm_slot["h"][dbl] = mm_v

                def mmh(te, dbl=dbl, lw=lw, pbk=pbk, bf=bf, hr=h_ready, src=src,
                        we=wait_extra):
                    te.wait_ge(c.sem[f"sh{dbl}"].h, lw)
                    te.wait_ge(sev.h, hr)
                    wait(te, c, "ac", we)
                    wait(te, c, bf[0], bf[1])
                    for k in range(HB):
                        ins = te.matmul(c.ps[pbk][:, :],
                                        hb[dbl][:, k * 128:(k + 1) * 128],
                                        src[:, k * 512:(k + 1) * 512],
                                        start=(k == 0), stop=(k == HB - 1))
                    ins.then_inc(smm.h, 1)
                sp.on("tensor", mmh)

                sac.n += 1
                sa = sac.n

                def gelu(a, pbk=pbk, mm_v=mm_v, m2=m2, dst=dst, so=scol_off):
                    a.wait_ge(smm.h, mm_v)
                    a.activation(dst[:, m2 * 512:(m2 + 1) * 512], c.ps[pbk][:, :],
                                 AF.Gelu, scale=s_cols[:, so + m2:so + m2 + 1])\
                     .then_inc(sac.h, 1)
                sp.on("scalar", gelu)
                c.ps_free[pbk] = ("ac", sa)
                g_done_local = sa
            return g_done_local

        g1_done = hidden_layer(h_str, su_str, h1T, 64, h_done, "h", 0)
        sev.n += 0
        g2_done = hidden_layer(su_str, h_str, h2T, 96, h_done, "h", g1_done)
        war[("h", n)] = smm.n

        yev_done = 0
        for e4 in range(4):
            w3_v = dma(sp, c, "sync", "sw3", [
                (w3b.rearrange("p (c m) -> p c m", c=HB),
                 w3T[:, e4 * 256:(e4 + 1) * 256].rearrange("(c p) m -> p c m", p=128)),
            ], gates=[("mm", war.get(("w3", n, e4 - 1), 0))] if e4 else
                     ([("mm", war.get(("w3", n - 1, 3), 0))] if n else []))
            for tt_i in range(4):
                pbk = 4 + tt_i % 2
                bf = c.ps_free[pbk]
                smm.n += 1
                mm_v = smm.n

                def mmy(te, tt_i=tt_i, pbk=pbk, bf=bf, w3_v=w3_v, g2d=g2_done):
                    te.wait_ge(c.sem["sw3"].h, w3_v)
                    te.wait_ge(sac.h, g2d)
                    wait(te, c, bf[0], bf[1])
                    for k in range(HB):
                        ins = te.matmul(c.ps[pbk][:, 0:256],
                                        h_str[:, k * 512 + tt_i * 128: k * 512 + (tt_i + 1) * 128],
                                        w3b[:, k * 256:(k + 1) * 256],
                                        start=(k == 0), stop=(k == HB - 1))
                    ins.then_inc(smm.h, 1)
                sp.on("tensor", mmy)
                war[("w3", n, e4)] = smm.n

                sev.n += 1
                yv = sev.n
                ybw = war.get(("yblk", n - 1), 0)

                def yev(v, tt_i=tt_i, e4=e4, pbk=pbk, mm_v=mm_v, ybw=ybw):
                    v.wait_ge(smm.h, mm_v)
                    wait(v, c, "ev", ybw)
                    v.scalar_tensor_tensor(
                        y_blk[:, tt_i * 1024 + e4 * 256: tt_i * 1024 + (e4 + 1) * 256],
                        c.ps[pbk][:, 0:256], 1.0,
                        s3_bc[:, e4 * 256:(e4 + 1) * 256],
                        ALU.mult, ALU.mult).then_inc(sev.h, 1)
                sp.on("vector", yev)
                c.ps_free[pbk] = ("ev", yv)
                yev_done = yv

        for tt_i in range(4):
            t = n * 4 + tt_i
            buf = t % 2
            xsl = c.xbuf[:, buf * D:(buf + 1) * D]
            xnsl = c.xnew[:, buf * D:(buf + 1) * D]
            gates = [("ev", ln_ready)]
            if t >= 2:
                gates.append(x_out_ready[t - 2])
            xv = dma(sp, c, "sync", f"xl{buf}",
                     [(xsl, x_dram[t * 128:(t + 1) * 128, :])], gates=gates)
            sev.n += 1
            rv = sev.n

            def resid(v, tt_i=tt_i, xv=xv, xsl=xsl, xnsl=xnsl, buf=buf, yd=yev_done):
                v.wait_ge(c.sem[f"xl{buf}"].h, xv)
                v.wait_ge(sev.h, yd)
                v.tensor_add(xnsl, xsl,
                             y_blk[:, tt_i * 1024:(tt_i + 1) * 1024]).then_inc(sev.h, 1)
            sp.on("vector", resid)
            x_out_ready[t] = emit_store_tile(sp, c, xo_dram, t, xnsl, rv)
        war[("yblk", n)] = sev.n
    c.ab.pop()
    c.af.pop()
    return x_out_ready



def emit_attn_l1b3(sp, c, wqT, bq, wkT, bk, wvT, bv, woT, bo, t_all, t_all2,
                   x_dram, xo_dram, ln_ready):
    smm, sev, sac, vch = c.S("mm"), c.S("ev"), c.S("ac"), c.S("vch")
    sach = c.S("ach")
    c.ab.push()
    c.af.push()
    v_tm = c.ab.alloc(TT * 1024)
    a_fm = c.ab.alloc(FC * S)
    _s1 = c.af.alloc(S)
    s1_str = [_s1, _s1]
    _s1t = c.af.alloc(512).bitcast(BF16)
    s1t_b = [_s1t[:, 0:512], _s1t[:, 512:1024]]
    t2f = c.af.alloc(3968)
    t2_sb = [t2f.bitcast(BF16)[:, 0:3968], t2f.bitcast(BF16)[:, 3968:7936]]
    dumf = c.af.alloc(128)
    dummy = dumf.bitcast(BF16)

    bias_v = emit_bias_loads(sp, c, [(bq, 0, 1024), (bk, 1024, 1024),
                                     (bv, 2048, 1024), (bo, 3072, 1024)])

    c.ab.push()
    wv_b = c.ab.alloc(FC * 512)
    for e in range(2):
        wv_v = dma(sp, c, "gpsimd", "wd",
                   [(wv_b.rearrange("p (c m) -> p c m", c=FC),
                     wvT[:, e * 512:(e + 1) * 512]
                     .rearrange("(c p) m -> p c m", p=128))],
                   gates=[("mm", c._wv_war)] if c._wv_war else [])
        for kc in range(TT):
            pbk = kc % 4
            mm_v = emit_fm_group(
                sp, c, pbk,
                [c.ln_fm[:, cc * S + kc * 128: cc * S + (kc + 1) * 128]
                 for cc in range(FC)],
                [wv_b[:, cc * 512:(cc + 1) * 512] for cc in range(FC)],
                bias_lhs=c.ones_row[0:1, 0:128],
                bias_rhs=c.bias_row[0:1, 2048 + e * 512: 2048 + (e + 1) * 512],
                waits=[("wd", wv_v), ("ev", ln_ready), ("aux2", bias_v)])
            emit_evac_copy(sp, c, pbk, 512,
                           v_tm[:, kc * 1024 + e * 512: kc * 1024 + (e + 1) * 512],
                           mm_v)
        c._wv_war = c.S("mm").n
    v_done = sev.n
    v_mm_done = c.S("mm").n
    c.ab.pop()
    c.ab.push()
    q_fm = c.ab.alloc(S)
    k_fm = c.ab.alloc(S)
    q2_b = [c.ab.alloc(S), c.ab.alloc(S)]
    k2_b = [c.ab.alloc(S), c.ab.alloc(S)]
    pt_sl = [[c.ab.alloc(512), c.ab.alloc(512)], [c.ab.alloc(512), c.ab.alloc(512)]]
    t_sb = [c.ab.alloc(3968), c.ab.alloc(3968)]
    mneg = c.ab.alloc(32)
    wqk_b2 = [c.ab.alloc(FC * 256)]
    wqk_b2 = [wqk_b2[0], wqk_b2[0]]

    war = {}
    t_war = 0
    NG = 4
    st_seq = [0]

    def _l1b3_group(p, g, war):
        for h in (0, 1):
            hh = 2 * p + h
            mrdy = war[(p * TT + g * NG + NG - 1, h, "mrdy")]
            afb = c.ps_free[4 + h]
            for kc in range(TT):
                pbk = st_seq[0] % 4
                st_seq[0] += 1
                smm.n += 1
                stv = smm.n
                bf = c.ps_free[pbk]

                def stmat(te, h=h, kc=kc, g=g, pbk=pbk, bf=bf, mrdy=mrdy):
                    wait(te, c, "ac", mrdy)
                    wait(te, c, bf[0], bf[1])
                    te.matmul(c.ps[pbk][:, 0:512],
                              k2_b[h][0:66, kc * 128:(kc + 1) * 128],
                              q2_b[h][0:66, g * 512:(g + 1) * 512],
                              start=True, stop=True).then_inc(smm.h, 1)
                sp.on("tensor", stmat)
                c0 = 1920 + g * 512 - kc * 128
                sev.n += 1
                stt_v = sev.n
                c.ps_free[pbk] = ("ev", stt_v)
                s1t = s1t_b[kc % 2]
                s1tw = war.get(("s1te", kc % 2), 0)

                def stt(v, pbk=pbk, stv=stv, s1t=s1t, h=h, c0=c0, s1tw=s1tw):
                    v.wait_ge(c.sem["mm"].h, stv)
                    wait(v, c, "ac", s1tw)
                    v.scalar_tensor_tensor(s1t, c.ps[pbk][:, 0:512], 1.0,
                                           t2_sb[h][:, c0:c0 + 512],
                                           ALU.mult, ALU.add).then_inc(sev.h, 1)
                sp.on("vector", stt)
                sac.n += 1
                ex_v = sac.n
                war[("s1te", kc % 2)] = ex_v
                slot2 = pt_sl[h][kc % 2]
                sw = war.get(("ptw", h, kc % 2), 0)

                def expt(a, s1t=s1t, stt_v=stt_v, slot2=slot2, sw=sw):
                    a.wait_ge(sev.h, stt_v)
                    wait(a, c, "mm", sw)
                    a.activation(slot2, s1t, AF.Exp).then_inc(sac.h, 1)
                sp.on("scalar", expt)
                smm.n += 1
                pv_v = smm.n
                war[("ptw", h, kc % 2)] = pv_v
                afw = afb if kc == 0 else None

                def pv(te, h=h, hh=hh, kc=kc, slot2=slot2, ex_v=ex_v, afw=afw,
                       first=(kc == 0), last=(kc == TT - 1)):
                    te.wait_ge(c.sem["ac"].h, ex_v)
                    if afw is not None:
                        wait(te, c, afw[0], afw[1])
                    te.matmul(c.ps[4 + h][64 * h:64 * h + 64, 0:512],
                              v_tm[:, kc * 1024 + 64 * hh: kc * 1024 + 64 * hh + 64],
                              slot2,
                              start=first, stop=last, skip_group_check=True,
                              tile_position=(0, 64 * h)).then_inc(smm.h, 1)
                sp.on("tensor", pv)
            sac.n += 1
            aev = sac.n

            def aevac(a, h=h, g=g, p=p, pvv=smm.n):
                a.wait_ge(c.sem["mm"].h, pvv)
                a.copy(a_fm[64 * h:64 * h + 64, p * S + g * 512:p * S + (g + 1) * 512],
                       c.ps[4 + h][64 * h:64 * h + 64, 0:512]).then_inc(sac.h, 1)
            sp.on("scalar", aevac)
            c.ps_free[4 + h] = ("ac", aev)

    for p in range(8):
        slot = p % 2
        wqk_b = wqk_b2[slot]
        prev_scores_done = c.S("mm").n
        wqk_v = dma(sp, c, "gpsimd", f"w{slot}", [
            (wqk_b[:, 0:FC * 128].rearrange("p (c m) -> p c m", c=FC),
             wqT[:, 128 * p:128 * (p + 1)].rearrange("(c p2) m -> p2 c m", p2=128)),
            (wqk_b[:, FC * 128:FC * 256].rearrange("p (c m) -> p c m", c=FC),
             wkT[:, 128 * p:128 * (p + 1)].rearrange("(c p2) m -> p2 c m", p2=128)),
        ], gates=[("mm", c._wqk_war.get(0, v_mm_done))])
        pend_g = None
        t_v = dma(sp, c, "gpsimd", "tld",
                  [(t_sb[0], t_all[2 * p][:, 127:4095]),
                   (t_sb[1], t_all[2 * p + 1][:, 127:4095]),
                   (t2_sb[0], t_all2[2 * p][:, 127:4095]),
                   (t2_sb[1], t_all2[2 * p + 1][:, 127:4095])],
                  gates=[("ev", t_war)] if t_war else [])
        qk_first = True
        for which, dst, boff in ((0, q_fm, 0), (1, k_fm, 1024)):
            for n in range(4):
                pbk = (which * 4 + n) % 4
                mm_v = emit_fm_group(
                    sp, c, pbk,
                    [wqk_b[:, which * FC * 128 + cc * 128: which * FC * 128 + (cc + 1) * 128]
                     for cc in range(FC)],
                    [c.ln_fm[:, cc * S + n * 512: cc * S + (n + 1) * 512]
                     for cc in range(FC)],
                    bias_lhs=c.bias_row[0:1, boff + 128 * p: boff + 128 * p + 128],
                    bias_rhs=c.ones_row[0:1, 0:512],
                    waits=[(f"w{slot}", wqk_v), ("ev", ln_ready), ("aux2", bias_v)])
                if qk_first:
                    sev.n += 1
                    vv = sev.n

                    def evq(v, mm_v=mm_v, dst=dst, n=n, psd=prev_scores_done, pbk=pbk):
                        v.wait_ge(c.sem["mm"].h, mm_v)
                        wait(v, c, "mm", psd)
                        v.tensor_copy(dst[:, n * 512:(n + 1) * 512],
                                      c.ps[pbk][:, :]).then_inc(sev.h, 1)
                    sp.on("vector", evq)
                    c.ps_free[pbk] = ("ev", vv)
                    qk_first = False
                else:
                    emit_evac_copy(sp, c, pbk, 512, dst[:, n * 512:(n + 1) * 512], mm_v)
        qk_done = sev.n
        c._wqk_war[0] = c.S("mm").n
        sev.n += 2
        q2_done = sev.n

        def mkq2(v, psd=prev_scores_done, qd=qk_done):
            v.wait_ge(sev.h, qd)
            wait(v, c, "mm", psd)
            for h in (0, 1):
                v.tensor_scalar_mul(q2_b[h][0:64, :], q_fm[64 * h:64 * h + 64, :],
                                    0.125)
                v.tensor_copy(k2_b[h][0:64, :], k_fm[64 * h:64 * h + 64, :])
                v.memset(k2_b[h][64:66, :], 1.0)
            v.engine_nop().then_inc(sev.h, 2)
        sp.on("vector", mkq2)

        for nb in range(TT):
            blk = p * TT + nb
            par = nb % 2
            g = nb // NG
            for h in (0, 1):
                sb = 32 + par * 8 + h * 4
                pcol = 64 + h * 8
                seg_done = 0
                for kt in range(4):
                    pbk = 2 * h + kt % 2
                    smm.n += 1
                    mm_v = smm.n
                    bf = c.ps_free[pbk]

                    def smat(te, h=h, nb=nb, kt=kt, pbk=pbk, bf=bf,
                             qd=max(qk_done, q2_done), vd=v_done, tv=t_v):
                        te.wait_ge(sev.h, max(qd, vd))
                        te.wait_ge(c.sem["tld"].h, tv)
                        wait(te, c, bf[0], bf[1])
                        te.matmul(c.ps[pbk][:, 0:512],
                                  q_fm[64 * h:64 * h + 64, nb * 128:(nb + 1) * 128],
                                  k_fm[64 * h:64 * h + 64, kt * 512:(kt + 1) * 512],
                                  start=True, stop=True,
                                  tile_position=(64 * h, 0)).then_inc(smm.h, 1)
                    sp.on("tensor", smat)
                    toff = 2047 - 127 - nb * 128 + kt * 512
                    sev.n += 1
                    ev_v = sev.n
                    c.ps_free[pbk] = ("ev", ev_v)
                    s1w = (war.get((blk, 0, "chunks"), 0) if h == 1
                           else war.get((blk - 1, 1, "chunks"), 0))

                    def evs(v, pbk=pbk, mm_v=mm_v, kt=kt, h=h, s1w=s1w, toff=toff):
                        v.wait_ge(c.sem["mm"].h, mm_v)
                        wait(v, c, "ach", s1w)
                        v.scalar_tensor_tensor(
                            s1_str[h][:, kt * 512:(kt + 1) * 512],
                            c.ps[pbk][:, 0:512], 0.125,
                            t_sb[h][:, toff:toff + 512],
                            ALU.mult, ALU.add).then_inc(sev.h, 1)
                    sp.on("vector", evs)
                    seg_done = ev_v
                vch.n += 1
                nm_v = vch.n

                def rmax(v, h=h, sb=sb, sd=seg_done):
                    v.wait_ge(sev.h, sd)
                    v.reduce_max(c.stat[:, sb:sb + 1], s1_str[h][:, 0:S], axis=AX.X,
                                 negate=True).then_inc(vch.h, 1)
                sp.on("vector", rmax)
                pw = war.get((blk - 2, h, "pcol"), 0)
                ch0 = sach.n
                sach.n += 8
                for ci in range(8):
                    def expc(a, h=h, sb=sb, ci=ci, nm_v=nm_v, pw=pw, ch0=ch0,
                             pcol=pcol):
                        if ci == 0:
                            a.wait_ge(c.sem["vch"].h, max(nm_v, pw))
                        else:
                            a.wait_ge(sach.h, ch0 + ci)
                        a.activation(dummy[:, 0:256],
                                     s1_str[h][:, ci * 256:(ci + 1) * 256],
                                     AF.Exp, bias=c.stat[:, sb:sb + 1],
                                     accum_out=c.stat[:, pcol + ci:pcol + ci + 1])\
                         .then_inc(sach.h, 1)
                    sp.on("scalar", expc)
                war[(blk, h, "chunks")] = sach.n
                vch.n += 3
                comb_v = vch.n - 2

                def comb(v, h=h, sb=sb, pcol=pcol, chn=war[(blk, h, "chunks")],
                         par=par, comb_v=comb_v):
                    v.wait_ge(sach.h, chn)
                    st = c.stat
                    v.tensor_add(st[:, pcol:pcol + 1], st[:, pcol:pcol + 1],
                                 st[:, pcol + 1:pcol + 2])
                    v.tensor_add(st[:, pcol + 2:pcol + 3], st[:, pcol + 2:pcol + 3],
                                 st[:, pcol + 3:pcol + 4])
                    v.tensor_add(st[:, pcol + 4:pcol + 5], st[:, pcol + 4:pcol + 5],
                                 st[:, pcol + 5:pcol + 6])
                    v.tensor_add(st[:, pcol + 6:pcol + 7], st[:, pcol + 6:pcol + 7],
                                 st[:, pcol + 7:pcol + 8]).then_inc(vch.h, 1)
                    v.wait_ge(vch.h, comb_v)
                    v.tensor_add(st[:, pcol:pcol + 1], st[:, pcol:pcol + 1],
                                 st[:, pcol + 2:pcol + 3])
                    v.tensor_add(st[:, pcol + 4:pcol + 5], st[:, pcol + 4:pcol + 5],
                                 st[:, pcol + 6:pcol + 7]).then_inc(vch.h, 1)
                    v.wait_ge(vch.h, comb_v + 1)
                    v.tensor_add(st[:, sb + 1:sb + 2], st[:, pcol:pcol + 1],
                                 st[:, pcol + 4:pcol + 5]).then_inc(vch.h, 1)
                sp.on("vector", comb)
                war[(blk, h, "pcol")] = vch.n - 2
                sac.n += 1
                lnd_v = sac.n

                def lnd(a, sb=sb, cv=vch.n):
                    a.wait_ge(c.sem["vch"].h, cv)
                    a.activation(c.stat[:, sb + 2:sb + 3], c.stat[:, sb + 1:sb + 2],
                                 AF.Ln).then_inc(sac.h, 1)
                sp.on("scalar", lnd)
                vch.n += 3
                mn_v = vch.n
                mcol = (2 * par + h) * 2
                fcol = 80 + 2 * par + h

                def mneg_f(v, sb=sb, lv=lnd_v, mcol=mcol, fcol=fcol, mn_v=mn_v,
                           mw=war.get((blk - 2, h, "mtr"), 0)):
                    v.wait_ge(sac.h, lv)
                    wait(v, c, "mm", mw)
                    v.tensor_sub(c.stat[:, fcol:fcol + 1], c.stat[:, sb:sb + 1],
                                 c.stat[:, sb + 2:sb + 3]).then_inc(vch.h, 1)
                    v.wait_ge(vch.h, mn_v - 2)
                    v.tensor_copy(mneg[:, mcol:mcol + 1],
                                  c.stat[:, fcol:fcol + 1]).then_inc(vch.h, 1)
                    v.wait_ge(vch.h, mn_v - 1)
                    v.tensor_sub(mneg[:, mcol + 1:mcol + 2],
                                 c.stat[:, fcol:fcol + 1],
                                 mneg[:, mcol:mcol + 1]).then_inc(vch.h, 1)
                sp.on("vector", mneg_f)
                pb = h
                ptf = c.pt_free[pb]
                smm.n += 1
                mtr_v = smm.n
                war[(blk, h, "mtr")] = mtr_v

                def mtr(te, pb=pb, mcol=mcol, mn_v=mn_v, ptf=ptf):
                    te.wait_ge(c.sem["vch"].h, mn_v)
                    wait(te, c, ptf[0], ptf[1])
                    te.transpose(c.pt[pb][0:2, 0:128], mneg[:, mcol:mcol + 2],
                                 c.ident[:, :]).then_inc(smm.h, 1)
                sp.on("tensor", mtr)
                sac.n += 1
                mev_v = sac.n
                c.pt_free[pb] = ("ac", mev_v)

                def mev(a, pb=pb, h=h, nb=nb, mtr_v=mtr_v):
                    a.wait_ge(c.sem["mm"].h, mtr_v)
                    a.copy(q2_b[h][64:66, nb * 128:(nb + 1) * 128],
                           c.pt[pb][0:2, 0:128]).then_inc(sac.h, 1)
                sp.on("scalar", mev)
                war[(blk, h, "mrdy")] = mev_v
            if nb % NG == NG - 1:
                if pend_g is not None:
                    _l1b3_group(pend_g[0], pend_g[1], war)
                pend_g = (p, g)
        _l1b3_group(pend_g[0], pend_g[1], war)
        pend_g = None
        t_war = sev.n
    a_done_ac = c.S("ac").n

    c.ab.pop()
    wout_b = c.ab.alloc(FC * 1024)
    wo_v = dma(sp, c, "gpsimd", "wp",
               [(wout_b.rearrange("p (c m) -> p c m", c=FC),
                 woT.rearrange("(c p) m -> p c m", p=128))],
               gates=[("mm", c.S("mm").n), ("ac", a_done_ac), ("ev", sev.n)])
    xor = emit_proj_residual(
        sp, c,
        lambda cc, t: a_fm[:, cc * S + t * 128: cc * S + (t + 1) * 128],
        lambda cc, e: wout_b[:, cc * 1024 + e * 512: cc * 1024 + (e + 1) * 512],
        3072, x_dram, xo_dram, wo_v,
        lambda t: ("ac", a_done_ac), ln_ready, bias_gate=("aux2", bias_v))
    c.ab.pop()
    c.af.pop()
    return xor


def finalize(sp, c, x_out_ready):
    tot = {}
    for g in x_out_ready:
        if g is not None:
            tot[g[0]] = max(tot.get(g[0], 0), g[1])

    def fin(g):
        for k, v in tot.items():
            g.wait_ge(c.sem[k].h, v)
    sp.on("gpsimd", fin)


def build_block(nc, sp):
    with nc.Block() as block:
        @block.sync
        def _(eng):
            for fn in sp.streams["sync"]:
                fn(eng)

        @block.tensor
        def _(eng):
            for fn in sp.streams["tensor"]:
                fn(eng)

        @block.vector
        def _(eng):
            for fn in sp.streams["vector"]:
                fn(eng)

        @block.scalar
        def _(eng):
            for fn in sp.streams["scalar"]:
                fn(eng)

        @block.gpsimd
        def _(eng):
            for fn in sp.streams["gpsimd"]:
                fn(eng)


def build_ffn_l0_program():
    nc = bass.Bass("TRN2", target_bir_lowering=False)
    x_in = nc.dram_tensor("x_in", [S, D], F32, kind="ExternalInput").ap()
    x_out = nc.dram_tensor("x_out", [S, D], F32, kind="ExternalOutput").ap()
    wgT = nc.dram_tensor("wgT", [D, 3072], I32, kind="ExternalInput").ap()
    wuT = nc.dram_tensor("wuT", [D, 3072], I32, kind="ExternalInput").ap()
    wdT = nc.dram_tensor("wdT", [3072, D], I32, kind="ExternalInput").ap()
    ident = nc.dram_tensor("ident", [128, 128], BF16, kind="ExternalInput").ap()
    exits = contextlib.ExitStack()
    with exits:
        c = make_ctx(nc, exits)
        sp = SP()
        emit_const_init(sp, c, ident)
        ln_ready = emit_ln(sp, c, x_in)
        xor = emit_ffn_l0(sp, c, wgT, wuT, wdT, x_in, x_out, ln_ready)
        finalize(sp, c, xor)
        build_block(nc, sp)
    return nc


def build_attn_l0a_program():
    nc = bass.Bass("TRN2", target_bir_lowering=False)
    x_in = nc.dram_tensor("x_in", [S, D], F32, kind="ExternalInput").ap()
    x_out = nc.dram_tensor("x_out", [S, D], F32, kind="ExternalOutput").ap()
    wqkv = nc.dram_tensor("wqkv", [D, 3072], F32, kind="ExternalInput").ap()
    bqkv = nc.dram_tensor("bqkv", [3072], F32, kind="ExternalInput").ap()
    wout = nc.dram_tensor("wout", [D, D], F32, kind="ExternalInput").ap()
    bout = nc.dram_tensor("bout", [D], F32, kind="ExternalInput").ap()
    ident = nc.dram_tensor("ident", [128, 128], BF16, kind="ExternalInput").ap()
    mask = nc.dram_tensor("mask_l0a", [128, 384], F32, kind="ExternalInput").ap()
    exits = contextlib.ExitStack()
    with exits:
        c = make_ctx(nc, exits)
        c.mask_l0a_dram = mask
        c._wv_war = 0
        c._wqk_war = {}
        sp = SP()
        emit_const_init(sp, c, ident)
        ln_ready = emit_ln(sp, c, x_in)
        xor = emit_attn_l0a(sp, c, wqkv, bqkv, wout, bout, x_in, x_out, ln_ready)
        finalize(sp, c, xor)
        build_block(nc, sp)
    return nc


def build_attn_l0b_program():
    nc = bass.Bass("TRN2", target_bir_lowering=False)
    x_in = nc.dram_tensor("x_in", [S, D], F32, kind="ExternalInput").ap()
    x_out = nc.dram_tensor("x_out", [S, D], F32, kind="ExternalOutput").ap()
    wqkvT = nc.dram_tensor("wqkvT", [D, 3072], F32, kind="ExternalInput").ap()
    woutT = nc.dram_tensor("woutT", [D, D], F32, kind="ExternalInput").ap()
    b_out = nc.dram_tensor("b_out", [D], F32, kind="ExternalInput").ap()
    ident = nc.dram_tensor("ident", [128, 128], BF16, kind="ExternalInput").ap()
    mask = nc.dram_tensor("mask_l0b", [128, 256], F32, kind="ExternalInput").ap()
    exits = contextlib.ExitStack()
    with exits:
        c = make_ctx(nc, exits)
        c.mask_l0b_dram = mask
        c._wv_war = 0
        c._wqk_war = {}
        sp = SP()
        emit_const_init(sp, c, ident)
        ln_ready = emit_ln(sp, c, x_in)
        xor = emit_attn_l0b(sp, c, wqkvT, woutT, b_out, x_in, x_out, ln_ready)
        finalize(sp, c, xor)
        build_block(nc, sp)
    return nc


def build_ffn_l1_program():
    nc = bass.Bass("TRN2", target_bir_lowering=False)
    x_in = nc.dram_tensor("x_in", [S, D], F32, kind="ExternalInput").ap()
    x_out = nc.dram_tensor("x_out", [S, D], F32, kind="ExternalOutput").ap()
    w1T = nc.dram_tensor("w1T", [D, 4096], F32, kind="ExternalInput").ap()
    w2T = nc.dram_tensor("w2T", [D, 4096], F32, kind="ExternalInput").ap()
    h1T = nc.dram_tensor("h1T", [4096, 4096], F32, kind="ExternalInput").ap()
    h2T = nc.dram_tensor("h2T", [4096, 4096], F32, kind="ExternalInput").ap()
    w3T = nc.dram_tensor("w3T", [4096, D], F32, kind="ExternalInput").ap()
    s1d = nc.dram_tensor("s1", [4096], F32, kind="ExternalInput").ap()
    s2d = nc.dram_tensor("s2", [4096], F32, kind="ExternalInput").ap()
    h1sd = nc.dram_tensor("h1s", [4096], F32, kind="ExternalInput").ap()
    h2sd = nc.dram_tensor("h2s", [4096], F32, kind="ExternalInput").ap()
    w3sd = nc.dram_tensor("w3s", [D], F32, kind="ExternalInput").ap()
    ident = nc.dram_tensor("ident", [128, 128], BF16, kind="ExternalInput").ap()
    exits = contextlib.ExitStack()
    with exits:
        c = make_ctx(nc, exits)
        sp = SP()
        emit_const_init(sp, c, ident)
        ln_ready = emit_ln(sp, c, x_in)
        xor = emit_ffn_l1(sp, c, w1T, w2T, h1T, h2T, w3T, s1d, s2d, h1sd, h2sd,
                          w3sd, ident, x_in, x_out, ln_ready)
        finalize(sp, c, xor)
        build_block(nc, sp)
    return nc


def build_attn_l1a_program():
    nc = bass.Bass("TRN2", target_bir_lowering=False)
    x_in = nc.dram_tensor("x_in", [S, D], F32, kind="ExternalInput").ap()
    x_out = nc.dram_tensor("x_out", [S, D], F32, kind="ExternalOutput").ap()
    wq = nc.dram_tensor("wq", [D, D], F32, kind="ExternalInput").ap()
    wk = nc.dram_tensor("wk", [D, D], F32, kind="ExternalInput").ap()
    wv = nc.dram_tensor("wv", [D, D], F32, kind="ExternalInput").ap()
    wproj = nc.dram_tensor("wproj", [D, D], F32, kind="ExternalInput").ap()
    bq = nc.dram_tensor("bq", [D], F32, kind="ExternalInput").ap()
    bproj = nc.dram_tensor("bproj", [D], F32, kind="ExternalInput").ap()
    wr = nc.dram_tensor("wr", [D, 160], F32, kind="ExternalInput").ap()
    br = nc.dram_tensor("br", [160], F32, kind="ExternalInput").ap()
    bnd_rev = nc.dram_tensor("bnd_rev", [10, 1024], F32, kind="ExternalInput").ap()
    e_scr = nc.dram_tensor("e_scr", [4, 128, 1664], F32, kind="Internal").ap()
    ident = nc.dram_tensor("ident", [128, 128], BF16, kind="ExternalInput").ap()
    exits = contextlib.ExitStack()
    with exits:
        c = make_ctx(nc, exits)
        c._wv_war = 0
        c._wqk_war = {}
        sp = SP()
        emit_const_init(sp, c, ident)
        ln_ready = emit_ln(sp, c, x_in)
        xor = emit_attn_l1a(sp, c, wq, bq, wk, wv, wproj, bproj, wr, br, bnd_rev,
                            e_scr, x_in, x_out, ln_ready)
        finalize(sp, c, xor)
        build_block(nc, sp)
    return nc


def build_attn_l1b_program():
    nc = bass.Bass("TRN2", target_bir_lowering=False)
    x_in = nc.dram_tensor("x_in", [S, D], F32, kind="ExternalInput").ap()
    x_out = nc.dram_tensor("x_out", [S, D], F32, kind="ExternalOutput").ap()
    wqT = nc.dram_tensor("wqT", [D, D], F32, kind="ExternalInput").ap()
    wkT = nc.dram_tensor("wkT", [D, D], F32, kind="ExternalInput").ap()
    wvT = nc.dram_tensor("wvT", [D, D], F32, kind="ExternalInput").ap()
    woT = nc.dram_tensor("woT", [D, D], F32, kind="ExternalInput").ap()
    bq = nc.dram_tensor("bq", [D], F32, kind="ExternalInput").ap()
    bk = nc.dram_tensor("bk", [D], F32, kind="ExternalInput").ap()
    bv = nc.dram_tensor("bv", [D], F32, kind="ExternalInput").ap()
    bo = nc.dram_tensor("bo", [D], F32, kind="ExternalInput").ap()
    t_all = nc.dram_tensor("t_all", [16, 128, 4224], F32, kind="ExternalInput").ap()
    t_all2 = nc.dram_tensor("t_all2", [16, 128, 4224], F32, kind="ExternalInput").ap()
    ident = nc.dram_tensor("ident", [128, 128], BF16, kind="ExternalInput").ap()
    exits = contextlib.ExitStack()
    with exits:
        c = make_ctx(nc, exits)
        c._wv_war = 0
        c._wqk_war = {}
        sp = SP()
        emit_const_init(sp, c, ident)
        ln_ready = emit_ln(sp, c, x_in)
        xor = emit_attn_l1b3(sp, c, wqT, bq, wkT, bk, wvT, bv, woT, bo, t_all,
                             t_all2, x_in, x_out, ln_ready)
        finalize(sp, c, xor)
        build_block(nc, sp)
    return nc


def t_all_np(pos_bias):
    H = pos_bias.shape[0]
    r = np.arange(128)[:, None]
    cc = np.arange(4224)[None, :]
    idx = 4094 - cc + r
    sel = np.clip(idx, 0, 4094)
    t = pos_bias[:, sel]
    t[:, (idx < 0) | (idx > 4094)] = 0.0
    return np.ascontiguousarray(t.astype(np.float32))


def t_all2_np(pos_bias):
    r = np.arange(128)[:, None]
    cc = np.arange(4224)[None, :]
    idx = cc - r
    sel = np.clip(idx, 0, 4094)
    t = pos_bias[:, sel]
    t[:, (idx < 0) | (idx > 4094)] = 0.0
    return np.ascontiguousarray(t.astype(np.float32))


def mask_l0b_np():
    m = np.zeros((128, 256), np.float32)
    i = np.arange(128)[:, None]
    j = np.arange(256)[None, :]
    d = i - (j - 128)
    valid = (d >= 0) & (d <= 128)
    m[~valid] = NEG
    return m


def mask_l0a_np():
    m = np.zeros((128, 384), np.float32)
    i = np.arange(128)[:, None]
    j = np.arange(384)[None, :]
    kq = (j - 128) - i
    m[np.abs(kq) > 128] = NEG
    return m


def _ident_bf16():
    return np.eye(128, dtype=ml_dtypes.bfloat16)


def build_fused_program():
    nc = bass.Bass("TRN2", target_bir_lowering=False)
    I = lambda n, sh, dt=F32: nc.dram_tensor(n, sh, dt, kind="ExternalInput").ap()
    x_in = I("x_in", [S, D])
    x_out = nc.dram_tensor("x_out", [S, D], F32, kind="ExternalOutput").ap()
    xa = nc.dram_tensor("xa", [S, D], F32, kind="Internal").ap()
    xb = nc.dram_tensor("xb", [S, D], F32, kind="Internal").ap()
    e_scr = nc.dram_tensor("e_scr", [4, 128, 1664], F32, kind="Internal").ap()
    ident = I("ident", [128, 128], BF16)
    l0a = dict(wqkv=I("l0a_wqkv", [D, 3072]), bqkv=I("l0a_bqkv", [3072]),
               wout=I("l0a_wout", [D, D]), bout=I("l0a_bout", [D]),
               mask=I("mask_l0a", [128, 384]))
    f0 = dict(wgT=I("f0_wgT", [D, 3072], I32), wuT=I("f0_wuT", [D, 3072], I32),
              wdT=I("f0_wdT", [3072, D], I32))
    l0b = dict(wqkvT=I("l0b_wqkvT", [D, 3072]), woutT=I("l0b_woutT", [D, D]),
               b_out=I("l0b_bout", [D]), mask=I("mask_l0b", [128, 256]))
    l1a = dict(wq=I("l1a_wq", [D, D]), wk=I("l1a_wk", [D, D]), wv=I("l1a_wv", [D, D]),
               wproj=I("l1a_wproj", [D, D]), bq=I("l1a_bq", [D]),
               bproj=I("l1a_bproj", [D]), wr=I("l1a_wr", [D, 160]),
               br=I("l1a_br", [160]), bnd_rev=I("l1a_bnd_rev", [10, 1024]))
    f1 = dict(w1T=I("f1_w1T", [D, 4096]), w2T=I("f1_w2T", [D, 4096]),
              h1T=I("f1_h1T", [4096, 4096]), h2T=I("f1_h2T", [4096, 4096]),
              w3T=I("f1_w3T", [4096, D]), s1=I("f1_s1", [4096]), s2=I("f1_s2", [4096]),
              h1s=I("f1_h1s", [4096]), h2s=I("f1_h2s", [4096]), w3s=I("f1_w3s", [D]))
    l1b = dict(wqT=I("l1b_wqT", [D, D]), wkT=I("l1b_wkT", [D, D]),
               wvT=I("l1b_wvT", [D, D]), woT=I("l1b_woT", [D, D]),
               bq=I("l1b_bq", [D]), bk=I("l1b_bk", [D]), bv=I("l1b_bv", [D]),
               bo=I("l1b_bo", [D]), t_all=I("t_all", [16, 128, 4224]),
               t_all2=I("t_all2", [16, 128, 4224]))
    f0c = dict(wgT=nc.dram_tensor("c_f0g", [D, 3072], BF16, kind="Internal").ap(),
               wuT=nc.dram_tensor("c_f0u", [D, 3072], BF16, kind="Internal").ap(),
               wdT=nc.dram_tensor("c_f0d", [3072, D], BF16, kind="Internal").ap())
    f1c = dict(w1T=nc.dram_tensor("c_f1w1", [D, 4096], BF16, kind="Internal").ap(),
               w2T=nc.dram_tensor("c_f1w2", [D, 4096], BF16, kind="Internal").ap(),
               h1T=nc.dram_tensor("c_f1h1", [4096, 4096], BF16, kind="Internal").ap(),
               h2T=nc.dram_tensor("c_f1h2", [4096, 4096], BF16, kind="Internal").ap(),
               w3T=nc.dram_tensor("c_f1w3", [4096, D], BF16, kind="Internal").ap())
    exits = contextlib.ExitStack()
    with exits:
        c = make_ctx(nc, exits)
        c.mask_l0a_dram = l0a["mask"]
        c.mask_l0b_dram = l0b["mask"]
        sp = SP()
        emit_const_init(sp, c, ident)
        prep0 = emit_prepass(sp, c, [(f0c["wgT"], f0["wgT"]), (f0c["wuT"], f0["wuT"]),
                                     (f0c["wdT"], f0["wdT"])])
        chain = [x_in, xa, xb, xa, xb, xa, xb, xa, x_out]
        state = None
        prep1 = None

        def boundary():
            c.entry_gates = [("mm", c.S("mm").n), ("ev", c.S("ev").n),
                             ("ac", c.S("ac").n)]
            c._wv_war = 0
            c._wqk_war = {}

        for i, sub in enumerate(["l0a", "ffn0", "l0b", "ffn0",
                                 "l1a", "ffn1", "l1b", "ffn1"]):
            src, dst = chain[i], chain[i + 1]
            boundary()
            lr = emit_ln(sp, c, src, x_tile_ready=state)
            if sub == "l0a":
                state = emit_attn_l0a(sp, c, l0a["wqkv"], l0a["bqkv"], l0a["wout"],
                                      l0a["bout"], src, dst, lr)
                prep1 = emit_prepass(sp, c, [(f1c[k], f1[k]) for k in
                                             ("w1T", "w2T", "h1T", "h2T", "w3T")],
                                     sem="prep1")
            elif sub == "ffn0":
                state = emit_ffn_l0(sp, c, f0c["wgT"], f0c["wuT"], f0c["wdT"],
                                    src, dst, lr, prep_gate=prep0)
            elif sub == "l0b":
                state = emit_attn_l0b(sp, c, l0b["wqkvT"], l0b["woutT"],
                                      l0b["b_out"], src, dst, lr)
            elif sub == "l1a":
                state = emit_attn_l1a(sp, c, l1a["wq"], l1a["bq"], l1a["wk"],
                                      l1a["wv"], l1a["wproj"], l1a["bproj"],
                                      l1a["wr"], l1a["br"], l1a["bnd_rev"],
                                      e_scr, src, dst, lr)
            elif sub == "ffn1":
                state = emit_ffn_l1(sp, c, f1c["w1T"], f1c["w2T"], f1c["h1T"],
                                    f1c["h2T"], f1c["w3T"], f1["s1"], f1["s2"],
                                    f1["h1s"], f1["h2s"], f1["w3s"], ident,
                                    src, dst, lr, prep_gate=prep1)
            elif sub == "l1b":
                state = emit_attn_l1b3(sp, c, l1b["wqT"], l1b["bq"], l1b["wkT"],
                                       l1b["bk"], l1b["wvT"], l1b["bv"],
                                       l1b["woT"], l1b["bo"], l1b["t_all"],
                                       l1b["t_all2"], src, dst, lr)
        finalize(sp, c, state)
        build_block(nc, sp)
    return nc


def _fused_in_map(inputs):
    T = np.ascontiguousarray
    m = dict(ident=_ident_bf16(), mask_l0a=mask_l0a_np(), mask_l0b=mask_l0b_np())
    m.update(l0a_wqkv=inputs["L0_attn_a_Wqkv"], l0a_bqkv=inputs["L0_attn_a_bqkv"],
             l0a_wout=inputs["L0_attn_a_Wout"], l0a_bout=inputs["L0_attn_a_bout"])
    m.update(f0_wgT=T(inputs["L0_ffn_w_gate"].T), f0_wuT=T(inputs["L0_ffn_w_up"].T),
             f0_wdT=T(inputs["L0_ffn_w_down"].T))
    m.update(l0b_wqkvT=T(inputs["L0_attn_b_w_qkv"].T),
             l0b_woutT=T(inputs["L0_attn_b_w_out"].T),
             l0b_bout=inputs["L0_attn_b_b_out"])
    m.update(l1a_wq=inputs["L1_attn_a_Wq"], l1a_wk=inputs["L1_attn_a_Wk"],
             l1a_wv=inputs["L1_attn_a_Wv"], l1a_wproj=inputs["L1_attn_a_Wproj"],
             l1a_bq=inputs["L1_attn_a_bq"], l1a_bproj=inputs["L1_attn_a_bproj"],
             l1a_wr=inputs["L1_attn_a_Wr"], l1a_br=inputs["L1_attn_a_br"],
             l1a_bnd_rev=T(inputs["L1_attn_a_b_nd"][:, ::-1]))
    m.update(f1_w1T=T(inputs["L1_ffn_w1_w"].T), f1_w2T=T(inputs["L1_ffn_w2_w"].T),
             f1_h1T=T(inputs["L1_ffn_h1_w"].T), f1_h2T=T(inputs["L1_ffn_h2_w"].T),
             f1_w3T=T(inputs["L1_ffn_w3_w"].T),
             f1_s1=inputs["L1_ffn_w1_s"], f1_s2=inputs["L1_ffn_w2_s"],
             f1_h1s=inputs["L1_ffn_h1_s"], f1_h2s=inputs["L1_ffn_h2_s"],
             f1_w3s=inputs["L1_ffn_w3_s"])
    m.update(l1b_wqT=T(inputs["L1_attn_b_Wq"].T), l1b_wkT=T(inputs["L1_attn_b_Wk"].T),
             l1b_wvT=T(inputs["L1_attn_b_Wv"].T), l1b_woT=T(inputs["L1_attn_b_Wo"].T),
             l1b_bq=inputs["L1_attn_b_bq"], l1b_bk=inputs["L1_attn_b_bk"],
             l1b_bv=inputs["L1_attn_b_bv"], l1b_bo=inputs["L1_attn_b_bo"],
             t_all=t_all_np(inputs["L1_attn_b_pos_bias"]),
             t_all2=t_all2_np(inputs["L1_attn_b_pos_bias"]))
    return m


def kernel_fused(**inputs):
    B = 8
    x = np.ascontiguousarray(inputs["L0_attn_a_x"].astype(np.float32, copy=False))
    nc = _get_program("fused", build_fused_program)
    common = _fused_in_map(inputs)
    in_maps = [dict(common, x_in=x[b]) for b in range(B)]
    res = run_bass_kernel_spmd(nc, in_maps, core_ids=list(range(B)))
    return np.stack([np.asarray(res.results[b]["x_out"]) for b in range(B)]
                    ).astype(np.float32)


_PROGRAMS = {}


def _get_program(name, builder):
    if name not in _PROGRAMS:
        _PROGRAMS[name] = builder()
    return _PROGRAMS[name]


def _prep_host(inputs):
    T = np.ascontiguousarray
    h = {}
    h["ident"] = _ident_bf16()
    h["l0a"] = dict(
        wqkv=inputs["L0_attn_a_Wqkv"], bqkv=inputs["L0_attn_a_bqkv"],
        wout=inputs["L0_attn_a_Wout"], bout=inputs["L0_attn_a_bout"],
        mask_l0a=mask_l0a_np(), ident=h["ident"])
    h["ffn0"] = dict(
        wgT=T(inputs["L0_ffn_w_gate"].T), wuT=T(inputs["L0_ffn_w_up"].T),
        wdT=T(inputs["L0_ffn_w_down"].T), ident=h["ident"])
    h["l0b"] = dict(
        wqkvT=T(inputs["L0_attn_b_w_qkv"].T), woutT=T(inputs["L0_attn_b_w_out"].T),
        b_out=inputs["L0_attn_b_b_out"], mask_l0b=mask_l0b_np(), ident=h["ident"])
    h["l1a"] = dict(
        wq=inputs["L1_attn_a_Wq"], wk=inputs["L1_attn_a_Wk"],
        wv=inputs["L1_attn_a_Wv"], wproj=inputs["L1_attn_a_Wproj"],
        bq=inputs["L1_attn_a_bq"], bproj=inputs["L1_attn_a_bproj"],
        wr=inputs["L1_attn_a_Wr"], br=inputs["L1_attn_a_br"],
        bnd_rev=T(inputs["L1_attn_a_b_nd"][:, ::-1]), ident=h["ident"])
    h["ffn1"] = dict(
        w1T=T(inputs["L1_ffn_w1_w"].T), w2T=T(inputs["L1_ffn_w2_w"].T),
        h1T=T(inputs["L1_ffn_h1_w"].T), h2T=T(inputs["L1_ffn_h2_w"].T),
        w3T=T(inputs["L1_ffn_w3_w"].T),
        s1=inputs["L1_ffn_w1_s"], s2=inputs["L1_ffn_w2_s"],
        h1s=inputs["L1_ffn_h1_s"], h2s=inputs["L1_ffn_h2_s"],
        w3s=inputs["L1_ffn_w3_s"], ident=h["ident"])
    h["l1b"] = dict(
        wqT=T(inputs["L1_attn_b_Wq"].T), wkT=T(inputs["L1_attn_b_Wk"].T),
        wvT=T(inputs["L1_attn_b_Wv"].T), woT=T(inputs["L1_attn_b_Wo"].T),
        bq=inputs["L1_attn_b_bq"], bk=inputs["L1_attn_b_bk"],
        bv=inputs["L1_attn_b_bv"], bo=inputs["L1_attn_b_bo"],
        t_all=t_all_np(inputs["L1_attn_b_pos_bias"]), ident=h["ident"])
    return h


_BUILDERS = dict(
    l0a=("l0a", lambda: build_attn_l0a_program()),
    ffn0=("ffn0", lambda: build_ffn_l0_program()),
    l0b=("l0b", lambda: build_attn_l0b_program()),
    l1a=("l1a", lambda: build_attn_l1a_program()),
    ffn1=("ffn1", lambda: build_ffn_l1_program()),
    l1b=("l1b", lambda: build_attn_l1b_program()),
)


def kernel(**inputs):
    return kernel_fused(**inputs)


# version 11
# speedup vs baseline: 1.1753x; 1.1753x over previous
import contextlib
import math
import numpy as np
import ml_dtypes
import concourse.bass as bass
import concourse.mybir as mybir
from concourse.bass_utils import run_bass_kernel_spmd
from concourse.ap import AP

F32 = mybir.dt.float32
BF16 = mybir.dt.bfloat16
I32 = mybir.dt.int32
AF = mybir.ActivationFunctionType
ALU = mybir.AluOpType
AX = mybir.AxisListType

S = 2048
D = 1024
TT = 16
FC = 8
NEG = -1.0e30

ENGINES = ("sync", "tensor", "vector", "scalar", "gpsimd")


class Sem:
    def __init__(self, handle):
        self.h = handle
        self.n = 0


class SP:
    def __init__(self):
        self.streams = {e: [] for e in ENGINES}

    def on(self, engine, fn):
        self.streams[engine].append(fn)


class Arena:
    def __init__(self, handle, cols):
        self.h = handle
        self.cols = cols
        self.off = 0
        self.marks = []

    def alloc(self, n, align=32):
        o = (self.off + align - 1) // align * align
        assert o + n <= self.cols, f"arena overflow: {o}+{n} > {self.cols}"
        self.off = o + n
        return self.h[:, o:o + n]

    def push(self):
        self.marks.append(self.off)

    def pop(self):
        self.off = self.marks.pop()


def cdiv(a, b):
    return (a + b - 1) // b


class Ctx:
    def S(self, name):
        if name not in self.sem:
            h = self.exits.enter_context(self.nc.semaphore("s_" + name))
            self.sem[name] = Sem(h)
        return self.sem[name]


def make_ctx(nc, exits):
    c = Ctx()
    c.nc = nc
    c.exits = exits
    c.sem = {}
    ent = exits.enter_context
    c.ln_fm = ent(nc.sbuf_tensor("ln_fm", [128, FC * S], BF16))
    c.ident = ent(nc.sbuf_tensor("c_ident", [128, 128], BF16))
    c.ones_row = ent(nc.sbuf_tensor("c_ones", [1, 512], BF16))
    ab_cols = 57344
    af_cols = 8192
    c.ab = Arena(ent(nc.sbuf_tensor("arena_bf", [128, ab_cols], BF16)), ab_cols)
    c.af = Arena(ent(nc.sbuf_tensor("arena_f32", [128, af_cols], F32)), af_cols)
    c.stat = ent(nc.sbuf_tensor("stat", [128, 96], F32))
    c.sq_scr = ent(nc.sbuf_tensor("sq_scr", [128, D], BF16))
    c.xbuf = ent(nc.sbuf_tensor("xbuf", [128, 2 * D], F32))
    c.xnew = ent(nc.sbuf_tensor("xnew", [128, 2 * D], F32))
    c.lntm = ent(nc.sbuf_tensor("lntm", [128, 2 * D], BF16))
    c.bias_row = ent(nc.sbuf_tensor("bias_row", [1, 4096], BF16))
    c.bias_col = ent(nc.sbuf_tensor("bias_col", [128, 32], F32))
    c.ps = [ent(nc.psum_tensor(f"ps{i}", [128, 512], F32)) for i in range(6)]
    c.pt = [ent(nc.psum_tensor(f"pt{i}", [128, 1024], BF16)) for i in range(2)]
    for name in ("mm", "ev", "ac", "gp", "vch", "ach"):
        c.S(name)
    c.ps_free = [("ev", 0)] * 6
    c.pt_free = [("ev", 0)] * 2
    return c


def wait(eng, c, key, v):
    if v > 0:
        eng.wait_ge(c.sem[key].h, v)


def dma(sp, c, queue, sem_name, pairs, gates=()):
    sm = c.S(sem_name)
    sm.n += 16 * len(pairs)
    tot = sm.n

    eg = tuple(getattr(c, "entry_gates", ()))

    def fn(g, pairs=pairs, gates=tuple(gates), eg=eg):
        for (k, v) in eg:
            wait(g, c, k, v)
        for (k, v) in gates:
            wait(g, c, k, v)
        for (out_ap, in_ap) in pairs:
            g.dma_start(out=out_ap, in_=in_ap).then_inc(sm.h, 16)
    sp.on(queue, fn)
    return tot


def emit_prepass(sp, c, pairs, sem="prep", chunk_rows=512):
    dmas = []
    for (dst, src) in pairs:
        rows = src.shape[0]
        step = min(chunk_rows, rows)
        for r0 in range(0, rows, step):
            r1 = min(rows, r0 + step)
            dmas.append((dst[r0:r1], src[r0:r1]))
    tot = dma(sp, c, "gpsimd", sem, dmas)
    return (sem, tot)


def emit_const_init(sp, c, ident_dram):
    sev = c.S("ev")
    idv = dma(sp, c, "sync", "aux", [(c.ident[:, :], ident_dram)])
    sev.n += 1
    done = sev.n

    def ones(v):
        v.memset(c.ones_row[:, :], 1.0)
        v.memset(c.bias_col[:, 0:1], 1e-5).then_inc(sev.h, 1)
    sp.on("vector", ones)

    def bar(te, idv=idv, done=done):
        te.wait_ge(c.sem["aux"].h, idv)
        te.wait_ge(sev.h, done)
    sp.on("tensor", bar)
    c.const_ready = done


def emit_ln(sp, c, x_dram, x_tile_ready=None):
    sac, sev, smm = c.S("ac"), c.S("ev"), c.S("mm")
    eps = 1e-5
    ln2_done = [0] * TT
    for t in range(TT):
        buf = t % 2
        sb = buf * 16
        xsl = c.xbuf[:, buf * D:(buf + 1) * D]
        lnsl = c.lntm[:, buf * D:(buf + 1) * D]

        gates = []
        if x_tile_ready is not None and x_tile_ready[t] is not None:
            gates.append(x_tile_ready[t])
        if t >= 2:
            gates.append(("ev", ln2_done[t - 2]))
        xv = dma(sp, c, "sync", f"xl{buf}",
                 [(xsl, x_dram[t * 128:(t + 1) * 128, :])], gates=gates)

        stat_war = 0 if t < 2 else ln2_done[t - 2]
        sac.n += 1
        sd = sac.n

        def stats(a, xsl=xsl, xv=xv, sb=sb, war=stat_war, buf=buf, lnsl_=lnsl):
            a.wait_ge(c.sem[f"xl{buf}"].h, xv)
            wait(a, c, "ev", war)
            a.activation(c.sq_scr[:, :], xsl, AF.Square,
                         accum_out=c.stat[:, sb + 0:sb + 1])
            a.activation(lnsl_, xsl, AF.Identity,
                         accum_out=c.stat[:, sb + 1:sb + 2]).then_inc(sac.h, 1)
        sp.on("scalar", stats)

        vch = c.S("vch")
        vb = vch.n
        vch.n += 3
        sev.n += 1
        v1 = sev.n

        def lnop(v, sb=sb, sd=sd, vb=vb):
            v.wait_ge(sac.h, sd)
            st = c.stat
            v.tensor_mul(st[:, sb + 2:sb + 3], st[:, sb + 1:sb + 2],
                         st[:, sb + 1:sb + 2]).then_inc(vch.h, 1)
            v.tensor_scalar_mul(st[:, sb + 3:sb + 4], st[:, sb + 1:sb + 2],
                                1.0 / D).then_inc(vch.h, 1)
            v.wait_ge(vch.h, vb + 1)
            v.scalar_tensor_tensor(st[:, sb + 4:sb + 5], st[:, sb + 2:sb + 3],
                                   1.0 / D, st[:, sb + 0:sb + 1],
                                   ALU.mult, ALU.subtract).then_inc(sev.h, 1)
        sp.on("vector", lnop)

        sac.n += 1
        rs = sac.n

        def rstd(a, sb=sb, v1=v1):
            a.wait_ge(sev.h, v1)
            a.activation(c.stat[:, sb + 5:sb + 6], c.stat[:, sb + 4:sb + 5],
                         AF.Sqrt, scale=-1.0 / D,
                         bias=c.bias_col[:, 0:1]).then_inc(sac.h, 1)
        sp.on("scalar", rstd)

        sev.n += 1
        ln2_done[t] = sev.n

        def lnop2(v, xsl=xsl, lnsl=lnsl, sb=sb, rs=rs, vb=vb):
            v.wait_ge(sac.h, rs)
            v.reciprocal(c.stat[:, sb + 6:sb + 7],
                         c.stat[:, sb + 5:sb + 6]).then_inc(vch.h, 1)
            v.wait_ge(vch.h, vb + 3)
            v.tensor_scalar(lnsl, xsl, c.stat[:, sb + 3:sb + 4],
                            c.stat[:, sb + 6:sb + 7],
                            ALU.subtract, ALU.mult).then_inc(sev.h, 1)
        sp.on("vector", lnop2)

        for cc in range(FC):
            pb = (t * FC + cc) % 2
            smm.n += 1
            trv = smm.n
            ptf = c.pt_free[pb]

            def tr(te, lnsl=lnsl, cc=cc, pb=pb, l2=ln2_done[t], ptf=ptf):
                te.wait_ge(sev.h, l2)
                wait(te, c, ptf[0], ptf[1])
                te.transpose(c.pt[pb][:, 0:128], lnsl[:, cc * 128:(cc + 1) * 128],
                             c.ident[:, :]).then_inc(smm.h, 1)
            sp.on("tensor", tr)

            sev.n += 1
            c.pt_free[pb] = ("ev", sev.n)

            def ev(v, cc=cc, t=t, pb=pb, trv=trv):
                v.wait_ge(smm.h, trv)
                v.tensor_copy(c.ln_fm[:, cc * S + t * 128: cc * S + (t + 1) * 128],
                              c.pt[pb][:, 0:128]).then_inc(sev.h, 1)
            sp.on("vector", ev)
    return sev.n


def emit_store_tile(sp, c, xo_dram, t, xnsl, ready_ev):
    stv = dma(sp, c, "sync", f"st{t % 2}",
              [(xo_dram[t * 128:(t + 1) * 128, :], xnsl)],
              gates=[("ev", ready_ev)])
    return (f"st{t % 2}", stv)


def emit_ffn_l0(sp, c, wgT, wuT, wdT, x_dram, xo_dram, ln_ready, prep_gate=None):
    nc = c.nc
    smm, sev, sac = c.S("mm"), c.S("ev"), c.S("ac")
    HB = 24
    c.ab.push()
    h_fm = c.ab.alloc(HB * 512)
    sg2 = [c.ab.alloc(512), c.ab.alloc(512)]
    wg_b = [c.ab.alloc(FC * 128) for _ in range(2)]
    wu_b = [c.ab.alloc(FC * 128) for _ in range(2)]
    wd_b = c.ab.alloc(HB * 1024)
    x_out_ready = [None] * TT

    wd_done = dma(sp, c, "sync", "fwd",
                  [(wd_b[:, m * 1024:(m + 1) * 1024], wdT[m * 128:(m + 1) * 128, :])
                   for m in range(HB)],
                  gates=[prep_gate] if prep_gate else [])

    mm_done_slot = [0, 0]
    hmul_done = {}
    down_done = 0
    evac_seq = 0
    for n in range(4):
        tok = slice(n * 512, (n + 1) * 512)
        for m in range(HB):
            dbl = m % 2
            war = [("mm", mm_done_slot[dbl])] if mm_done_slot[dbl] else []
            lg = dma(sp, c, "sync", f"fw{dbl}", [
                (wg_b[dbl].rearrange("p (c m) -> p c m", c=FC),
                 wgT[:, m * 128:(m + 1) * 128].rearrange("(c p) m -> p c m", p=128)),
                (wu_b[dbl].rearrange("p (c m) -> p c m", c=FC),
                 wuT[:, m * 128:(m + 1) * 128].rearrange("(c p) m -> p c m", p=128)),
            ], gates=war)
            pg = 2 * dbl
            pu = pg + 1
            gate_free = c.ps_free[pg]
            up_free = c.ps_free[pu]
            smm.n += 2
            mm_v = smm.n
            mm_done_slot[dbl] = mm_v

            def mms(te, dbl=dbl, lg=lg, tok=tok, pg=pg, pu=pu,
                    gf=gate_free, uf=up_free, lr=ln_ready):
                te.wait_ge(c.sem[f"fw{dbl}"].h, lg)
                te.wait_ge(sev.h, lr)
                wait(te, c, gf[0], gf[1])
                wait(te, c, uf[0], uf[1])
                for cc in range(FC):
                    ins = te.matmul(c.ps[pg][:, :], wg_b[dbl][:, cc * 128:(cc + 1) * 128],
                                    c.ln_fm[:, cc * S:(cc + 1) * S][:, tok],
                                    start=(cc == 0), stop=(cc == FC - 1))
                    if cc == FC - 1:
                        ins.then_inc(smm.h, 1)
                for cc in range(FC):
                    ins = te.matmul(c.ps[pu][:, :], wu_b[dbl][:, cc * 128:(cc + 1) * 128],
                                    c.ln_fm[:, cc * S:(cc + 1) * S][:, tok],
                                    start=(cc == 0), stop=(cc == FC - 1))
                    if cc == FC - 1:
                        ins.then_inc(smm.h, 1)
            sp.on("tensor", mms)

            sac.n += 1
            sa = sac.n
            sg = sg2[dbl]
            sg_war = hmul_done.get((n, m - 2), hmul_done.get((n - 1, HB - 2 + dbl), 0))

            def silu(a, pg=pg, mm_v=mm_v, sg=sg, war=sg_war):
                a.wait_ge(smm.h, mm_v - 1)
                wait(a, c, "ev", war)
                a.activation(sg, c.ps[pg][:, :], AF.Silu,
                             scale=1.0 / 127.0).then_inc(sac.h, 1)
            sp.on("scalar", silu)
            c.ps_free[pg] = ("ac", sa)

            sev.n += 1
            hv = sev.n
            hmul_done[(n, m)] = hv

            def hmul(v, m=m, pu=pu, sa=sa, mm_v=mm_v, sg=sg, dd=down_done):
                v.wait_ge(sac.h, sa)
                v.wait_ge(smm.h, mm_v)
                wait(v, c, "mm", dd)
                v.scalar_tensor_tensor(h_fm[:, m * 512:(m + 1) * 512],
                                       c.ps[pu][:, :], 1.0 / 127.0, sg,
                                       ALU.mult, ALU.mult).then_inc(sev.h, 1)
            sp.on("vector", hmul)
            c.ps_free[pu] = ("ev", hv)
        h_done = sev.n

        for tt_i in range(4):
            t = n * 4 + tt_i
            buf = t % 2
            xsl = c.xbuf[:, buf * D:(buf + 1) * D]
            xnsl = c.xnew[:, buf * D:(buf + 1) * D]
            gates = [("ev", ln_ready)]
            if t >= 2:
                gates.append(x_out_ready[t - 2])
            xv = dma(sp, c, "sync", f"xl{buf}",
                     [(xsl, x_dram[t * 128:(t + 1) * 128, :])], gates=gates)

            for e in range(2):
                pb = 4 + (evac_seq % 2)
                evac_seq += 1
                bf = c.ps_free[pb]
                smm.n += 1
                mm_v = smm.n

                def mmd(te, tt_i=tt_i, e=e, pb=pb, bf=bf, hd=h_done, wd=wd_done):
                    te.wait_ge(sev.h, hd)
                    te.wait_ge(c.sem["fwd"].h, wd)
                    wait(te, c, bf[0], bf[1])
                    for m in range(HB):
                        ins = te.matmul(c.ps[pb][:, :],
                                        h_fm[:, m * 512 + tt_i * 128: m * 512 + (tt_i + 1) * 128],
                                        wd_b[:, m * 1024 + e * 512: m * 1024 + (e + 1) * 512],
                                        start=(m == 0), stop=(m == HB - 1))
                    ins.then_inc(smm.h, 1)
                sp.on("tensor", mmd)
                down_done = smm.n

                sev.n += 1
                rv = sev.n
                c.ps_free[pb] = ("ev", rv)

                def resid(v, e=e, pb=pb, mm_v=mm_v, xv=xv, xsl=xsl, xnsl=xnsl, buf=buf):
                    v.wait_ge(smm.h, mm_v)
                    v.wait_ge(c.sem[f"xl{buf}"].h, xv)
                    v.scalar_tensor_tensor(
                        xnsl[:, e * 512:(e + 1) * 512], c.ps[pb][:, :],
                        1.0 / 127.0, xsl[:, e * 512:(e + 1) * 512],
                        ALU.mult, ALU.add).then_inc(sev.h, 1)
                sp.on("vector", resid)
            x_out_ready[t] = emit_store_tile(sp, c, xo_dram, t, xnsl, sev.n)
    c.ab.pop()
    return x_out_ready



def emit_bias_loads(sp, c, items):
    pairs = [(c.bias_row[0:1, off:off + n], ap[None, :]) for (ap, off, n) in items]
    return dma(sp, c, "gpsimd", "aux2", pairs)


def emit_fm_group(sp, c, ps_bank, lhs_tiles, rhs_tiles, bias_lhs=None, bias_rhs=None,
                  waits=(), tag=""):
    smm = c.S("mm")
    bf = c.ps_free[ps_bank]
    smm.n += 1
    mm_v = smm.n
    n_k = len(lhs_tiles)

    M = lhs_tiles[0].shape[-1]

    def fn(te, waits=tuple(waits), bf=bf, M=M):
        for (k, v) in waits:
            wait(te, c, k, v)
        wait(te, c, bf[0], bf[1])
        first = True
        if bias_lhs is not None:
            te.matmul(c.ps[ps_bank][0:M, 0:bias_rhs.shape[-1]], bias_lhs, bias_rhs,
                      start=True, stop=False)
            first = False
        for i in range(n_k):
            ins = te.matmul(c.ps[ps_bank][0:M, 0:rhs_tiles[i].shape[-1]],
                            lhs_tiles[i], rhs_tiles[i],
                            start=first, stop=(i == n_k - 1))
            first = False
        ins.then_inc(smm.h, 1)
    sp.on("tensor", fn)
    return mm_v


def emit_evac_copy(sp, c, ps_bank, width, out_ap, mm_v, engine="vector"):
    if engine == "vector":
        sev = c.S("ev")
        sev.n += 1
        vv = sev.n

        def fn(v, mm_v=mm_v):
            v.wait_ge(c.sem["mm"].h, mm_v)
            v.tensor_copy(out_ap, c.ps[ps_bank][:, 0:width]).then_inc(sev.h, 1)
        sp.on("vector", fn)
        c.ps_free[ps_bank] = ("ev", vv)
        return ("ev", vv)
    else:
        sac = c.S("ac")
        sac.n += 1
        vv = sac.n

        def fn(a, mm_v=mm_v):
            a.wait_ge(c.sem["mm"].h, mm_v)
            a.copy(out_ap, c.ps[ps_bank][:, 0:width]).then_inc(sac.h, 1)
        sp.on("scalar", fn)
        c.ps_free[ps_bank] = ("ac", vv)
        return ("ac", vv)


def emit_proj_residual(sp, c, a_fm_chunks, w_b_getter, bias_off, x_dram, xo_dram,
                       w_ready, a_ready_fn, ln_ready, scale=1.0, bias_gate=None):
    sev = c.S("ev")
    x_out_ready = [None] * TT
    evac_seq = getattr(c, "_proj_seq", 0)
    for t in range(TT):
        buf = t % 2
        xsl = c.xbuf[:, buf * D:(buf + 1) * D]
        xnsl = c.xnew[:, buf * D:(buf + 1) * D]
        gates = [("ev", ln_ready)]
        if t >= 2:
            gates.append(x_out_ready[t - 2])
        xv = dma(sp, c, "sync", f"xl{buf}",
                 [(xsl, x_dram[t * 128:(t + 1) * 128, :])], gates=gates)
        for e in range(2):
            pb = 4 + (evac_seq % 2)
            evac_seq += 1
            waits = [a_ready_fn(t), ("wp", w_ready)]
            if bias_gate is not None:
                waits.append(bias_gate)
            bias_lhs = c.ones_row[0:1, 0:128] if bias_off is not None else None
            bias_rhs = (c.bias_row[0:1, bias_off + e * 512: bias_off + (e + 1) * 512]
                        if bias_off is not None else None)
            mm_v = emit_fm_group(
                sp, c, pb,
                [a_fm_chunks(cc, t) for cc in range(FC)],
                [w_b_getter(cc, e) for cc in range(FC)],
                bias_lhs=bias_lhs, bias_rhs=bias_rhs, waits=waits)
            sev.n += 1
            rv = sev.n
            c.ps_free[pb] = ("ev", rv)

            def resid(v, e=e, pb=pb, mm_v=mm_v, xv=xv, xsl=xsl, xnsl=xnsl,
                      buf=buf, scale=scale):
                v.wait_ge(c.sem["mm"].h, mm_v)
                v.wait_ge(c.sem[f"xl{buf}"].h, xv)
                v.scalar_tensor_tensor(
                    xnsl[:, e * 512:(e + 1) * 512], c.ps[pb][:, :],
                    scale, xsl[:, e * 512:(e + 1) * 512],
                    ALU.mult, ALU.add).then_inc(sev.h, 1)
            sp.on("vector", resid)
        x_out_ready[t] = emit_store_tile(sp, c, xo_dram, t, xnsl, sev.n)
    c._proj_seq = evac_seq
    return x_out_ready


def emit_softmax(sp, c, strips, scale, par, stat_base=32):
    sev, sac, vch = c.S("ev"), c.S("ac"), c.S("vch")
    for h, st in enumerate(strips):
        sb = stat_base + par * 8 + h * 4
        W = st["W"]
        s1 = st["s1"]
        seg_done = st.get("pre_done", 0)
        for (pbk, off, w, mm_v, bias_ap) in st["segments"]:
            sev.n += 1
            ev_v = sev.n
            c.ps_free[pbk] = ("ev", ev_v)

            def evac(v, pbk=pbk, off=off, w=w, mm_v=mm_v, bias_ap=bias_ap,
                     s1=s1, war=st.get("s1_war", 0), sg=st.get("seg_gate")):
                v.wait_ge(c.sem["mm"].h, mm_v)
                wait(v, c, "ac", war)
                if sg is not None:
                    wait(v, c, sg[0], sg[1])
                if bias_ap is not None:
                    v.scalar_tensor_tensor(s1[:, off:off + w], c.ps[pbk][:, 0:w],
                                           scale, bias_ap, ALU.mult, ALU.add)\
                     .then_inc(sev.h, 1)
                else:
                    v.tensor_scalar(s1[:, off:off + w], c.ps[pbk][:, 0:w],
                                    scale, None, ALU.mult).then_inc(sev.h, 1)
            sp.on("vector", evac)
            seg_done = ev_v
        st["seg_evac_done"] = seg_done
        vch.n += 1
        nm_v = vch.n
        sd = seg_done

        def rmax(v, s1=s1, sb=sb, W=W, sd=sd):
            v.wait_ge(sev.h, sd)
            v.reduce_max(c.stat[:, sb:sb + 1], s1[:, 0:W], axis=AX.X,
                         negate=True).then_inc(vch.h, 1)
        sp.on("vector", rmax)
        sac.n += 1
        ex_v = sac.n
        p_war_h = st.get("p_war", 0)
        sum_war = st.get("sum_war", 0)

        def expf(a, s1=s1, p=st["p"], sb=sb, W=W, nm_v=nm_v, pw=p_war_h,
                 sw=sum_war):
            a.wait_ge(c.sem["vch"].h, max(nm_v, sw))
            wait(a, c, "mm", pw)
            a.activation(p[:, 0:W], s1[:, 0:W], AF.Exp,
                         bias=c.stat[:, sb:sb + 1],
                         accum_out=c.stat[:, sb + 1:sb + 2]).then_inc(sac.h, 1)
        sp.on("scalar", expf)
        st["exp_done"] = ex_v
        st["_sb"] = sb
    return strips


def finish_softmax(sp, c, strips):
    sev, sac, vch = c.S("ev"), c.S("ac"), c.S("vch")
    p_ready = []
    for st in strips:
        sb = st["_sb"]
        W = st["W"]
        ex_v = st["exp_done"]
        vch.n += 2
        rc_v = vch.n - 1
        sev.n += 1
        pn_v = sev.n
        st["recip_done"] = rc_v

        def norm(v, p=st["p"], sb=sb, W=W, ex_v=ex_v, rc_v=rc_v):
            v.wait_ge(sac.h, ex_v)
            v.reciprocal(c.stat[:, sb + 2:sb + 3],
                         c.stat[:, sb + 1:sb + 2]).then_inc(vch.h, 1)
            v.wait_ge(c.sem["vch"].h, rc_v)
            v.tensor_scalar_mul(p[:, 0:W], p[:, 0:W],
                                c.stat[:, sb + 2:sb + 3]).then_inc(vch.h, 1)
            v.engine_nop().then_inc(sev.h, 1)
        sp.on("vector", norm)
        p_ready.append(pn_v)
    return p_ready


def emit_pv2(sp, c, p_strips, p_ready, kc_list, v_tm, heads, a_banks, qw,
             pt_slots, slot_war):
    smm, sev = c.S("mm"), c.S("ev")
    nk = len(kc_list)
    af = [c.ps_free[a_banks[0]], c.ps_free[a_banks[1]]]
    for h in (0, 1):
        gi = 0
        for g0 in range(0, nk, 4):
            gn = min(4, nk - g0)
            pb = gi % 2
            gi += 1
            ptf = c.pt_free[pb]
            trv0 = smm.n
            for u in range(gn):
                j = g0 + u
                smm.n += 1

                def tr(te, h=h, j=j, u=u, pb=pb, pr=p_ready[h], ptf=ptf,
                       p=p_strips[h], first=(u == 0)):
                    if first:
                        te.wait_ge(sev.h, pr)
                        wait(te, c, ptf[0], ptf[1])
                    te.transpose(c.pt[pb][:, u * 128:(u + 1) * 128],
                                 p[:, j * 128:(j + 1) * 128],
                                 c.ident[:, :]).then_inc(smm.h, 1)
                sp.on("tensor", tr)
            tr_v = smm.n
            slot = pt_slots[h][(g0 // 4) % 2]
            sw = slot_war.get((h, (g0 // 4) % 2), 0)
            if gi % 2:
                sev2 = c.S("ev")
                sev2.n += 1
                pt_done = ("ev", sev2.n)
                c.pt_free[pb] = pt_done

                def ptevac(v, pb=pb, tr_v=tr_v, slot=slot, sw=sw, gn=gn):
                    v.wait_ge(smm.h, tr_v)
                    wait(v, c, "mm", sw)
                    v.tensor_copy(slot[:, 0:gn * 128],
                                  c.pt[pb][:, 0:gn * 128]).then_inc(c.sem["ev"].h, 1)
                sp.on("vector", ptevac)
            else:
                sac = c.S("ac")
                sac.n += 1
                pt_done = ("ac", sac.n)
                c.pt_free[pb] = pt_done

                def ptevac(a, pb=pb, tr_v=tr_v, slot=slot, sw=sw, gn=gn):
                    a.wait_ge(smm.h, tr_v)
                    wait(a, c, "mm", sw)
                    a.copy(slot[:, 0:gn * 128],
                           c.pt[pb][:, 0:gn * 128]).then_inc(sac.h, 1)
                sp.on("scalar", ptevac)
            pv_last = 0
            for u in range(gn):
                j = g0 + u
                kc = kc_list[j]
                smm.n += 1
                pv_last = smm.n
                afw = af[h] if j == 0 else None

                def pv(te, h=h, kc=kc, u=u, slot=slot, pt_done=pt_done, afw=afw,
                       first=(j == 0), last=(j == nk - 1)):
                    wait(te, c, pt_done[0], pt_done[1])
                    if afw is not None:
                        wait(te, c, afw[0], afw[1])
                    hh = heads[h]
                    te.matmul(c.ps[a_banks[h]][64 * h:64 * h + 64, 0:qw],
                              v_tm[:, kc * 1024 + 64 * hh: kc * 1024 + 64 * hh + 64],
                              slot[:, u * qw:(u + 1) * qw] if qw == 128 else slot[:, 0:qw],
                              start=first, stop=last, skip_group_check=True,
                              tile_position=(0, 64 * h)).then_inc(smm.h, 1)
                sp.on("tensor", pv)
            slot_war[(h, (g0 // 4) % 2)] = pv_last
    return smm.n


def a_evac2(sp, c, a_banks, qw, out_ap_fn, mm_v):
    sac = c.S("ac")
    for h in (0, 1):
        sac.n += 1

        def ev(a, h=h, mm_v=mm_v):
            a.wait_ge(c.sem["mm"].h, mm_v)
            a.copy(out_ap_fn(h),
                   c.ps[a_banks[h]][64 * h:64 * h + 64, 0:qw]).then_inc(sac.h, 1)
        sp.on("scalar", ev)
        c.ps_free[a_banks[h]] = ("ac", sac.n)
    return sac.n


def a_evac(sp, c, a_ps, qw, out_ap, mm_v):
    return emit_evac_copy(sp, c, a_ps, qw, out_ap, mm_v, engine="vector")


def emit_attn_l0a(sp, c, wqkv, bqkv, wout, bout, x_dram, xo_dram, ln_ready):
    nc = c.nc
    smm, sev, sac = c.S("mm"), c.S("ev"), c.S("ac")
    c.ab.push()
    c.af.push()
    v_tm = c.ab.alloc(TT * 1024)
    a_fm = c.ab.alloc(FC * S)
    q_fm = c.ab.alloc(S)
    k_fm = c.ab.alloc(S)
    p_str2 = [[c.ab.alloc(512) for _ in range(3)], [c.ab.alloc(512) for _ in range(3)]]
    pt_sl = [[c.ab.alloc(384), c.ab.alloc(384)], [c.ab.alloc(384), c.ab.alloc(384)]]
    c.ab.push()
    wv_b = c.ab.alloc(FC * 512)
    s1_str = [c.af.alloc(512), c.af.alloc(512)]
    mask_sb = c.af.alloc(384)

    bias_v = emit_bias_loads(sp, c, [(bqkv, 0, 3072), (bout, 3072, 1024)])
    mask_v = dma(sp, c, "sync", "aux", [(mask_sb, c.mask_l0a_dram)])

    v_done_tile = [0] * TT
    for e in range(2):
        wv_v = dma(sp, c, "gpsimd", "wd",
                   [(wv_b.rearrange("p (c m) -> p c m", c=FC),
                     wqkv[:, 2048 + e * 512: 2048 + (e + 1) * 512]
                     .rearrange("(c p) m -> p c m", p=128))],
                   gates=[("mm", c._wv_war)] if c._wv_war else [])
        last_mm = 0
        for kc in range(TT):
            pbk = kc % 4
            mm_v = emit_fm_group(
                sp, c, pbk,
                [c.ln_fm[:, cc * S + kc * 128: cc * S + (kc + 1) * 128]
                 for cc in range(FC)],
                [wv_b[:, cc * 512:(cc + 1) * 512] for cc in range(FC)],
                bias_lhs=c.ones_row[0:1, 0:128],
                bias_rhs=c.bias_row[0:1, 2048 + e * 512: 2048 + (e + 1) * 512],
                waits=[("wd", wv_v), ("ev", ln_ready), ("aux2", bias_v)])
            emit_evac_copy(sp, c, pbk, 512,
                           v_tm[:, kc * 1024 + e * 512: kc * 1024 + (e + 1) * 512],
                           mm_v)
            last_mm = mm_v
            v_done_tile[kc] = max(v_done_tile[kc], c.sem["ev"].n)
        c._wv_war = last_mm
    v_done = sev.n
    v_mm_done = c.S("mm").n
    c.ab.pop()
    wqk_b2 = [c.ab.alloc(FC * 256), c.ab.alloc(FC * 256)]
    wout_b = c.ab.alloc(FC * 1024)

    war = {}
    pt_slot_war = {}
    seq = 0

    def _l0a_fin(pend):
        blk, par, strips, kc_list = pend
        pr = finish_softmax(sp, c, strips)
        for h in (0, 1):
            war[(blk, h, "recip")] = strips[h]["recip_done"]
        return pr

    def _l0a_pv(pend, p_ready):
        blk, par, strips, kc_list = pend
        pp = blk // TT
        nb = blk % TT
        pv_v = emit_pv2(sp, c, [p_str2[0][par], p_str2[1][par]], p_ready,
                        kc_list, v_tm, (2 * pp, 2 * pp + 1), (4, 5), 128,
                        pt_sl, pt_slot_war)
        for h in (0, 1):
            war[(blk, h, "tr")] = pv_v
        a_evac2(sp, c, (4, 5), 128,
                lambda h, pp=pp, nb=nb: a_fm[64 * h:64 * h + 64,
                                             pp * S + nb * 128:pp * S + (nb + 1) * 128],
                pv_v)

    def _l0a_flush(pend):
        _l0a_pv(pend, _l0a_fin(pend))

    pendq = [None, None]
    for p in range(8):
        slot = p % 2
        wqk_b = wqk_b2[slot]
        prev_scores_done = c.S("mm").n
        wqk_v = dma(sp, c, "gpsimd", f"w{slot}", [
            (wqk_b[:, 0:FC * 128].rearrange("p (c m) -> p c m", c=FC),
             wqkv[:, 128 * p:128 * (p + 1)].rearrange("(c p2) m -> p2 c m", p2=128)),
            (wqk_b[:, FC * 128:FC * 256].rearrange("p (c m) -> p c m", c=FC),
             wqkv[:, 1024 + 128 * p: 1024 + 128 * (p + 1)]
             .rearrange("(c p2) m -> p2 c m", p2=128)),
        ], gates=[("mm", c._wqk_war.get(slot, v_mm_done))])
        qk_first = True
        for which, dst, boff in ((0, q_fm, 128 * p), (1, k_fm, 1024 + 128 * p)):
            for n in range(4):
                pbk = seq % 4
                seq += 1
                mm_v = emit_fm_group(
                    sp, c, pbk,
                    [wqk_b[:, which * FC * 128 + cc * 128: which * FC * 128 + (cc + 1) * 128]
                     for cc in range(FC)],
                    [c.ln_fm[:, cc * S + n * 512: cc * S + (n + 1) * 512]
                     for cc in range(FC)],
                    bias_lhs=c.bias_row[0:1, boff:boff + 128],
                    bias_rhs=c.ones_row[0:1, 0:512],
                    waits=[(f"w{slot}", wqk_v), ("ev", ln_ready), ("aux2", bias_v)])
                if qk_first:
                    sev.n += 1
                    vv = sev.n

                    def evq(v, mm_v=mm_v, dst=dst, n=n, psd=prev_scores_done, pbk=pbk):
                        v.wait_ge(c.sem["mm"].h, mm_v)
                        wait(v, c, "mm", psd)
                        v.tensor_copy(dst[:, n * 512:(n + 1) * 512],
                                      c.ps[pbk][:, :]).then_inc(sev.h, 1)
                    sp.on("vector", evq)
                    c.ps_free[pbk] = ("ev", vv)
                    qk_first = False
                else:
                    emit_evac_copy(sp, c, pbk, 512, dst[:, n * 512:(n + 1) * 512], mm_v)
        qk_done = sev.n
        c._wqk_war[slot] = c.S("mm").n

        for nb in range(TT):
            blk = p * TT + nb
            lo = max(0, (nb - 1) * 128)
            hi = min(S, (nb + 2) * 128)
            w = hi - lo
            moff = 128 if nb == 0 else 0
            par = blk % 3
            strips = []
            pend_pr = _l0a_fin(pendq[0]) if pendq[0] is not None else None
            for h in (0, 1):
                pbk = (nb * 2 + h) % 4
                smm.n += 1
                mm_v = smm.n
                bf = c.ps_free[pbk]

                def smat(te, h=h, nb=nb, lo=lo, hi=hi, w=w, pbk=pbk, bf=bf,
                         qd=qk_done, vd=v_done, mv=mask_v):
                    te.wait_ge(sev.h, max(qd, vd))
                    te.wait_ge(c.sem["aux"].h, mv)
                    wait(te, c, bf[0], bf[1])
                    te.matmul(c.ps[pbk][:, 0:w],
                              q_fm[64 * h:64 * h + 64, nb * 128:(nb + 1) * 128],
                              k_fm[64 * h:64 * h + 64, lo:hi],
                              start=True, stop=True,
                              tile_position=(64 * h, 0)).then_inc(smm.h, 1)
                sp.on("tensor", smat)
                strip = dict(
                    W=w, s1=s1_str[h], p=p_str2[h][par],
                    segments=[(pbk, 0, w, mm_v, mask_sb[:, moff:moff + w])],
                    s1_war=war.get((blk - 1, h, "exp"), 0),
                    p_war=war.get((blk - 3, h, "tr"), 0),
                    sum_war=war.get((blk - 3, h, "recip"), 0),
                )
                emit_softmax(sp, c, [strip], 1.0 / 32.0, par, stat_base=32 + 4 * h)
                strips.append(strip)
                war[(blk, h, "exp")] = strip["exp_done"]
            if pendq[0] is not None:
                _l0a_pv(pendq[0], pend_pr)
            pendq = [pendq[1], (blk, par, strips, list(range(lo // 128, hi // 128)))]
    for pe_ in pendq:
        if pe_ is not None:
            _l0a_flush(pe_)
    a_done_ac = c.S("ac").n

    wo_v = dma(sp, c, "gpsimd", "wp",
               [(wout_b.rearrange("p (c m) -> p c m", c=FC),
                 wout.rearrange("(c p) m -> p c m", p=128))],
               gates=[("mm", c.S("mm").n), ("ac", a_done_ac), ("ev", sev.n)])
    xor = emit_proj_residual(
        sp, c,
        lambda cc, t: a_fm[:, cc * S + t * 128: cc * S + (t + 1) * 128],
        lambda cc, e: wout_b[:, cc * 1024 + e * 512: cc * 1024 + (e + 1) * 512],
        3072, x_dram, xo_dram, wo_v,
        lambda t: ("ac", a_done_ac), ln_ready, bias_gate=("aux2", bias_v))
    c.ab.pop()
    c.af.pop()
    return xor



def emit_attn_l0b(sp, c, wqkvT, woutT, b_out, x_dram, xo_dram, ln_ready):
    nc = c.nc
    smm, sev, sac = c.S("mm"), c.S("ev"), c.S("ac")
    c.ab.push()
    c.af.push()
    v_tm = c.ab.alloc(TT * 1024)
    a_fm = c.ab.alloc(FC * S)
    q_fm = c.ab.alloc(S)
    k_fm = c.ab.alloc(S)
    p_str2 = [[c.ab.alloc(512) for _ in range(3)], [c.ab.alloc(512) for _ in range(3)]]
    pg_str2 = [[c.ab.alloc(16) for _ in range(3)], [c.ab.alloc(16) for _ in range(3)]]
    pt_sl = [[c.ab.alloc(256)], [c.ab.alloc(256)]]
    pgt_sl = [c.ab.alloc(128), c.ab.alloc(128)]
    ln_g = c.ab.alloc(FC * 16)
    vg_tm = c.ab.alloc(1024)
    kg_b = c.ab.alloc(16)
    s1_str = [c.af.alloc(512), c.af.alloc(512)]
    s1g_str = [c.af.alloc(16), c.af.alloc(16)]
    mask_sb = c.af.alloc(256)
    c.ab.push()
    wv_b = c.ab.alloc(FC * 512)

    bias_v = emit_bias_loads(sp, c, [(b_out, 0, 1024)])
    mask_v = dma(sp, c, "sync", "aux", [(mask_sb, c.mask_l0b_dram)])

    sev.n += 1
    lng_v = sev.n

    def lngf(v, lr=ln_ready):
        v.wait_ge(sev.h, lr)
        for cc in range(FC):
            src = AP(c.ln_fm, cc * S, [[FC * S, 128], [128, 16]])
            v.tensor_copy(ln_g[:, cc * 16:(cc + 1) * 16], src)
        v.engine_nop().then_inc(sev.h, 1)
    sp.on("vector", lngf)

    for e in range(2):
        wv_v = dma(sp, c, "gpsimd", "wd",
                   [(wv_b.rearrange("p (c m) -> p c m", c=FC),
                     wqkvT[:, 2048 + e * 512: 2048 + (e + 1) * 512]
                     .rearrange("(c p) m -> p c m", p=128))],
                   gates=[("mm", c._wv_war)] if c._wv_war else [])
        last_mm = 0
        for kc in range(TT):
            pbk = kc % 4
            mm_v = emit_fm_group(
                sp, c, pbk,
                [c.ln_fm[:, cc * S + kc * 128: cc * S + (kc + 1) * 128]
                 for cc in range(FC)],
                [wv_b[:, cc * 512:(cc + 1) * 512] for cc in range(FC)],
                waits=[("wd", wv_v), ("ev", ln_ready)])
            emit_evac_copy(sp, c, pbk, 512,
                           v_tm[:, kc * 1024 + e * 512: kc * 1024 + (e + 1) * 512],
                           mm_v)
            last_mm = mm_v
        pbk = 16 % 4
        mm_v = emit_fm_group(
            sp, c, pbk,
            [ln_g[:, cc * 16:(cc + 1) * 16] for cc in range(FC)],
            [wv_b[:, cc * 512:(cc + 1) * 512] for cc in range(FC)],
            waits=[("wd", wv_v), ("ev", lng_v)])
        sev.n += 1
        vv = sev.n

        def evg(v, mm_v=mm_v, e=e, pbk=pbk):
            v.wait_ge(c.sem["mm"].h, mm_v)
            v.tensor_copy(vg_tm[0:16, e * 512:(e + 1) * 512],
                          c.ps[pbk][0:16, :]).then_inc(sev.h, 1)
        sp.on("vector", evg)
        c.ps_free[pbk] = ("ev", vv)
        c._wv_war = c.S("mm").n
    v_done = sev.n
    v_mm_done = c.S("mm").n
    c.ab.pop()
    wqk_b2 = [c.ab.alloc(FC * 256), c.ab.alloc(FC * 256)]
    wout_b = c.ab.alloc(FC * 1024)

    war = {}
    pt_slot_war = {}
    pgt_slot_war = {}
    seq = 0

    def _l0b_fin(pend):
        blk, par, strips, stripsg, kc_list = pend
        pr = finish_softmax(sp, c, strips)
        prg = finish_softmax(sp, c, stripsg)
        for h in (0, 1):
            war[(blk, h, "recip")] = strips[h]["recip_done"]
            war[(blk, h, "grecip")] = stripsg[h]["recip_done"]
        return (pr, prg)

    def _l0b_pv(pend, prs):
        blk, par, strips, stripsg, kc_list = pend
        p_ready, pg_ready = prs
        pp = blk // TT
        nb = blk % TT
        pv_v = emit_pv_l0b(sp, c, [p_str2[0][par], p_str2[1][par]],
                           [pg_str2[0][par], pg_str2[1][par]], p_ready, pg_ready,
                           kc_list, v_tm, vg_tm, (2 * pp, 2 * pp + 1), (4, 5),
                           pt_sl, pgt_sl, pt_slot_war, pgt_slot_war)
        for h in (0, 1):
            war[(blk, h, "tr")] = pv_v
        a_evac2(sp, c, (4, 5), 128,
                lambda h, pp=pp, nb=nb: a_fm[64 * h:64 * h + 64,
                                             pp * S + nb * 128:pp * S + (nb + 1) * 128],
                pv_v)

    def _l0b_flush(pend):
        _l0b_pv(pend, _l0b_fin(pend))

    pendq = [None, None]
    for p in range(8):
        slot = p % 2
        wqk_b = wqk_b2[slot]
        prev_scores_done = c.S("mm").n
        wqk_v = dma(sp, c, "gpsimd", f"w{slot}", [
            (wqk_b[:, 0:FC * 128].rearrange("p (c m) -> p c m", c=FC),
             wqkvT[:, 128 * p:128 * (p + 1)].rearrange("(c p2) m -> p2 c m", p2=128)),
            (wqk_b[:, FC * 128:FC * 256].rearrange("p (c m) -> p c m", c=FC),
             wqkvT[:, 1024 + 128 * p: 1024 + 128 * (p + 1)]
             .rearrange("(c p2) m -> p2 c m", p2=128)),
        ], gates=[("mm", c._wqk_war.get(slot, v_mm_done))])
        qk_first = True
        for which, dst in ((0, q_fm), (1, k_fm)):
            for n in range(4):
                pbk = seq % 4
                seq += 1
                mm_v = emit_fm_group(
                    sp, c, pbk,
                    [wqk_b[:, which * FC * 128 + cc * 128: which * FC * 128 + (cc + 1) * 128]
                     for cc in range(FC)],
                    [c.ln_fm[:, cc * S + n * 512: cc * S + (n + 1) * 512]
                     for cc in range(FC)],
                    waits=[(f"w{slot}", wqk_v), ("ev", ln_ready)])
                if qk_first:
                    sev.n += 1
                    vv = sev.n

                    def evq(v, mm_v=mm_v, dst=dst, n=n, psd=prev_scores_done, pbk=pbk):
                        v.wait_ge(c.sem["mm"].h, mm_v)
                        wait(v, c, "mm", psd)
                        v.tensor_copy(dst[:, n * 512:(n + 1) * 512],
                                      c.ps[pbk][:, :]).then_inc(sev.h, 1)
                    sp.on("vector", evq)
                    c.ps_free[pbk] = ("ev", vv)
                    qk_first = False
                else:
                    emit_evac_copy(sp, c, pbk, 512, dst[:, n * 512:(n + 1) * 512], mm_v)
        pbk = seq % 4
        seq += 1
        mm_v = emit_fm_group(
            sp, c, pbk,
            [wqk_b[:, FC * 128 + cc * 128: FC * 128 + (cc + 1) * 128] for cc in range(FC)],
            [ln_g[:, cc * 16:(cc + 1) * 16] for cc in range(FC)],
            waits=[(f"w{slot}", wqk_v), ("ev", lng_v)])
        sev.n += 1
        kgv = sev.n

        def evkg(v, mm_v=mm_v, pbk=pbk, psd=prev_scores_done):
            v.wait_ge(c.sem["mm"].h, mm_v)
            wait(v, c, "mm", psd)
            v.tensor_copy(kg_b, c.ps[pbk][:, 0:16]).then_inc(sev.h, 1)
        sp.on("vector", evkg)
        c.ps_free[pbk] = ("ev", kgv)
        qk_done = sev.n
        c._wqk_war[slot] = c.S("mm").n

        for nb in range(TT):
            blk = p * TT + nb
            lo = max(0, (nb - 1) * 128)
            hi = (nb + 1) * 128
            w = hi - lo
            moff = 256 - w
            par = blk % 3
            strips = []
            stripsg = []
            pend_pr = _l0b_fin(pendq[0]) if pendq[0] is not None else None
            for h in (0, 1):
                pbk = (nb * 2 + h) % 4
                smm.n += 1
                mm_v = smm.n
                bf = c.ps_free[pbk]

                def smat(te, h=h, nb=nb, lo=lo, hi=hi, w=w, pbk=pbk, bf=bf,
                         qd=qk_done, vd=v_done, mv=mask_v):
                    te.wait_ge(sev.h, max(qd, vd))
                    te.wait_ge(c.sem["aux"].h, mv)
                    wait(te, c, bf[0], bf[1])
                    te.matmul(c.ps[pbk][:, 0:w],
                              q_fm[64 * h:64 * h + 64, nb * 128:(nb + 1) * 128],
                              k_fm[64 * h:64 * h + 64, lo:hi],
                              start=True, stop=True,
                              tile_position=(64 * h, 0)).then_inc(smm.h, 1)
                sp.on("tensor", smat)
                pbk2 = (nb * 2 + h + 2) % 4
                smm.n += 1
                mmg_v = smm.n
                bfg = c.ps_free[pbk2]

                def smatg(te, h=h, nb=nb, pbk2=pbk2, bfg=bfg, qd=qk_done):
                    te.wait_ge(sev.h, qd)
                    wait(te, c, bfg[0], bfg[1])
                    te.matmul(c.ps[pbk2][:, 0:16],
                              q_fm[64 * h:64 * h + 64, nb * 128:(nb + 1) * 128],
                              kg_b[64 * h:64 * h + 64, 0:16],
                              start=True, stop=True,
                              tile_position=(64 * h, 0)).then_inc(smm.h, 1)
                sp.on("tensor", smatg)
                strip = dict(
                    W=w, s1=s1_str[h], p=p_str2[h][par],
                    segments=[(pbk, 0, w, mm_v, mask_sb[:, moff:moff + w])],
                    s1_war=war.get((blk - 1, h, "exp"), 0),
                    p_war=war.get((blk - 3, h, "tr"), 0),
                    sum_war=war.get((blk - 3, h, "recip"), 0),
                )
                emit_softmax(sp, c, [strip], 1.0 / 8.0, par, stat_base=32 + 4 * h)
                strips.append(strip)
                war[(blk, h, "exp")] = strip["exp_done"]
                stripg = dict(
                    W=16, s1=s1g_str[h], p=pg_str2[h][par],
                    segments=[(pbk2, 0, 16, mmg_v, None)],
                    s1_war=war.get((blk - 1, h, "gexp"), 0),
                    p_war=war.get((blk - 3, h, "tr"), 0),
                    sum_war=war.get((blk - 3, h, "grecip"), 0),
                )
                emit_softmax(sp, c, [stripg], 1.0 / 8.0, par, stat_base=56 + 4 * h)
                stripsg.append(stripg)
                war[(blk, h, "gexp")] = stripg["exp_done"]
            if pendq[0] is not None:
                _l0b_pv(pendq[0], pend_pr)
            pendq = [pendq[1],
                     (blk, par, strips, stripsg, list(range(lo // 128, hi // 128)))]
    for pe_ in pendq:
        if pe_ is not None:
            _l0b_flush(pe_)
    a_done_ac = c.S("ac").n

    wo_v = dma(sp, c, "gpsimd", "wp",
               [(wout_b.rearrange("p (c m) -> p c m", c=FC),
                 woutT.rearrange("(c p) m -> p c m", p=128))],
               gates=[("mm", c.S("mm").n), ("ac", a_done_ac), ("ev", sev.n)])
    xor = emit_proj_residual(
        sp, c,
        lambda cc, t: a_fm[:, cc * S + t * 128: cc * S + (t + 1) * 128],
        lambda cc, e: wout_b[:, cc * 1024 + e * 512: cc * 1024 + (e + 1) * 512],
        0, x_dram, xo_dram, wo_v,
        lambda t: ("ac", a_done_ac), ln_ready, bias_gate=("aux2", bias_v))
    c.ab.pop()
    c.af.pop()
    return xor


def emit_pv_l0b(sp, c, p_str, pg_str, p_ready, pg_ready, kc_list, v_tm, vg_tm,
                heads, a_banks, pt_sl, pgt_sl, slot_war, gslot_war):
    smm, sev = c.S("mm"), c.S("ev")
    nk = len(kc_list)
    af = [c.ps_free[a_banks[0]], c.ps_free[a_banks[1]]]
    for h in (0, 1):
        pb = h % 2
        ptf = c.pt_free[pb]
        for u in range(nk):
            smm.n += 1

            def tr(te, h=h, u=u, pb=pb, pr=p_ready[h], ptf=ptf,
                   p=p_str[h], first=(u == 0)):
                if first:
                    te.wait_ge(sev.h, pr)
                    wait(te, c, ptf[0], ptf[1])
                te.transpose(c.pt[pb][:, u * 128:(u + 1) * 128],
                             p[:, u * 128:(u + 1) * 128],
                             c.ident[:, :]).then_inc(smm.h, 1)
            sp.on("tensor", tr)
        smm.n += 1
        tr_v = smm.n

        def trg(te, h=h, pb=pb, pgr=pg_ready[h], nk=nk, p=pg_str[h]):
            te.wait_ge(sev.h, pgr)
            te.transpose(c.pt[pb][0:16, nk * 128:(nk + 1) * 128],
                         p[:, 0:16], c.ident[:, :]).then_inc(smm.h, 1)
        sp.on("tensor", trg)
        slot = pt_sl[h][0]
        gslot = pgt_sl[h]
        sw = max(slot_war.get(h, 0), gslot_war.get(h, 0))
        sev.n += 1
        pt_v = sev.n
        c.pt_free[pb] = ("ev", pt_v)

        def ptevac(v, pb=pb, tr_v=tr_v, slot=slot, gslot=gslot, sw=sw, nk=nk):
            v.wait_ge(smm.h, tr_v)
            wait(v, c, "mm", sw)
            v.tensor_copy(slot[:, 0:nk * 128], c.pt[pb][:, 0:nk * 128])
            v.tensor_copy(gslot[0:16, 0:128],
                          c.pt[pb][0:16, nk * 128:(nk + 1) * 128]).then_inc(sev.h, 1)
        sp.on("vector", ptevac)
        pv_last = 0
        for u in range(nk):
            kc = kc_list[u]
            smm.n += 1
            pv_last = smm.n
            afw = af[h] if u == 0 else None

            def pv(te, h=h, kc=kc, u=u, slot=slot, pt_v=pt_v, afw=afw,
                   first=(u == 0)):
                te.wait_ge(sev.h, pt_v)
                if afw is not None:
                    wait(te, c, afw[0], afw[1])
                hh = heads[h]
                te.matmul(c.ps[a_banks[h]][64 * h:64 * h + 64, 0:128],
                          v_tm[:, kc * 1024 + 64 * hh: kc * 1024 + 64 * hh + 64],
                          slot[:, u * 128:(u + 1) * 128],
                          start=first, stop=False, skip_group_check=True,
                          tile_position=(0, 64 * h)).then_inc(smm.h, 1)
            sp.on("tensor", pv)
        smm.n += 1
        pv_last = smm.n
        slot_war[h] = pv_last
        gslot_war[h] = pv_last

        def pvg(te, h=h, gslot=gslot, pt_v=pt_v):
            te.wait_ge(sev.h, pt_v)
            hh = heads[h]
            te.matmul(c.ps[a_banks[h]][64 * h:64 * h + 64, 0:128],
                      vg_tm[0:16, 64 * hh:64 * hh + 64],
                      gslot[0:16, :],
                      start=False, stop=True, skip_group_check=True,
                      tile_position=(0, 64 * h)).then_inc(smm.h, 1)
        sp.on("tensor", pvg)
    return smm.n


def emit_attn_l1b(sp, c, wqT, bq, wkT, bk, wvT, bv, woT, bo, t_all,
                  x_dram, xo_dram, ln_ready):
    smm, sev, sac = c.S("mm"), c.S("ev"), c.S("ac")
    c.ab.push()
    c.af.push()
    v_tm = c.ab.alloc(TT * 1024)
    a_fm = c.ab.alloc(FC * S)
    q_fm = c.ab.alloc(S)
    k_fm = c.ab.alloc(S)
    _p3 = c.af.alloc(S).bitcast(BF16)
    p_str2 = [[c.ab.alloc(S), c.ab.alloc(S), _p3[:, 0:S]],
              [c.ab.alloc(S), c.ab.alloc(S), _p3[:, S:2 * S]]]
    pt_sl = [[c.ab.alloc(512), c.ab.alloc(512)], [c.ab.alloc(512), c.ab.alloc(512)]]
    s1_str = [c.af.alloc(S), c.af.alloc(S)]

    bias_v = emit_bias_loads(sp, c, [(bq, 0, 1024), (bk, 1024, 1024),
                                     (bv, 2048, 1024), (bo, 3072, 1024)])

    c.ab.push()
    wv_b = c.ab.alloc(FC * 512)
    for e in range(2):
        wv_v = dma(sp, c, "gpsimd", "wd",
                   [(wv_b.rearrange("p (c m) -> p c m", c=FC),
                     wvT[:, e * 512:(e + 1) * 512]
                     .rearrange("(c p) m -> p c m", p=128))],
                   gates=[("mm", c._wv_war)] if c._wv_war else [])
        for kc in range(TT):
            pbk = kc % 4
            mm_v = emit_fm_group(
                sp, c, pbk,
                [c.ln_fm[:, cc * S + kc * 128: cc * S + (kc + 1) * 128]
                 for cc in range(FC)],
                [wv_b[:, cc * 512:(cc + 1) * 512] for cc in range(FC)],
                bias_lhs=c.ones_row[0:1, 0:128],
                bias_rhs=c.bias_row[0:1, 2048 + e * 512: 2048 + (e + 1) * 512],
                waits=[("wd", wv_v), ("ev", ln_ready), ("aux2", bias_v)])
            emit_evac_copy(sp, c, pbk, 512,
                           v_tm[:, kc * 1024 + e * 512: kc * 1024 + (e + 1) * 512],
                           mm_v)
        c._wv_war = c.S("mm").n
    v_done = sev.n
    v_mm_done = c.S("mm").n
    c.ab.pop()
    c.ab.push()
    t_sb = [c.ab.alloc(3968), c.ab.alloc(3968)]
    wqk_b2 = [c.ab.alloc(FC * 256)]
    wqk_b2 = [wqk_b2[0], wqk_b2[0]]

    war = {}
    pt_slot_war = {}
    t_war = 0
    seq = 0

    def _l1b_fin(pend):
        blk, par, strips = pend
        pr = finish_softmax(sp, c, strips)
        for h in (0, 1):
            war[(blk, h, "recip")] = strips[h]["recip_done"]
        return pr

    def _l1b_pv(pend, p_ready):
        blk, par, strips = pend
        p = blk // TT
        nb = blk % TT
        pv_v = emit_pv2(sp, c, [p_str2[0][par], p_str2[1][par]], p_ready,
                        list(range(TT)), v_tm, (2 * p, 2 * p + 1), (4, 5), 128,
                        pt_sl, pt_slot_war)
        for h in (0, 1):
            war[(blk, h, "tr")] = pv_v
        a_evac2(sp, c, (4, 5), 128,
                lambda h: a_fm[64 * h:64 * h + 64,
                               p * S + nb * 128:p * S + (nb + 1) * 128], pv_v)

    def _l1b_flush_pv(pend):
        _l1b_pv(pend, _l1b_fin(pend))

    for p in range(8):
        slot = p % 2
        wqk_b = wqk_b2[slot]
        prev_scores_done = c.S("mm").n
        wqk_v = dma(sp, c, "gpsimd", f"w{slot}", [
            (wqk_b[:, 0:FC * 128].rearrange("p (c m) -> p c m", c=FC),
             wqT[:, 128 * p:128 * (p + 1)].rearrange("(c p2) m -> p2 c m", p2=128)),
            (wqk_b[:, FC * 128:FC * 256].rearrange("p (c m) -> p c m", c=FC),
             wkT[:, 128 * p:128 * (p + 1)].rearrange("(c p2) m -> p2 c m", p2=128)),
        ], gates=[("mm", c._wqk_war.get(0, v_mm_done))])
        t_v = dma(sp, c, "gpsimd", "tld",
                  [(t_sb[0], t_all[2 * p][:, 127:4095]),
                   (t_sb[1], t_all[2 * p + 1][:, 127:4095])],
                  gates=[("ev", t_war)] if t_war else [])
        qk_first = True
        for which, dst, boff in ((0, q_fm, 0), (1, k_fm, 1024)):
            for n in range(4):
                pbk = seq % 4
                seq += 1
                mm_v = emit_fm_group(
                    sp, c, pbk,
                    [wqk_b[:, which * FC * 128 + cc * 128: which * FC * 128 + (cc + 1) * 128]
                     for cc in range(FC)],
                    [c.ln_fm[:, cc * S + n * 512: cc * S + (n + 1) * 512]
                     for cc in range(FC)],
                    bias_lhs=c.bias_row[0:1, boff + 128 * p: boff + 128 * p + 128],
                    bias_rhs=c.ones_row[0:1, 0:512],
                    waits=[(f"w{slot}", wqk_v), ("ev", ln_ready), ("aux2", bias_v)])
                if qk_first:
                    sev.n += 1
                    vv = sev.n

                    def evq(v, mm_v=mm_v, dst=dst, n=n, psd=prev_scores_done, pbk=pbk):
                        v.wait_ge(c.sem["mm"].h, mm_v)
                        wait(v, c, "mm", psd)
                        v.tensor_copy(dst[:, n * 512:(n + 1) * 512],
                                      c.ps[pbk][:, :]).then_inc(sev.h, 1)
                    sp.on("vector", evq)
                    c.ps_free[pbk] = ("ev", vv)
                    qk_first = False
                else:
                    emit_evac_copy(sp, c, pbk, 512, dst[:, n * 512:(n + 1) * 512], mm_v)
        qk_done = sev.n
        c._wqk_war[0] = c.S("mm").n

        if p == 0:
            pendq = [None, None]
        for nb in range(TT):
            blk = p * TT + nb
            par = blk % 3
            strips = []
            pend_pr = _l1b_fin(pendq[0]) if pendq[0] is not None else None
            seg_done = {0: 0, 1: 0}
            for kt in range(4):
                mmvs = {}
                for h in (0, 1):
                    pbk = 2 * h + (kt % 2)
                    smm.n += 1
                    mm_v = smm.n
                    mmvs[h] = (pbk, mm_v)
                    bf = c.ps_free[pbk]

                    def smat(te, h=h, nb=nb, kt=kt, pbk=pbk, bf=bf,
                             qd=qk_done, vd=v_done, tv=t_v):
                        te.wait_ge(sev.h, max(qd, vd))
                        te.wait_ge(c.sem["tld"].h, tv)
                        wait(te, c, bf[0], bf[1])
                        te.matmul(c.ps[pbk][:, 0:512],
                                  q_fm[64 * h:64 * h + 64, nb * 128:(nb + 1) * 128],
                                  k_fm[64 * h:64 * h + 64, kt * 512:(kt + 1) * 512],
                                  start=True, stop=True,
                                  tile_position=(64 * h, 0)).then_inc(smm.h, 1)
                    sp.on("tensor", smat)
                for h in (0, 1):
                    pbk, mm_v = mmvs[h]
                    toff = 2047 - 127 - nb * 128 + kt * 512
                    bias_ap = t_sb[h][:, toff:toff + 512]
                    sev.n += 1
                    ev_v = sev.n
                    c.ps_free[pbk] = ("ev", ev_v)
                    s1w = war.get((blk - 1, h, "exp"), 0)

                    def evs(v, pbk=pbk, mm_v=mm_v, kt=kt, bias_ap=bias_ap,
                            h=h, s1w=s1w):
                        v.wait_ge(c.sem["mm"].h, mm_v)
                        wait(v, c, "ac", s1w)
                        v.scalar_tensor_tensor(
                            s1_str[h][:, kt * 512:(kt + 1) * 512],
                            c.ps[pbk][:, 0:512], 1.0 / 8.0, bias_ap,
                            ALU.mult, ALU.add).then_inc(sev.h, 1)
                    sp.on("vector", evs)
                    seg_done[h] = ev_v
            for h in (0, 1):
                strip = dict(
                    W=S, s1=s1_str[h], p=p_str2[h][par], segments=[],
                    pre_done=seg_done[h],
                    p_war=war.get((blk - 3, h, "tr"), 0),
                    sum_war=war.get((blk - 3, h, "recip"), 0),
                )
                emit_softmax(sp, c, [strip], 1.0 / 8.0, par, stat_base=32 + 4 * h)
                strips.append(strip)
                war[(blk, h, "exp")] = strip["exp_done"]
            if pendq[0] is not None:
                _l1b_pv(pendq[0], pend_pr)
            pendq = [pendq[1], (blk, par, strips)]
        if p == 7:
            for pe_ in pendq:
                if pe_ is not None:
                    _l1b_flush_pv(pe_)
        t_war = sev.n
    a_done_ac = c.S("ac").n

    c.ab.pop()
    wout_b = c.ab.alloc(FC * 1024)
    wo_v = dma(sp, c, "gpsimd", "wp",
               [(wout_b.rearrange("p (c m) -> p c m", c=FC),
                 woT.rearrange("(c p) m -> p c m", p=128))],
               gates=[("mm", c.S("mm").n), ("ac", a_done_ac), ("ev", sev.n)])
    xor = emit_proj_residual(
        sp, c,
        lambda cc, t: a_fm[:, cc * S + t * 128: cc * S + (t + 1) * 128],
        lambda cc, e: wout_b[:, cc * 1024 + e * 512: cc * 1024 + (e + 1) * 512],
        3072, x_dram, xo_dram, wo_v,
        lambda t: ("ac", a_done_ac), ln_ready, bias_gate=("aux2", bias_v))
    c.ab.pop()
    c.af.pop()
    return xor



def emit_attn_l1a(sp, c, wq, bq, wk, wv, wproj, bproj, wr, br, bnd_rev, e_scr,
                  x_dram, xo_dram, ln_ready):
    smm, sev, sac = c.S("mm"), c.S("ev"), c.S("ac")
    c.ab.push()
    c.af.push()
    v_tm = c.ab.alloc(TT * 1024)
    a_fm = c.ab.alloc(FC * S)
    c.ab.push()
    q_fm = c.ab.alloc(S)
    k_fm = c.ab.alloc(S)
    _p3 = c.af.alloc(1280).bitcast(BF16)
    p_str2 = [[c.ab.alloc(1280), c.ab.alloc(1280), _p3[:, 0:1280]],
              [c.ab.alloc(1280), c.ab.alloc(1280), _p3[:, 1280:2560]]]
    pt_sl = [[c.ab.alloc(512), c.ab.alloc(512)], [c.ab.alloc(512), c.ab.alloc(512)]]
    r_fm_a = c.ab.alloc(S)
    r_fm_b = c.ab.alloc(S)
    r_head = [c.ab.alloc(S), c.ab.alloc(S)]
    bnd_sb = c.ab.alloc(1024)
    s1_str = [c.af.alloc(1152), c.af.alloc(1152)]
    _es = c.af.alloc(2304).bitcast(BF16)
    es_sb2 = [[_es[:, 0:1152], _es[:, 1152:2304]],
              [_es[:, 2304:3456], _es[:, 3456:4608]]]
    e_sb = [c.af.alloc(1024)]
    e_sb = [e_sb[0], e_sb[0]]
    neg_sb = c.af.alloc(256)

    bias_v = emit_bias_loads(sp, c, [(bq, 0, 1024), (br, 1024, 160),
                                     (bproj, 2048, 1024)])
    bnd_v = dma(sp, c, "gpsimd", "aux2", [(bnd_sb[0:10, :], bnd_rev)])
    bias_v = bnd_v

    sev.n += 1
    neg_v = sev.n

    def negset(v):
        v.memset(neg_sb, NEG / 2.0).then_inc(sev.h, 1)
    sp.on("vector", negset)
    marg = []
    for b in range(4):
        marg.append((e_scr[b][:, 0:128], neg_sb[:, 0:128]))
        marg.append((e_scr[b][:, 1152:1408], neg_sb[:, 0:256]))
        marg.append((e_scr[b][:, 1408:1664], neg_sb[:, 0:256]))
    marg_v = dma(sp, c, "sync", "aux", marg, gates=[("ev", neg_v)])

    c.ab.push()
    wv_b = c.ab.alloc(FC * 256)
    for e in range(4):
        wv_v = dma(sp, c, "gpsimd", "wd",
                   [(wv_b.rearrange("p (c m) -> p c m", c=FC),
                     wv[:, e * 256:(e + 1) * 256]
                     .rearrange("(c p) m -> p c m", p=128))],
                   gates=[("mm", c._wv_war)] if c._wv_war else [])
        for kc in range(TT):
            pbk = kc % 4
            mm_v = emit_fm_group(
                sp, c, pbk,
                [c.ln_fm[:, cc * S + kc * 128: cc * S + (kc + 1) * 128]
                 for cc in range(FC)],
                [wv_b[:, cc * 256:(cc + 1) * 256] for cc in range(FC)],
                waits=[("wd", wv_v), ("ev", ln_ready)])
            emit_evac_copy(sp, c, pbk, 256,
                           v_tm[:, kc * 1024 + e * 256: kc * 1024 + (e + 1) * 256],
                           mm_v)
        c._wv_war = c.S("mm").n
    wr_v = dma(sp, c, "gpsimd", "wd",
               [(wv_b[:, 0:FC * 160].rearrange("p (c m) -> p c m", c=FC),
                 wr.rearrange("(c p) m -> p c m", p=128))],
               gates=[("mm", c._wv_war)])
    for which, dst, mw, boff in ((0, r_fm_a, 128, 1024), (1, r_fm_b, 32, 1152)):
        for n in range(4):
            pbk = n % 4
            mm_v = emit_fm_group(
                sp, c, pbk,
                [wv_b[:, cc * 160 + which * 128: cc * 160 + which * 128 + mw]
                 for cc in range(FC)],
                [c.ln_fm[:, cc * S + n * 512: cc * S + (n + 1) * 512]
                 for cc in range(FC)],
                bias_lhs=c.bias_row[0:1, boff:boff + mw],
                bias_rhs=c.ones_row[0:1, 0:512],
                waits=[("wd", wr_v), ("ev", ln_ready), ("aux2", max(bias_v, bnd_v))])
            sev.n += 1
            vv = sev.n

            def evr(v, mm_v=mm_v, dst=dst, n=n, pbk=pbk, mw=mw):
                v.wait_ge(c.sem["mm"].h, mm_v)
                v.tensor_copy(dst[0:mw, n * 512:(n + 1) * 512],
                              c.ps[pbk][0:mw, :]).then_inc(sev.h, 1)
            sp.on("vector", evr)
            c.ps_free[pbk] = ("ev", vv)
    r_done = sev.n
    v_done = sev.n
    v_mm_done = c.S("mm").n
    c.ab.pop()
    wqk_b2 = [c.ab.alloc(FC * 256), c.ab.alloc(FC * 256)]

    war = {}
    pt_slot_war = {}
    e_scr_war = [0, 0, 0, 0]
    seq = 0
    eb = 0

    def _l1a_fin(pend):
        blk, par, strips, kc_list = pend
        pr = finish_softmax(sp, c, strips)
        for h in (0, 1):
            war[(blk, h, "recip")] = strips[h]["recip_done"]
        return pr

    def _l1a_pv(pend, p_ready):
        blk, par, strips, kc_list = pend
        pp = blk // TT
        nb = blk % TT
        pv_v = emit_pv2(sp, c, [p_str2[0][par], p_str2[1][par]], p_ready,
                        kc_list, v_tm, (2 * pp, 2 * pp + 1), (4, 5), 128,
                        pt_sl, pt_slot_war)
        for h in (0, 1):
            war[(blk, h, "tr")] = pv_v
        a_evac2(sp, c, (4, 5), 128,
                lambda h, pp=pp, nb=nb: a_fm[64 * h:64 * h + 64,
                                             pp * S + nb * 128:pp * S + (nb + 1) * 128],
                pv_v)

    def _l1a_flush(pend):
        _l1a_pv(pend, _l1a_fin(pend))

    pendq = [None, None]
    for p in range(8):
        slot = p % 2
        wqk_b = wqk_b2[slot]
        prev_scores_done = c.S("mm").n
        wqk_v = dma(sp, c, "gpsimd", f"w{slot}", [
            (wqk_b[:, 0:FC * 128].rearrange("p (c m) -> p c m", c=FC),
             wq[:, 128 * p:128 * (p + 1)].rearrange("(c p2) m -> p2 c m", p2=128)),
            (wqk_b[:, FC * 128:FC * 256].rearrange("p (c m) -> p c m", c=FC),
             wk[:, 128 * p:128 * (p + 1)].rearrange("(c p2) m -> p2 c m", p2=128)),
        ], gates=[("mm", c._wqk_war.get(slot, v_mm_done))])
        rrep_pairs = []
        for h in (0, 1):
            hh = 2 * p + h
            lo = 10 * hh
            if lo + 10 <= 128:
                rrep_pairs.append((r_head[h][0:10, :], r_fm_a[lo:lo + 10, :]))
            elif lo < 128:
                k1 = 128 - lo
                rrep_pairs.append((r_head[h][0:k1, :], r_fm_a[lo:128, :]))
                rrep_pairs.append((r_head[h][k1:10, :], r_fm_b[0:10 - k1, :]))
            else:
                rrep_pairs.append((r_head[h][0:10, :], r_fm_b[lo - 128:lo - 118, :]))
        rr_v = dma(sp, c, "sync", "rrep", rrep_pairs,
                   gates=[("ev", r_done), ("mm", war.get(("rrep", p - 2), 0))])
        war[("rrep", p)] = 0
        qk_first = True
        for which, dst, boff in ((0, q_fm, 0), (1, k_fm, None)):
            for n in range(4):
                pbk = seq % 4
                seq += 1
                mm_v = emit_fm_group(
                    sp, c, pbk,
                    [wqk_b[:, which * FC * 128 + cc * 128: which * FC * 128 + (cc + 1) * 128]
                     for cc in range(FC)],
                    [c.ln_fm[:, cc * S + n * 512: cc * S + (n + 1) * 512]
                     for cc in range(FC)],
                    bias_lhs=(c.bias_row[0:1, boff + 128 * p: boff + 128 * p + 128]
                              if boff is not None else None),
                    bias_rhs=(c.ones_row[0:1, 0:512] if boff is not None else None),
                    waits=[(f"w{slot}", wqk_v), ("ev", ln_ready), ("aux2", bias_v)])
                if qk_first:
                    sev.n += 1
                    vv = sev.n

                    def evq(v, mm_v=mm_v, dst=dst, n=n, psd=prev_scores_done, pbk=pbk):
                        v.wait_ge(c.sem["mm"].h, mm_v)
                        wait(v, c, "mm", psd)
                        v.tensor_copy(dst[:, n * 512:(n + 1) * 512],
                                      c.ps[pbk][:, :]).then_inc(sev.h, 1)
                    sp.on("vector", evq)
                    c.ps_free[pbk] = ("ev", vv)
                    qk_first = False
                else:
                    emit_evac_copy(sp, c, pbk, 512, dst[:, n * 512:(n + 1) * 512], mm_v)
        qk_done = sev.n
        c._wqk_war[slot] = c.S("mm").n
        ew_prev = {0: 0, 1: 0}
        sk_prev = {}

        def emit_E(h, nb):
            nonlocal eb
            t0 = nb * 128
            p_start = max(0, t0 - 1024)
            blk = p * TT + nb
            ebuf = eb % 4
            eb += 1
            sk_war = e_scr_war[ebuf]
            for e2 in range(2):
                pbk = c._l1a_seq % 4
                c._l1a_seq += 1
                mm_v = emit_fm_group(
                    sp, c, pbk,
                    [r_head[h][0:10, t0:t0 + 128]],
                    [bnd_sb[0:10, e2 * 512:(e2 + 1) * 512]],
                    waits=[("rrep", rr_v), ("aux2", bnd_v)])
                sev.n += 1
                vv = sev.n
                ewp = ew_prev[h]

                def eve(v, mm_v=mm_v, e2=e2, pbk=pbk, h=h, ewp=ewp):
                    v.wait_ge(c.sem["mm"].h, mm_v)
                    wait(v, c, "ewr", ewp)
                    v.tensor_copy(e_sb[h][:, e2 * 512:(e2 + 1) * 512],
                                  c.ps[pbk][:, :]).then_inc(sev.h, 1)
                sp.on("vector", eve)
                c.ps_free[pbk] = ("ev", vv)
            e_ev = sev.n
            ew_v = dma(sp, c, "sync", "ewr",
                       [(e_scr[ebuf][:, 128:1152], e_sb[h])],
                       gates=[("ev", e_ev), ("sk", sk_war)])
            ew_prev[0] = ew_prev[1] = ew_v
            off0 = 128 + 1023 - t0 + p_start
            Wt = t0 + 128 - p_start
            skew_ap = AP(e_scr[ebuf].tensor,
                         e_scr[ebuf].offset + off0, [[1663, 128], [1, Wt]])
            es_war = war.get((blk - 2, h, "segev"), 0)
            sk_chain = c.S("sk").n
            sk_v = dma(sp, c, "gpsimd", "sk",
                       [(es_sb2[h][nb % 2][:, 0:Wt], skew_ap)],
                       gates=[("ewr", ew_v), ("aux", marg_v), ("ev", es_war),
                              ("sk", sk_chain)])
            e_scr_war[ebuf] = sk_v
            sk_prev[(h, nb)] = sk_v

        c._l1a_seq = seq
        for h in (0, 1):
            emit_E(h, 0)

        for nb in range(TT):
            blk = p * TT + nb
            t0 = nb * 128
            p_start = max(0, t0 - 1024)
            p_end = t0 + 128
            Wt = p_end - p_start
            par = blk % 3
            strips = []
            pend_pr = _l1a_fin(pendq[0]) if pendq[0] is not None else None
            for h in (0, 1):
                if nb < TT - 1:
                    emit_E(h, nb + 1)
                sk_v = sk_prev[(h, nb)]
                segs = []
                nseg = cdiv(Wt, 512)
                for si in range(nseg):
                    soff = si * 512
                    wseg = min(512, Wt - soff)
                    pbk = c._l1a_seq % 4
                    c._l1a_seq += 1
                    smm.n += 1
                    mm_v = smm.n
                    bf = c.ps_free[pbk]

                    def smat(te, h=h, nb=nb, soff=soff, wseg=wseg, pbk=pbk, bf=bf,
                             qd=qk_done, vd=v_done, ps0=p_start):
                        te.wait_ge(sev.h, max(qd, vd))
                        wait(te, c, bf[0], bf[1])
                        te.matmul(c.ps[pbk][:, 0:wseg],
                                  q_fm[64 * h:64 * h + 64, nb * 128:(nb + 1) * 128],
                                  k_fm[64 * h:64 * h + 64,
                                       ps0 + soff: ps0 + soff + wseg],
                                  start=True, stop=True,
                                  tile_position=(64 * h, 0)).then_inc(smm.h, 1)
                    sp.on("tensor", smat)
                    segs.append((pbk, soff, wseg, mm_v,
                                 es_sb2[h][nb % 2][:, soff:soff + wseg]))
                strip = dict(
                    W=Wt, s1=s1_str[h], p=p_str2[h][par], segments=segs,
                    s1_war=war.get((blk - 1, h, "exp"), 0),
                    p_war=war.get((blk - 3, h, "tr"), 0),
                    sum_war=war.get((blk - 3, h, "recip"), 0),
                    seg_gate=("sk", sk_v),
                )
                emit_softmax(sp, c, [strip], 1.0 / 8.0, par,
                             stat_base=32 + 4 * h)
                strips.append(strip)
                war[(blk, h, "segev")] = strip["seg_evac_done"]
            for h in (0, 1):
                war[(blk, h, "exp")] = strips[h]["exp_done"]
            if pendq[0] is not None:
                _l1a_pv(pendq[0], pend_pr)
            pendq = [pendq[1],
                     (blk, par, strips, list(range(p_start // 128, p_end // 128)))]
        war[("rrep", p)] = c.S("mm").n
    for pe_ in pendq:
        if pe_ is not None:
            _l1a_flush(pe_)
    a_done_ac = c.S("ac").n

    c.ab.pop()
    wout_b = c.ab.alloc(FC * 1024)
    wo_v = dma(sp, c, "gpsimd", "wp",
               [(wout_b.rearrange("p (c m) -> p c m", c=FC),
                 wproj.rearrange("(c p) m -> p c m", p=128))],
               gates=[("mm", c.S("mm").n), ("ac", a_done_ac), ("ev", sev.n)])
    xor = emit_proj_residual(
        sp, c,
        lambda cc, t: a_fm[:, cc * S + t * 128: cc * S + (t + 1) * 128],
        lambda cc, e: wout_b[:, cc * 1024 + e * 512: cc * 1024 + (e + 1) * 512],
        2048, x_dram, xo_dram, wo_v,
        lambda t: ("ac", a_done_ac), ln_ready, bias_gate=("aux2", bias_v))
    c.ab.pop()
    c.af.pop()
    return xor



def emit_ffn_l1(sp, c, w1T, w2T, h1T, h2T, w3T, s1d, s2d, h1sd, h2sd, w3sd,
                ident_dram, x_dram, xo_dram, ln_ready, prep_gate=None):
    smm, sev, sac, vch = c.S("mm"), c.S("ev"), c.S("ac"), c.S("vch")
    HB = 32
    c.ab.push()
    c.af.push()
    su_str = c.ab.alloc(HB * 512)
    h_str = c.ab.alloc(HB * 512)
    wb1 = [c.ab.alloc(FC * 128) for _ in range(2)]
    wb2 = [c.ab.alloc(FC * 128) for _ in range(2)]
    hb = [c.ab.alloc(HB * 128) for _ in range(2)]
    w3b = c.ab.alloc(HB * 256)
    ident32 = c.af.alloc(128)
    s_cols = c.af.alloc(160)
    sc_rows = c.af.alloc(128)
    s3_bc = c.af.alloc(1024)
    s3row = c.af.alloc(1024)
    onef = c.af.alloc(128)
    y_blk = c.af.alloc(4096)

    id32_v = dma(sp, c, "gpsimd", "aux2", [(ident32, ident_dram)])
    last_tr = 0
    sc_all = 0
    for i, vec in enumerate((s1d, s2d, h1sd, h2sd)):
        ld_v = dma(sp, c, "sync", "aux",
                   [(sc_rows[0:32, :], vec.rearrange("(m p) -> m p", p=128))],
                   gates=[("mm", last_tr)] if last_tr else [])
        bf = c.ps_free[5]
        smm.n += 1
        tr_v = smm.n
        last_tr = tr_v

        def trsc(te, ld_v=ld_v, id32_v=id32_v, bf=bf):
            te.wait_ge(c.sem["aux"].h, ld_v)
            te.wait_ge(c.sem["aux2"].h, id32_v)
            wait(te, c, bf[0], bf[1])
            te.transpose(c.ps[5][:, 0:32], sc_rows[0:32, :],
                         ident32[0:32, 0:32]).then_inc(smm.h, 1)
        sp.on("tensor", trsc)
        sev.n += 1
        ev_v = sev.n

        def evsc(v, tr_v=tr_v, i=i):
            v.wait_ge(smm.h, tr_v)
            v.tensor_copy(s_cols[:, i * 32:(i + 1) * 32],
                          c.ps[5][:, 0:32]).then_inc(sev.h, 1)
        sp.on("vector", evsc)
        c.ps_free[5] = ("ev", ev_v)
        sc_all = ev_v
    s3_v = dma(sp, c, "sync", "aux", [(s3row[0:1, :], w3sd[None, :])],
               gates=[("mm", last_tr)])
    sev.n += 1
    onef_v = sev.n

    def onesf(v):
        v.memset(onef[0:1, 0:128], 1.0).then_inc(sev.h, 1)
    sp.on("vector", onesf)
    s3bc_done = 0
    for e in range(2):
        bf = c.ps_free[4]
        smm.n += 1
        mm_v = smm.n

        def s3mm(te, e=e, s3_v=s3_v, onef_v=onef_v, bf=bf):
            te.wait_ge(c.sem["aux"].h, s3_v)
            te.wait_ge(sev.h, onef_v)
            wait(te, c, bf[0], bf[1])
            te.matmul(c.ps[4][0:128, 0:512], onef[0:1, 0:128],
                      s3row[0:1, e * 512:(e + 1) * 512],
                      start=True, stop=True).then_inc(smm.h, 1)
        sp.on("tensor", s3mm)
        sev.n += 1
        vv = sev.n

        def s3ev(v, mm_v=mm_v, e=e):
            v.wait_ge(smm.h, mm_v)
            v.tensor_copy(s3_bc[:, e * 512:(e + 1) * 512],
                          c.ps[4][:, 0:512]).then_inc(sev.h, 1)
        sp.on("vector", s3ev)
        c.ps_free[4] = ("ev", vv)
        s3bc_done = vv
        prep_done = max(sc_all, s3bc_done)

    war = {}
    mm_slot = {"w1": [0, 0], "h": [0, 0]}
    x_out_ready = [None] * TT
    for n in range(4):
        tok = slice(n * 512, (n + 1) * 512)
        for m in range(HB):
            dbl = m % 2
            g0 = [("mm", mm_slot["w1"][dbl])] if mm_slot["w1"][dbl] else []
            if prep_gate and n == 0 and m == dbl:
                g0 = g0 + [prep_gate]
            lw = dma(sp, c, "sync", f"sw{dbl}", [
                (wb1[dbl].rearrange("p (c m) -> p c m", c=FC),
                 w1T[:, m * 128:(m + 1) * 128].rearrange("(c p) m -> p c m", p=128)),
                (wb2[dbl].rearrange("p (c m) -> p c m", c=FC),
                 w2T[:, m * 128:(m + 1) * 128].rearrange("(c p) m -> p c m", p=128)),
            ], gates=g0)
            pg, pu = 2 * dbl, 2 * dbl + 1
            gf, uf = c.ps_free[pg], c.ps_free[pu]
            smm.n += 2
            mm_v = smm.n
            mm_slot["w1"][dbl] = mm_v

            def mmu(te, dbl=dbl, lw=lw, tok=tok, pg=pg, pu=pu, gf=gf, uf=uf,
                    lr=ln_ready, pd=prep_done):
                te.wait_ge(c.sem[f"sw{dbl}"].h, lw)
                te.wait_ge(sev.h, max(lr, pd))
                wait(te, c, gf[0], gf[1])
                wait(te, c, uf[0], uf[1])
                for cc in range(FC):
                    ins = te.matmul(c.ps[pg][:, :], wb1[dbl][:, cc * 128:(cc + 1) * 128],
                                    c.ln_fm[:, cc * S:(cc + 1) * S][:, tok],
                                    start=(cc == 0), stop=(cc == FC - 1))
                    if cc == FC - 1:
                        ins.then_inc(smm.h, 1)
                for cc in range(FC):
                    ins = te.matmul(c.ps[pu][:, :], wb2[dbl][:, cc * 128:(cc + 1) * 128],
                                    c.ln_fm[:, cc * S:(cc + 1) * S][:, tok],
                                    start=(cc == 0), stop=(cc == FC - 1))
                    if cc == FC - 1:
                        ins.then_inc(smm.h, 1)
            sp.on("tensor", mmu)

            sac.n += 1
            sa = sac.n
            su_war = war.get(("su", n - 1), 0) if m < 2 else 0

            def silu(a, pg=pg, mm_v=mm_v, m=m, sw=su_war):
                a.wait_ge(smm.h, mm_v - 1)
                wait(a, c, "ev", sw)
                a.activation(su_str[:, m * 512:(m + 1) * 512], c.ps[pg][:, :],
                             AF.Silu, scale=s_cols[:, m:m + 1]).then_inc(sac.h, 1)
            sp.on("scalar", silu)
            c.ps_free[pg] = ("ac", sa)

            sev.n += 1
            hv = sev.n
            h_war = war.get(("h", n - 1), 0) if m < 2 else 0

            def hmul(v, m=m, pu=pu, sa=sa, mm_v=mm_v, hw=h_war):
                v.wait_ge(sac.h, sa)
                v.wait_ge(smm.h, mm_v)
                wait(v, c, "mm", hw)
                v.scalar_tensor_tensor(h_str[:, m * 512:(m + 1) * 512],
                                       c.ps[pu][:, :], s_cols[:, 32 + m:33 + m],
                                       su_str[:, m * 512:(m + 1) * 512],
                                       ALU.mult, ALU.mult).then_inc(sev.h, 1)
            sp.on("vector", hmul)
            c.ps_free[pu] = ("ev", hv)
        h_done = sev.n
        war[("su", n)] = sev.n

        def hidden_layer(src, dst, wT, scol_off, h_ready, wslot_key, wait_extra):
            g_done_local = 0
            for m2 in range(HB):
                dbl = m2 % 2
                lw = dma(sp, c, "sync", f"sh{dbl}", [
                    (hb[dbl].rearrange("p (c m) -> p c m", c=HB),
                     wT[:, m2 * 128:(m2 + 1) * 128]
                     .rearrange("(c p) m -> p c m", p=128)),
                ], gates=[("mm", mm_slot["h"][dbl])] if mm_slot["h"][dbl] else [])
                pbk = 2 * dbl
                bf = c.ps_free[pbk]
                smm.n += 1
                mm_v = smm.n
                mm_slot["h"][dbl] = mm_v

                def mmh(te, dbl=dbl, lw=lw, pbk=pbk, bf=bf, hr=h_ready, src=src,
                        we=wait_extra):
                    te.wait_ge(c.sem[f"sh{dbl}"].h, lw)
                    te.wait_ge(sev.h, hr)
                    wait(te, c, "ac", we)
                    wait(te, c, bf[0], bf[1])
                    for k in range(HB):
                        ins = te.matmul(c.ps[pbk][:, :],
                                        hb[dbl][:, k * 128:(k + 1) * 128],
                                        src[:, k * 512:(k + 1) * 512],
                                        start=(k == 0), stop=(k == HB - 1))
                    ins.then_inc(smm.h, 1)
                sp.on("tensor", mmh)

                sac.n += 1
                sa = sac.n

                def gelu(a, pbk=pbk, mm_v=mm_v, m2=m2, dst=dst, so=scol_off):
                    a.wait_ge(smm.h, mm_v)
                    a.activation(dst[:, m2 * 512:(m2 + 1) * 512], c.ps[pbk][:, :],
                                 AF.Gelu, scale=s_cols[:, so + m2:so + m2 + 1])\
                     .then_inc(sac.h, 1)
                sp.on("scalar", gelu)
                c.ps_free[pbk] = ("ac", sa)
                g_done_local = sa
            return g_done_local

        g1_done = hidden_layer(h_str, su_str, h1T, 64, h_done, "h", 0)
        sev.n += 0
        g2_done = hidden_layer(su_str, h_str, h2T, 96, h_done, "h", g1_done)
        war[("h", n)] = smm.n

        yev_done = 0
        for e4 in range(4):
            w3_v = dma(sp, c, "sync", "sw3", [
                (w3b.rearrange("p (c m) -> p c m", c=HB),
                 w3T[:, e4 * 256:(e4 + 1) * 256].rearrange("(c p) m -> p c m", p=128)),
            ], gates=[("mm", war.get(("w3", n, e4 - 1), 0))] if e4 else
                     ([("mm", war.get(("w3", n - 1, 3), 0))] if n else []))
            for tt_i in range(4):
                pbk = 4 + tt_i % 2
                bf = c.ps_free[pbk]
                smm.n += 1
                mm_v = smm.n

                def mmy(te, tt_i=tt_i, pbk=pbk, bf=bf, w3_v=w3_v, g2d=g2_done):
                    te.wait_ge(c.sem["sw3"].h, w3_v)
                    te.wait_ge(sac.h, g2d)
                    wait(te, c, bf[0], bf[1])
                    for k in range(HB):
                        ins = te.matmul(c.ps[pbk][:, 0:256],
                                        h_str[:, k * 512 + tt_i * 128: k * 512 + (tt_i + 1) * 128],
                                        w3b[:, k * 256:(k + 1) * 256],
                                        start=(k == 0), stop=(k == HB - 1))
                    ins.then_inc(smm.h, 1)
                sp.on("tensor", mmy)
                war[("w3", n, e4)] = smm.n

                sev.n += 1
                yv = sev.n
                ybw = war.get(("yblk", n - 1), 0)

                def yev(v, tt_i=tt_i, e4=e4, pbk=pbk, mm_v=mm_v, ybw=ybw):
                    v.wait_ge(smm.h, mm_v)
                    wait(v, c, "ev", ybw)
                    v.scalar_tensor_tensor(
                        y_blk[:, tt_i * 1024 + e4 * 256: tt_i * 1024 + (e4 + 1) * 256],
                        c.ps[pbk][:, 0:256], 1.0,
                        s3_bc[:, e4 * 256:(e4 + 1) * 256],
                        ALU.mult, ALU.mult).then_inc(sev.h, 1)
                sp.on("vector", yev)
                c.ps_free[pbk] = ("ev", yv)
                yev_done = yv

        for tt_i in range(4):
            t = n * 4 + tt_i
            buf = t % 2
            xsl = c.xbuf[:, buf * D:(buf + 1) * D]
            xnsl = c.xnew[:, buf * D:(buf + 1) * D]
            gates = [("ev", ln_ready)]
            if t >= 2:
                gates.append(x_out_ready[t - 2])
            xv = dma(sp, c, "sync", f"xl{buf}",
                     [(xsl, x_dram[t * 128:(t + 1) * 128, :])], gates=gates)
            sev.n += 1
            rv = sev.n

            def resid(v, tt_i=tt_i, xv=xv, xsl=xsl, xnsl=xnsl, buf=buf, yd=yev_done):
                v.wait_ge(c.sem[f"xl{buf}"].h, xv)
                v.wait_ge(sev.h, yd)
                v.tensor_add(xnsl, xsl,
                             y_blk[:, tt_i * 1024:(tt_i + 1) * 1024]).then_inc(sev.h, 1)
            sp.on("vector", resid)
            x_out_ready[t] = emit_store_tile(sp, c, xo_dram, t, xnsl, rv)
        war[("yblk", n)] = sev.n
    c.ab.pop()
    c.af.pop()
    return x_out_ready


def finalize(sp, c, x_out_ready):
    tot = {}
    for g in x_out_ready:
        if g is not None:
            tot[g[0]] = max(tot.get(g[0], 0), g[1])

    def fin(g):
        for k, v in tot.items():
            g.wait_ge(c.sem[k].h, v)
    sp.on("gpsimd", fin)


def build_block(nc, sp):
    with nc.Block() as block:
        @block.sync
        def _(eng):
            for fn in sp.streams["sync"]:
                fn(eng)

        @block.tensor
        def _(eng):
            for fn in sp.streams["tensor"]:
                fn(eng)

        @block.vector
        def _(eng):
            for fn in sp.streams["vector"]:
                fn(eng)

        @block.scalar
        def _(eng):
            for fn in sp.streams["scalar"]:
                fn(eng)

        @block.gpsimd
        def _(eng):
            for fn in sp.streams["gpsimd"]:
                fn(eng)


def build_ffn_l0_program():
    nc = bass.Bass("TRN2", target_bir_lowering=False)
    x_in = nc.dram_tensor("x_in", [S, D], F32, kind="ExternalInput").ap()
    x_out = nc.dram_tensor("x_out", [S, D], F32, kind="ExternalOutput").ap()
    wgT = nc.dram_tensor("wgT", [D, 3072], I32, kind="ExternalInput").ap()
    wuT = nc.dram_tensor("wuT", [D, 3072], I32, kind="ExternalInput").ap()
    wdT = nc.dram_tensor("wdT", [3072, D], I32, kind="ExternalInput").ap()
    ident = nc.dram_tensor("ident", [128, 128], BF16, kind="ExternalInput").ap()
    exits = contextlib.ExitStack()
    with exits:
        c = make_ctx(nc, exits)
        sp = SP()
        emit_const_init(sp, c, ident)
        ln_ready = emit_ln(sp, c, x_in)
        xor = emit_ffn_l0(sp, c, wgT, wuT, wdT, x_in, x_out, ln_ready)
        finalize(sp, c, xor)
        build_block(nc, sp)
    return nc


def build_attn_l0a_program():
    nc = bass.Bass("TRN2", target_bir_lowering=False)
    x_in = nc.dram_tensor("x_in", [S, D], F32, kind="ExternalInput").ap()
    x_out = nc.dram_tensor("x_out", [S, D], F32, kind="ExternalOutput").ap()
    wqkv = nc.dram_tensor("wqkv", [D, 3072], F32, kind="ExternalInput").ap()
    bqkv = nc.dram_tensor("bqkv", [3072], F32, kind="ExternalInput").ap()
    wout = nc.dram_tensor("wout", [D, D], F32, kind="ExternalInput").ap()
    bout = nc.dram_tensor("bout", [D], F32, kind="ExternalInput").ap()
    ident = nc.dram_tensor("ident", [128, 128], BF16, kind="ExternalInput").ap()
    mask = nc.dram_tensor("mask_l0a", [128, 384], F32, kind="ExternalInput").ap()
    exits = contextlib.ExitStack()
    with exits:
        c = make_ctx(nc, exits)
        c.mask_l0a_dram = mask
        c._wv_war = 0
        c._wqk_war = {}
        sp = SP()
        emit_const_init(sp, c, ident)
        ln_ready = emit_ln(sp, c, x_in)
        xor = emit_attn_l0a(sp, c, wqkv, bqkv, wout, bout, x_in, x_out, ln_ready)
        finalize(sp, c, xor)
        build_block(nc, sp)
    return nc


def build_attn_l0b_program():
    nc = bass.Bass("TRN2", target_bir_lowering=False)
    x_in = nc.dram_tensor("x_in", [S, D], F32, kind="ExternalInput").ap()
    x_out = nc.dram_tensor("x_out", [S, D], F32, kind="ExternalOutput").ap()
    wqkvT = nc.dram_tensor("wqkvT", [D, 3072], F32, kind="ExternalInput").ap()
    woutT = nc.dram_tensor("woutT", [D, D], F32, kind="ExternalInput").ap()
    b_out = nc.dram_tensor("b_out", [D], F32, kind="ExternalInput").ap()
    ident = nc.dram_tensor("ident", [128, 128], BF16, kind="ExternalInput").ap()
    mask = nc.dram_tensor("mask_l0b", [128, 256], F32, kind="ExternalInput").ap()
    exits = contextlib.ExitStack()
    with exits:
        c = make_ctx(nc, exits)
        c.mask_l0b_dram = mask
        c._wv_war = 0
        c._wqk_war = {}
        sp = SP()
        emit_const_init(sp, c, ident)
        ln_ready = emit_ln(sp, c, x_in)
        xor = emit_attn_l0b(sp, c, wqkvT, woutT, b_out, x_in, x_out, ln_ready)
        finalize(sp, c, xor)
        build_block(nc, sp)
    return nc


def build_ffn_l1_program():
    nc = bass.Bass("TRN2", target_bir_lowering=False)
    x_in = nc.dram_tensor("x_in", [S, D], F32, kind="ExternalInput").ap()
    x_out = nc.dram_tensor("x_out", [S, D], F32, kind="ExternalOutput").ap()
    w1T = nc.dram_tensor("w1T", [D, 4096], F32, kind="ExternalInput").ap()
    w2T = nc.dram_tensor("w2T", [D, 4096], F32, kind="ExternalInput").ap()
    h1T = nc.dram_tensor("h1T", [4096, 4096], F32, kind="ExternalInput").ap()
    h2T = nc.dram_tensor("h2T", [4096, 4096], F32, kind="ExternalInput").ap()
    w3T = nc.dram_tensor("w3T", [4096, D], F32, kind="ExternalInput").ap()
    s1d = nc.dram_tensor("s1", [4096], F32, kind="ExternalInput").ap()
    s2d = nc.dram_tensor("s2", [4096], F32, kind="ExternalInput").ap()
    h1sd = nc.dram_tensor("h1s", [4096], F32, kind="ExternalInput").ap()
    h2sd = nc.dram_tensor("h2s", [4096], F32, kind="ExternalInput").ap()
    w3sd = nc.dram_tensor("w3s", [D], F32, kind="ExternalInput").ap()
    ident = nc.dram_tensor("ident", [128, 128], BF16, kind="ExternalInput").ap()
    exits = contextlib.ExitStack()
    with exits:
        c = make_ctx(nc, exits)
        sp = SP()
        emit_const_init(sp, c, ident)
        ln_ready = emit_ln(sp, c, x_in)
        xor = emit_ffn_l1(sp, c, w1T, w2T, h1T, h2T, w3T, s1d, s2d, h1sd, h2sd,
                          w3sd, ident, x_in, x_out, ln_ready)
        finalize(sp, c, xor)
        build_block(nc, sp)
    return nc


def build_attn_l1a_program():
    nc = bass.Bass("TRN2", target_bir_lowering=False)
    x_in = nc.dram_tensor("x_in", [S, D], F32, kind="ExternalInput").ap()
    x_out = nc.dram_tensor("x_out", [S, D], F32, kind="ExternalOutput").ap()
    wq = nc.dram_tensor("wq", [D, D], F32, kind="ExternalInput").ap()
    wk = nc.dram_tensor("wk", [D, D], F32, kind="ExternalInput").ap()
    wv = nc.dram_tensor("wv", [D, D], F32, kind="ExternalInput").ap()
    wproj = nc.dram_tensor("wproj", [D, D], F32, kind="ExternalInput").ap()
    bq = nc.dram_tensor("bq", [D], F32, kind="ExternalInput").ap()
    bproj = nc.dram_tensor("bproj", [D], F32, kind="ExternalInput").ap()
    wr = nc.dram_tensor("wr", [D, 160], F32, kind="ExternalInput").ap()
    br = nc.dram_tensor("br", [160], F32, kind="ExternalInput").ap()
    bnd_rev = nc.dram_tensor("bnd_rev", [10, 1024], F32, kind="ExternalInput").ap()
    e_scr = nc.dram_tensor("e_scr", [4, 128, 1664], F32, kind="Internal").ap()
    ident = nc.dram_tensor("ident", [128, 128], BF16, kind="ExternalInput").ap()
    exits = contextlib.ExitStack()
    with exits:
        c = make_ctx(nc, exits)
        c._wv_war = 0
        c._wqk_war = {}
        sp = SP()
        emit_const_init(sp, c, ident)
        ln_ready = emit_ln(sp, c, x_in)
        xor = emit_attn_l1a(sp, c, wq, bq, wk, wv, wproj, bproj, wr, br, bnd_rev,
                            e_scr, x_in, x_out, ln_ready)
        finalize(sp, c, xor)
        build_block(nc, sp)
    return nc


def build_attn_l1b_program():
    nc = bass.Bass("TRN2", target_bir_lowering=False)
    x_in = nc.dram_tensor("x_in", [S, D], F32, kind="ExternalInput").ap()
    x_out = nc.dram_tensor("x_out", [S, D], F32, kind="ExternalOutput").ap()
    wqT = nc.dram_tensor("wqT", [D, D], F32, kind="ExternalInput").ap()
    wkT = nc.dram_tensor("wkT", [D, D], F32, kind="ExternalInput").ap()
    wvT = nc.dram_tensor("wvT", [D, D], F32, kind="ExternalInput").ap()
    woT = nc.dram_tensor("woT", [D, D], F32, kind="ExternalInput").ap()
    bq = nc.dram_tensor("bq", [D], F32, kind="ExternalInput").ap()
    bk = nc.dram_tensor("bk", [D], F32, kind="ExternalInput").ap()
    bv = nc.dram_tensor("bv", [D], F32, kind="ExternalInput").ap()
    bo = nc.dram_tensor("bo", [D], F32, kind="ExternalInput").ap()
    t_all = nc.dram_tensor("t_all", [16, 128, 4224], F32, kind="ExternalInput").ap()
    ident = nc.dram_tensor("ident", [128, 128], BF16, kind="ExternalInput").ap()
    exits = contextlib.ExitStack()
    with exits:
        c = make_ctx(nc, exits)
        c._wv_war = 0
        c._wqk_war = {}
        sp = SP()
        emit_const_init(sp, c, ident)
        ln_ready = emit_ln(sp, c, x_in)
        xor = emit_attn_l1b(sp, c, wqT, bq, wkT, bk, wvT, bv, woT, bo, t_all,
                            x_in, x_out, ln_ready)
        finalize(sp, c, xor)
        build_block(nc, sp)
    return nc


def t_all_np(pos_bias):
    H = pos_bias.shape[0]
    r = np.arange(128)[:, None]
    cc = np.arange(4224)[None, :]
    idx = 4094 - cc + r
    sel = np.clip(idx, 0, 4094)
    t = pos_bias[:, sel]
    t[:, (idx < 0) | (idx > 4094)] = 0.0
    return np.ascontiguousarray(t.astype(np.float32))


def mask_l0b_np():
    m = np.zeros((128, 256), np.float32)
    i = np.arange(128)[:, None]
    j = np.arange(256)[None, :]
    d = i - (j - 128)
    valid = (d >= 0) & (d <= 128)
    m[~valid] = NEG
    return m


def mask_l0a_np():
    m = np.zeros((128, 384), np.float32)
    i = np.arange(128)[:, None]
    j = np.arange(384)[None, :]
    kq = (j - 128) - i
    m[np.abs(kq) > 128] = NEG
    return m


def _ident_bf16():
    return np.eye(128, dtype=ml_dtypes.bfloat16)


def build_fused_program():
    nc = bass.Bass("TRN2", target_bir_lowering=False)
    I = lambda n, sh, dt=F32: nc.dram_tensor(n, sh, dt, kind="ExternalInput").ap()
    x_in = I("x_in", [S, D])
    x_out = nc.dram_tensor("x_out", [S, D], F32, kind="ExternalOutput").ap()
    xa = nc.dram_tensor("xa", [S, D], F32, kind="Internal").ap()
    xb = nc.dram_tensor("xb", [S, D], F32, kind="Internal").ap()
    e_scr = nc.dram_tensor("e_scr", [4, 128, 1664], F32, kind="Internal").ap()
    ident = I("ident", [128, 128], BF16)
    l0a = dict(wqkv=I("l0a_wqkv", [D, 3072]), bqkv=I("l0a_bqkv", [3072]),
               wout=I("l0a_wout", [D, D]), bout=I("l0a_bout", [D]),
               mask=I("mask_l0a", [128, 384]))
    f0 = dict(wgT=I("f0_wgT", [D, 3072], I32), wuT=I("f0_wuT", [D, 3072], I32),
              wdT=I("f0_wdT", [3072, D], I32))
    l0b = dict(wqkvT=I("l0b_wqkvT", [D, 3072]), woutT=I("l0b_woutT", [D, D]),
               b_out=I("l0b_bout", [D]), mask=I("mask_l0b", [128, 256]))
    l1a = dict(wq=I("l1a_wq", [D, D]), wk=I("l1a_wk", [D, D]), wv=I("l1a_wv", [D, D]),
               wproj=I("l1a_wproj", [D, D]), bq=I("l1a_bq", [D]),
               bproj=I("l1a_bproj", [D]), wr=I("l1a_wr", [D, 160]),
               br=I("l1a_br", [160]), bnd_rev=I("l1a_bnd_rev", [10, 1024]))
    f1 = dict(w1T=I("f1_w1T", [D, 4096]), w2T=I("f1_w2T", [D, 4096]),
              h1T=I("f1_h1T", [4096, 4096]), h2T=I("f1_h2T", [4096, 4096]),
              w3T=I("f1_w3T", [4096, D]), s1=I("f1_s1", [4096]), s2=I("f1_s2", [4096]),
              h1s=I("f1_h1s", [4096]), h2s=I("f1_h2s", [4096]), w3s=I("f1_w3s", [D]))
    l1b = dict(wqT=I("l1b_wqT", [D, D]), wkT=I("l1b_wkT", [D, D]),
               wvT=I("l1b_wvT", [D, D]), woT=I("l1b_woT", [D, D]),
               bq=I("l1b_bq", [D]), bk=I("l1b_bk", [D]), bv=I("l1b_bv", [D]),
               bo=I("l1b_bo", [D]), t_all=I("t_all", [16, 128, 4224]))
    f0c = dict(wgT=nc.dram_tensor("c_f0g", [D, 3072], BF16, kind="Internal").ap(),
               wuT=nc.dram_tensor("c_f0u", [D, 3072], BF16, kind="Internal").ap(),
               wdT=nc.dram_tensor("c_f0d", [3072, D], BF16, kind="Internal").ap())
    f1c = dict(w1T=nc.dram_tensor("c_f1w1", [D, 4096], BF16, kind="Internal").ap(),
               w2T=nc.dram_tensor("c_f1w2", [D, 4096], BF16, kind="Internal").ap(),
               h1T=nc.dram_tensor("c_f1h1", [4096, 4096], BF16, kind="Internal").ap(),
               h2T=nc.dram_tensor("c_f1h2", [4096, 4096], BF16, kind="Internal").ap(),
               w3T=nc.dram_tensor("c_f1w3", [4096, D], BF16, kind="Internal").ap())
    exits = contextlib.ExitStack()
    with exits:
        c = make_ctx(nc, exits)
        c.mask_l0a_dram = l0a["mask"]
        c.mask_l0b_dram = l0b["mask"]
        sp = SP()
        emit_const_init(sp, c, ident)
        prep0 = emit_prepass(sp, c, [(f0c["wgT"], f0["wgT"]), (f0c["wuT"], f0["wuT"]),
                                     (f0c["wdT"], f0["wdT"])])
        chain = [x_in, xa, xb, xa, xb, xa, xb, xa, x_out]
        state = None
        prep1 = None

        def boundary():
            c.entry_gates = [("mm", c.S("mm").n), ("ev", c.S("ev").n),
                             ("ac", c.S("ac").n)]
            c._wv_war = 0
            c._wqk_war = {}

        for i, sub in enumerate(["l0a", "ffn0", "l0b", "ffn0",
                                 "l1a", "ffn1", "l1b", "ffn1"]):
            src, dst = chain[i], chain[i + 1]
            boundary()
            lr = emit_ln(sp, c, src, x_tile_ready=state)
            if sub == "l0a":
                state = emit_attn_l0a(sp, c, l0a["wqkv"], l0a["bqkv"], l0a["wout"],
                                      l0a["bout"], src, dst, lr)
                prep1 = emit_prepass(sp, c, [(f1c[k], f1[k]) for k in
                                             ("w1T", "w2T", "h1T", "h2T", "w3T")],
                                     sem="prep1")
            elif sub == "ffn0":
                state = emit_ffn_l0(sp, c, f0c["wgT"], f0c["wuT"], f0c["wdT"],
                                    src, dst, lr, prep_gate=prep0)
            elif sub == "l0b":
                state = emit_attn_l0b(sp, c, l0b["wqkvT"], l0b["woutT"],
                                      l0b["b_out"], src, dst, lr)
            elif sub == "l1a":
                state = emit_attn_l1a(sp, c, l1a["wq"], l1a["bq"], l1a["wk"],
                                      l1a["wv"], l1a["wproj"], l1a["bproj"],
                                      l1a["wr"], l1a["br"], l1a["bnd_rev"],
                                      e_scr, src, dst, lr)
            elif sub == "ffn1":
                state = emit_ffn_l1(sp, c, f1c["w1T"], f1c["w2T"], f1c["h1T"],
                                    f1c["h2T"], f1c["w3T"], f1["s1"], f1["s2"],
                                    f1["h1s"], f1["h2s"], f1["w3s"], ident,
                                    src, dst, lr, prep_gate=prep1)
            elif sub == "l1b":
                state = emit_attn_l1b(sp, c, l1b["wqT"], l1b["bq"], l1b["wkT"],
                                      l1b["bk"], l1b["wvT"], l1b["bv"],
                                      l1b["woT"], l1b["bo"], l1b["t_all"],
                                      src, dst, lr)
        finalize(sp, c, state)
        build_block(nc, sp)
    return nc


def _fused_in_map(inputs):
    T = np.ascontiguousarray
    m = dict(ident=_ident_bf16(), mask_l0a=mask_l0a_np(), mask_l0b=mask_l0b_np())
    m.update(l0a_wqkv=inputs["L0_attn_a_Wqkv"], l0a_bqkv=inputs["L0_attn_a_bqkv"],
             l0a_wout=inputs["L0_attn_a_Wout"], l0a_bout=inputs["L0_attn_a_bout"])
    m.update(f0_wgT=T(inputs["L0_ffn_w_gate"].T), f0_wuT=T(inputs["L0_ffn_w_up"].T),
             f0_wdT=T(inputs["L0_ffn_w_down"].T))
    m.update(l0b_wqkvT=T(inputs["L0_attn_b_w_qkv"].T),
             l0b_woutT=T(inputs["L0_attn_b_w_out"].T),
             l0b_bout=inputs["L0_attn_b_b_out"])
    m.update(l1a_wq=inputs["L1_attn_a_Wq"], l1a_wk=inputs["L1_attn_a_Wk"],
             l1a_wv=inputs["L1_attn_a_Wv"], l1a_wproj=inputs["L1_attn_a_Wproj"],
             l1a_bq=inputs["L1_attn_a_bq"], l1a_bproj=inputs["L1_attn_a_bproj"],
             l1a_wr=inputs["L1_attn_a_Wr"], l1a_br=inputs["L1_attn_a_br"],
             l1a_bnd_rev=T(inputs["L1_attn_a_b_nd"][:, ::-1]))
    m.update(f1_w1T=T(inputs["L1_ffn_w1_w"].T), f1_w2T=T(inputs["L1_ffn_w2_w"].T),
             f1_h1T=T(inputs["L1_ffn_h1_w"].T), f1_h2T=T(inputs["L1_ffn_h2_w"].T),
             f1_w3T=T(inputs["L1_ffn_w3_w"].T),
             f1_s1=inputs["L1_ffn_w1_s"], f1_s2=inputs["L1_ffn_w2_s"],
             f1_h1s=inputs["L1_ffn_h1_s"], f1_h2s=inputs["L1_ffn_h2_s"],
             f1_w3s=inputs["L1_ffn_w3_s"])
    m.update(l1b_wqT=T(inputs["L1_attn_b_Wq"].T), l1b_wkT=T(inputs["L1_attn_b_Wk"].T),
             l1b_wvT=T(inputs["L1_attn_b_Wv"].T), l1b_woT=T(inputs["L1_attn_b_Wo"].T),
             l1b_bq=inputs["L1_attn_b_bq"], l1b_bk=inputs["L1_attn_b_bk"],
             l1b_bv=inputs["L1_attn_b_bv"], l1b_bo=inputs["L1_attn_b_bo"],
             t_all=t_all_np(inputs["L1_attn_b_pos_bias"]))
    return m


def kernel_fused(**inputs):
    B = 8
    x = np.ascontiguousarray(inputs["L0_attn_a_x"].astype(np.float32, copy=False))
    nc = _get_program("fused", build_fused_program)
    common = _fused_in_map(inputs)
    in_maps = [dict(common, x_in=x[b]) for b in range(B)]
    res = run_bass_kernel_spmd(nc, in_maps, core_ids=list(range(B)))
    return np.stack([np.asarray(res.results[b]["x_out"]) for b in range(B)]
                    ).astype(np.float32)


_PROGRAMS = {}


def _get_program(name, builder):
    if name not in _PROGRAMS:
        _PROGRAMS[name] = builder()
    return _PROGRAMS[name]


def _prep_host(inputs):
    T = np.ascontiguousarray
    h = {}
    h["ident"] = _ident_bf16()
    h["l0a"] = dict(
        wqkv=inputs["L0_attn_a_Wqkv"], bqkv=inputs["L0_attn_a_bqkv"],
        wout=inputs["L0_attn_a_Wout"], bout=inputs["L0_attn_a_bout"],
        mask_l0a=mask_l0a_np(), ident=h["ident"])
    h["ffn0"] = dict(
        wgT=T(inputs["L0_ffn_w_gate"].T), wuT=T(inputs["L0_ffn_w_up"].T),
        wdT=T(inputs["L0_ffn_w_down"].T), ident=h["ident"])
    h["l0b"] = dict(
        wqkvT=T(inputs["L0_attn_b_w_qkv"].T), woutT=T(inputs["L0_attn_b_w_out"].T),
        b_out=inputs["L0_attn_b_b_out"], mask_l0b=mask_l0b_np(), ident=h["ident"])
    h["l1a"] = dict(
        wq=inputs["L1_attn_a_Wq"], wk=inputs["L1_attn_a_Wk"],
        wv=inputs["L1_attn_a_Wv"], wproj=inputs["L1_attn_a_Wproj"],
        bq=inputs["L1_attn_a_bq"], bproj=inputs["L1_attn_a_bproj"],
        wr=inputs["L1_attn_a_Wr"], br=inputs["L1_attn_a_br"],
        bnd_rev=T(inputs["L1_attn_a_b_nd"][:, ::-1]), ident=h["ident"])
    h["ffn1"] = dict(
        w1T=T(inputs["L1_ffn_w1_w"].T), w2T=T(inputs["L1_ffn_w2_w"].T),
        h1T=T(inputs["L1_ffn_h1_w"].T), h2T=T(inputs["L1_ffn_h2_w"].T),
        w3T=T(inputs["L1_ffn_w3_w"].T),
        s1=inputs["L1_ffn_w1_s"], s2=inputs["L1_ffn_w2_s"],
        h1s=inputs["L1_ffn_h1_s"], h2s=inputs["L1_ffn_h2_s"],
        w3s=inputs["L1_ffn_w3_s"], ident=h["ident"])
    h["l1b"] = dict(
        wqT=T(inputs["L1_attn_b_Wq"].T), wkT=T(inputs["L1_attn_b_Wk"].T),
        wvT=T(inputs["L1_attn_b_Wv"].T), woT=T(inputs["L1_attn_b_Wo"].T),
        bq=inputs["L1_attn_b_bq"], bk=inputs["L1_attn_b_bk"],
        bv=inputs["L1_attn_b_bv"], bo=inputs["L1_attn_b_bo"],
        t_all=t_all_np(inputs["L1_attn_b_pos_bias"]), ident=h["ident"])
    return h


_BUILDERS = dict(
    l0a=("l0a", lambda: build_attn_l0a_program()),
    ffn0=("ffn0", lambda: build_ffn_l0_program()),
    l0b=("l0b", lambda: build_attn_l0b_program()),
    l1a=("l1a", lambda: build_attn_l1a_program()),
    ffn1=("ffn1", lambda: build_ffn_l1_program()),
    l1b=("l1b", lambda: build_attn_l1b_program()),
)


def kernel(**inputs):
    return kernel_fused(**inputs)


# version 12 (final)
# speedup vs baseline: 1.2270x; 1.0440x over previous
import contextlib
import math
import numpy as np
import ml_dtypes
import concourse.bass as bass
import concourse.mybir as mybir
from concourse.bass_utils import run_bass_kernel_spmd
from concourse.ap import AP

F32 = mybir.dt.float32
BF16 = mybir.dt.bfloat16
I32 = mybir.dt.int32
AF = mybir.ActivationFunctionType
ALU = mybir.AluOpType
AX = mybir.AxisListType

S = 2048
D = 1024
TT = 16
FC = 8
NEG = -1.0e30

ENGINES = ("sync", "tensor", "vector", "scalar", "gpsimd")


class Sem:
    def __init__(self, handle):
        self.h = handle
        self.n = 0


class SP:
    def __init__(self):
        self.streams = {e: [] for e in ENGINES}

    def on(self, engine, fn):
        self.streams[engine].append(fn)


class Arena:
    def __init__(self, handle, cols):
        self.h = handle
        self.cols = cols
        self.off = 0
        self.marks = []

    def alloc(self, n, align=32):
        o = (self.off + align - 1) // align * align
        assert o + n <= self.cols, f"arena overflow: {o}+{n} > {self.cols}"
        self.off = o + n
        return self.h[:, o:o + n]

    def push(self):
        self.marks.append(self.off)

    def pop(self):
        self.off = self.marks.pop()


def cdiv(a, b):
    return (a + b - 1) // b


class Ctx:
    def S(self, name):
        if name not in self.sem:
            h = self.exits.enter_context(self.nc.semaphore("s_" + name))
            self.sem[name] = Sem(h)
        return self.sem[name]


def make_ctx(nc, exits):
    c = Ctx()
    c.nc = nc
    c.exits = exits
    c.sem = {}
    ent = exits.enter_context
    c.ln_fm = ent(nc.sbuf_tensor("ln_fm", [128, FC * S], BF16))
    c.ident = ent(nc.sbuf_tensor("c_ident", [128, 128], BF16))
    c.ones_row = ent(nc.sbuf_tensor("c_ones", [1, 512], BF16))
    ab_cols = 57344
    af_cols = 8192
    c.ab = Arena(ent(nc.sbuf_tensor("arena_bf", [128, ab_cols], BF16)), ab_cols)
    c.af = Arena(ent(nc.sbuf_tensor("arena_f32", [128, af_cols], F32)), af_cols)
    c.stat = ent(nc.sbuf_tensor("stat", [128, 96], F32))
    c.sq_scr = ent(nc.sbuf_tensor("sq_scr", [128, D], BF16))
    c.xbuf = ent(nc.sbuf_tensor("xbuf", [128, 2 * D], F32))
    c.xnew = ent(nc.sbuf_tensor("xnew", [128, 2 * D], F32))
    c.lntm = ent(nc.sbuf_tensor("lntm", [128, 2 * D], BF16))
    c.bias_row = ent(nc.sbuf_tensor("bias_row", [1, 4096], BF16))
    c.bias_col = ent(nc.sbuf_tensor("bias_col", [128, 32], F32))
    c.ps = [ent(nc.psum_tensor(f"ps{i}", [128, 512], F32)) for i in range(6)]
    c.pt = [ent(nc.psum_tensor(f"pt{i}", [128, 1024], BF16)) for i in range(2)]
    for name in ("mm", "ev", "ac", "gp", "vch", "ach"):
        c.S(name)
    c.ps_free = [("ev", 0)] * 6
    c.pt_free = [("ev", 0)] * 2
    return c


def wait(eng, c, key, v):
    if v > 0:
        eng.wait_ge(c.sem[key].h, v)


def dma(sp, c, queue, sem_name, pairs, gates=()):
    sm = c.S(sem_name)
    sm.n += 16 * len(pairs)
    tot = sm.n

    eg = tuple(getattr(c, "entry_gates", ()))

    def fn(g, pairs=pairs, gates=tuple(gates), eg=eg):
        for (k, v) in eg:
            wait(g, c, k, v)
        for (k, v) in gates:
            wait(g, c, k, v)
        for (out_ap, in_ap) in pairs:
            g.dma_start(out=out_ap, in_=in_ap).then_inc(sm.h, 16)
    sp.on(queue, fn)
    return tot


def emit_prepass(sp, c, pairs, sem="prep", chunk_rows=512, defer=False):
    dmas = []
    for (dst, src) in pairs:
        rows = src.shape[0]
        step = min(chunk_rows, rows)
        for r0 in range(0, rows, step):
            r1 = min(rows, r0 + step)
            dmas.append((dst[r0:r1], src[r0:r1]))
    if defer:
        c.bg_chunks = list(dmas)
        c.bg_sem = sem
        return (sem, c.S(sem).n + 16 * len(dmas))
    tot = dma(sp, c, "gpsimd", sem, dmas)
    return (sem, tot)


def drain_bg(sp, c, n):
    chunks = getattr(c, "bg_chunks", None)
    if not chunks:
        return
    batch, c.bg_chunks = chunks[:n], chunks[n:]
    dma(sp, c, "gpsimd", c.bg_sem, batch)


def emit_const_init(sp, c, ident_dram):
    sev = c.S("ev")
    idv = dma(sp, c, "sync", "aux", [(c.ident[:, :], ident_dram)])
    sev.n += 1
    done = sev.n

    def ones(v):
        v.memset(c.ones_row[:, :], 1.0)
        v.memset(c.bias_col[:, 0:1], 1e-5).then_inc(sev.h, 1)
    sp.on("vector", ones)

    def bar(te, idv=idv, done=done):
        te.wait_ge(c.sem["aux"].h, idv)
        te.wait_ge(sev.h, done)
    sp.on("tensor", bar)
    c.const_ready = done


def emit_ln(sp, c, x_dram, x_tile_ready=None):
    sac, sev, smm = c.S("ac"), c.S("ev"), c.S("mm")
    eps = 1e-5
    ln2_done = [0] * TT
    for t in range(TT):
        buf = t % 2
        sb = buf * 16
        xsl = c.xbuf[:, buf * D:(buf + 1) * D]
        lnsl = c.lntm[:, buf * D:(buf + 1) * D]

        gates = []
        if x_tile_ready is not None and x_tile_ready[t] is not None:
            gates.append(x_tile_ready[t])
        if t >= 2:
            gates.append(("ev", ln2_done[t - 2]))
        xv = dma(sp, c, "sync", f"xl{buf}",
                 [(xsl, x_dram[t * 128:(t + 1) * 128, :])], gates=gates)

        stat_war = 0 if t < 2 else ln2_done[t - 2]
        sac.n += 1
        sd = sac.n

        def stats(a, xsl=xsl, xv=xv, sb=sb, war=stat_war, buf=buf, lnsl_=lnsl):
            a.wait_ge(c.sem[f"xl{buf}"].h, xv)
            wait(a, c, "ev", war)
            a.activation(c.sq_scr[:, :], xsl, AF.Square,
                         accum_out=c.stat[:, sb + 0:sb + 1])
            a.activation(lnsl_, xsl, AF.Identity,
                         accum_out=c.stat[:, sb + 1:sb + 2]).then_inc(sac.h, 1)
        sp.on("scalar", stats)

        vch = c.S("vch")
        vb = vch.n
        vch.n += 3
        sev.n += 1
        v1 = sev.n

        def lnop(v, sb=sb, sd=sd, vb=vb):
            v.wait_ge(sac.h, sd)
            st = c.stat
            v.tensor_mul(st[:, sb + 2:sb + 3], st[:, sb + 1:sb + 2],
                         st[:, sb + 1:sb + 2]).then_inc(vch.h, 1)
            v.tensor_scalar_mul(st[:, sb + 3:sb + 4], st[:, sb + 1:sb + 2],
                                1.0 / D).then_inc(vch.h, 1)
            v.wait_ge(vch.h, vb + 1)
            v.scalar_tensor_tensor(st[:, sb + 4:sb + 5], st[:, sb + 2:sb + 3],
                                   1.0 / D, st[:, sb + 0:sb + 1],
                                   ALU.mult, ALU.subtract).then_inc(sev.h, 1)
        sp.on("vector", lnop)

        sac.n += 1
        rs = sac.n

        def rstd(a, sb=sb, v1=v1):
            a.wait_ge(sev.h, v1)
            a.activation(c.stat[:, sb + 5:sb + 6], c.stat[:, sb + 4:sb + 5],
                         AF.Sqrt, scale=-1.0 / D,
                         bias=c.bias_col[:, 0:1]).then_inc(sac.h, 1)
        sp.on("scalar", rstd)

        sev.n += 1
        ln2_done[t] = sev.n

        def lnop2(v, xsl=xsl, lnsl=lnsl, sb=sb, rs=rs, vb=vb):
            v.wait_ge(sac.h, rs)
            v.reciprocal(c.stat[:, sb + 6:sb + 7],
                         c.stat[:, sb + 5:sb + 6]).then_inc(vch.h, 1)
            v.wait_ge(vch.h, vb + 3)
            v.tensor_scalar(lnsl, xsl, c.stat[:, sb + 3:sb + 4],
                            c.stat[:, sb + 6:sb + 7],
                            ALU.subtract, ALU.mult).then_inc(sev.h, 1)
        sp.on("vector", lnop2)

        for cc in range(FC):
            pb = (t * FC + cc) % 2
            smm.n += 1
            trv = smm.n
            ptf = c.pt_free[pb]

            def tr(te, lnsl=lnsl, cc=cc, pb=pb, l2=ln2_done[t], ptf=ptf):
                te.wait_ge(sev.h, l2)
                wait(te, c, ptf[0], ptf[1])
                te.transpose(c.pt[pb][:, 0:128], lnsl[:, cc * 128:(cc + 1) * 128],
                             c.ident[:, :]).then_inc(smm.h, 1)
            sp.on("tensor", tr)

            sev.n += 1
            c.pt_free[pb] = ("ev", sev.n)

            def ev(v, cc=cc, t=t, pb=pb, trv=trv):
                v.wait_ge(smm.h, trv)
                v.tensor_copy(c.ln_fm[:, cc * S + t * 128: cc * S + (t + 1) * 128],
                              c.pt[pb][:, 0:128]).then_inc(sev.h, 1)
            sp.on("vector", ev)
    return sev.n


def emit_store_tile(sp, c, xo_dram, t, xnsl, ready_ev):
    stv = dma(sp, c, "sync", f"st{t % 2}",
              [(xo_dram[t * 128:(t + 1) * 128, :], xnsl)],
              gates=[("ev", ready_ev)])
    return (f"st{t % 2}", stv)


def emit_ffn_l0(sp, c, wgT, wuT, wdT, x_dram, xo_dram, ln_ready, prep_gate=None):
    nc = c.nc
    smm, sev, sac = c.S("mm"), c.S("ev"), c.S("ac")
    HB = 24
    c.ab.push()
    h_fm = c.ab.alloc(HB * 512)
    sg2 = [c.ab.alloc(512), c.ab.alloc(512)]
    wg_b = [c.ab.alloc(FC * 128) for _ in range(2)]
    wu_b = [c.ab.alloc(FC * 128) for _ in range(2)]
    wd_b = c.ab.alloc(HB * 1024)
    x_out_ready = [None] * TT

    wd_done = dma(sp, c, "sync", "fwd",
                  [(wd_b[:, m * 1024:(m + 1) * 1024], wdT[m * 128:(m + 1) * 128, :])
                   for m in range(HB)],
                  gates=[prep_gate] if prep_gate else [])

    mm_done_slot = [0, 0]
    hmul_done = {}
    down_done = 0
    evac_seq = 0
    for n in range(4):
        tok = slice(n * 512, (n + 1) * 512)
        for m in range(HB):
            drain_bg(sp, c, 2)
            dbl = m % 2
            war = [("mm", mm_done_slot[dbl])] if mm_done_slot[dbl] else []
            lg = dma(sp, c, "sync", f"fw{dbl}", [
                (wg_b[dbl].rearrange("p (c m) -> p c m", c=FC),
                 wgT[:, m * 128:(m + 1) * 128].rearrange("(c p) m -> p c m", p=128)),
                (wu_b[dbl].rearrange("p (c m) -> p c m", c=FC),
                 wuT[:, m * 128:(m + 1) * 128].rearrange("(c p) m -> p c m", p=128)),
            ], gates=war)
            pg = 2 * dbl
            pu = pg + 1
            gate_free = c.ps_free[pg]
            up_free = c.ps_free[pu]
            smm.n += 2
            mm_v = smm.n
            mm_done_slot[dbl] = mm_v

            def mms(te, dbl=dbl, lg=lg, tok=tok, pg=pg, pu=pu,
                    gf=gate_free, uf=up_free, lr=ln_ready):
                te.wait_ge(c.sem[f"fw{dbl}"].h, lg)
                te.wait_ge(sev.h, lr)
                wait(te, c, gf[0], gf[1])
                wait(te, c, uf[0], uf[1])
                for cc in range(FC):
                    ins = te.matmul(c.ps[pg][:, :], wg_b[dbl][:, cc * 128:(cc + 1) * 128],
                                    c.ln_fm[:, cc * S:(cc + 1) * S][:, tok],
                                    start=(cc == 0), stop=(cc == FC - 1))
                    if cc == FC - 1:
                        ins.then_inc(smm.h, 1)
                for cc in range(FC):
                    ins = te.matmul(c.ps[pu][:, :], wu_b[dbl][:, cc * 128:(cc + 1) * 128],
                                    c.ln_fm[:, cc * S:(cc + 1) * S][:, tok],
                                    start=(cc == 0), stop=(cc == FC - 1))
                    if cc == FC - 1:
                        ins.then_inc(smm.h, 1)
            sp.on("tensor", mms)

            sac.n += 1
            sa = sac.n
            sg = sg2[dbl]
            sg_war = hmul_done.get((n, m - 2), hmul_done.get((n - 1, HB - 2 + dbl), 0))

            def silu(a, pg=pg, mm_v=mm_v, sg=sg, war=sg_war):
                a.wait_ge(smm.h, mm_v - 1)
                wait(a, c, "ev", war)
                a.activation(sg, c.ps[pg][:, :], AF.Silu,
                             scale=1.0 / 127.0).then_inc(sac.h, 1)
            sp.on("scalar", silu)
            c.ps_free[pg] = ("ac", sa)

            sev.n += 1
            hv = sev.n
            hmul_done[(n, m)] = hv

            def hmul(v, m=m, pu=pu, sa=sa, mm_v=mm_v, sg=sg, dd=down_done):
                v.wait_ge(sac.h, sa)
                v.wait_ge(smm.h, mm_v)
                wait(v, c, "mm", dd)
                v.scalar_tensor_tensor(h_fm[:, m * 512:(m + 1) * 512],
                                       c.ps[pu][:, :], 1.0 / 127.0, sg,
                                       ALU.mult, ALU.mult).then_inc(sev.h, 1)
            sp.on("vector", hmul)
            c.ps_free[pu] = ("ev", hv)
        h_done = sev.n

        for tt_i in range(4):
            t = n * 4 + tt_i
            buf = t % 2
            xsl = c.xbuf[:, buf * D:(buf + 1) * D]
            xnsl = c.xnew[:, buf * D:(buf + 1) * D]
            gates = [("ev", ln_ready)]
            if t >= 2:
                gates.append(x_out_ready[t - 2])
            xv = dma(sp, c, "sync", f"xl{buf}",
                     [(xsl, x_dram[t * 128:(t + 1) * 128, :])], gates=gates)

            for e in range(2):
                pb = 4 + (evac_seq % 2)
                evac_seq += 1
                bf = c.ps_free[pb]
                smm.n += 1
                mm_v = smm.n

                def mmd(te, tt_i=tt_i, e=e, pb=pb, bf=bf, hd=h_done, wd=wd_done):
                    te.wait_ge(sev.h, hd)
                    te.wait_ge(c.sem["fwd"].h, wd)
                    wait(te, c, bf[0], bf[1])
                    for m in range(HB):
                        ins = te.matmul(c.ps[pb][:, :],
                                        h_fm[:, m * 512 + tt_i * 128: m * 512 + (tt_i + 1) * 128],
                                        wd_b[:, m * 1024 + e * 512: m * 1024 + (e + 1) * 512],
                                        start=(m == 0), stop=(m == HB - 1))
                    ins.then_inc(smm.h, 1)
                sp.on("tensor", mmd)
                down_done = smm.n

                sev.n += 1
                rv = sev.n
                c.ps_free[pb] = ("ev", rv)

                def resid(v, e=e, pb=pb, mm_v=mm_v, xv=xv, xsl=xsl, xnsl=xnsl, buf=buf):
                    v.wait_ge(smm.h, mm_v)
                    v.wait_ge(c.sem[f"xl{buf}"].h, xv)
                    v.scalar_tensor_tensor(
                        xnsl[:, e * 512:(e + 1) * 512], c.ps[pb][:, :],
                        1.0 / 127.0, xsl[:, e * 512:(e + 1) * 512],
                        ALU.mult, ALU.add).then_inc(sev.h, 1)
                sp.on("vector", resid)
            x_out_ready[t] = emit_store_tile(sp, c, xo_dram, t, xnsl, sev.n)
    c.ab.pop()
    return x_out_ready



def emit_bias_loads(sp, c, items):
    pairs = [(c.bias_row[0:1, off:off + n], ap[None, :]) for (ap, off, n) in items]
    return dma(sp, c, "gpsimd", "aux2", pairs)


def emit_fm_group(sp, c, ps_bank, lhs_tiles, rhs_tiles, bias_lhs=None, bias_rhs=None,
                  waits=(), tag=""):
    smm = c.S("mm")
    bf = c.ps_free[ps_bank]
    smm.n += 1
    mm_v = smm.n
    n_k = len(lhs_tiles)

    M = lhs_tiles[0].shape[-1]

    def fn(te, waits=tuple(waits), bf=bf, M=M):
        for (k, v) in waits:
            wait(te, c, k, v)
        wait(te, c, bf[0], bf[1])
        first = True
        if bias_lhs is not None:
            te.matmul(c.ps[ps_bank][0:M, 0:bias_rhs.shape[-1]], bias_lhs, bias_rhs,
                      start=True, stop=False)
            first = False
        for i in range(n_k):
            ins = te.matmul(c.ps[ps_bank][0:M, 0:rhs_tiles[i].shape[-1]],
                            lhs_tiles[i], rhs_tiles[i],
                            start=first, stop=(i == n_k - 1))
            first = False
        ins.then_inc(smm.h, 1)
    sp.on("tensor", fn)
    return mm_v


def emit_evac_copy(sp, c, ps_bank, width, out_ap, mm_v, engine="vector"):
    if engine == "vector":
        sev = c.S("ev")
        sev.n += 1
        vv = sev.n

        def fn(v, mm_v=mm_v):
            v.wait_ge(c.sem["mm"].h, mm_v)
            v.tensor_copy(out_ap, c.ps[ps_bank][:, 0:width]).then_inc(sev.h, 1)
        sp.on("vector", fn)
        c.ps_free[ps_bank] = ("ev", vv)
        return ("ev", vv)
    else:
        sac = c.S("ac")
        sac.n += 1
        vv = sac.n

        def fn(a, mm_v=mm_v):
            a.wait_ge(c.sem["mm"].h, mm_v)
            a.copy(out_ap, c.ps[ps_bank][:, 0:width]).then_inc(sac.h, 1)
        sp.on("scalar", fn)
        c.ps_free[ps_bank] = ("ac", vv)
        return ("ac", vv)


def emit_proj_residual(sp, c, a_fm_chunks, w_b_getter, bias_off, x_dram, xo_dram,
                       w_ready, a_ready_fn, ln_ready, scale=1.0, bias_gate=None):
    sev = c.S("ev")
    x_out_ready = [None] * TT
    evac_seq = getattr(c, "_proj_seq", 0)
    for t in range(TT):
        buf = t % 2
        xsl = c.xbuf[:, buf * D:(buf + 1) * D]
        xnsl = c.xnew[:, buf * D:(buf + 1) * D]
        gates = [("ev", ln_ready)]
        if t >= 2:
            gates.append(x_out_ready[t - 2])
        xv = dma(sp, c, "sync", f"xl{buf}",
                 [(xsl, x_dram[t * 128:(t + 1) * 128, :])], gates=gates)
        for e in range(2):
            pb = 4 + (evac_seq % 2)
            evac_seq += 1
            waits = [a_ready_fn(t), ("wp", w_ready)]
            if bias_gate is not None:
                waits.append(bias_gate)
            bias_lhs = c.ones_row[0:1, 0:128] if bias_off is not None else None
            bias_rhs = (c.bias_row[0:1, bias_off + e * 512: bias_off + (e + 1) * 512]
                        if bias_off is not None else None)
            mm_v = emit_fm_group(
                sp, c, pb,
                [a_fm_chunks(cc, t) for cc in range(FC)],
                [w_b_getter(cc, e) for cc in range(FC)],
                bias_lhs=bias_lhs, bias_rhs=bias_rhs, waits=waits)
            sev.n += 1
            rv = sev.n
            c.ps_free[pb] = ("ev", rv)

            def resid(v, e=e, pb=pb, mm_v=mm_v, xv=xv, xsl=xsl, xnsl=xnsl,
                      buf=buf, scale=scale):
                v.wait_ge(c.sem["mm"].h, mm_v)
                v.wait_ge(c.sem[f"xl{buf}"].h, xv)
                v.scalar_tensor_tensor(
                    xnsl[:, e * 512:(e + 1) * 512], c.ps[pb][:, :],
                    scale, xsl[:, e * 512:(e + 1) * 512],
                    ALU.mult, ALU.add).then_inc(sev.h, 1)
            sp.on("vector", resid)
        x_out_ready[t] = emit_store_tile(sp, c, xo_dram, t, xnsl, sev.n)
    c._proj_seq = evac_seq
    return x_out_ready


def emit_softmax(sp, c, strips, scale, par, stat_base=32):
    sev, sac, vch = c.S("ev"), c.S("ac"), c.S("vch")
    for h, st in enumerate(strips):
        sb = stat_base + par * 8 + h * 4
        W = st["W"]
        s1 = st["s1"]
        seg_done = st.get("pre_done", 0)
        for (pbk, off, w, mm_v, bias_ap) in st["segments"]:
            sev.n += 1
            ev_v = sev.n
            c.ps_free[pbk] = ("ev", ev_v)

            def evac(v, pbk=pbk, off=off, w=w, mm_v=mm_v, bias_ap=bias_ap,
                     s1=s1, war=st.get("s1_war", 0), sg=st.get("seg_gate")):
                v.wait_ge(c.sem["mm"].h, mm_v)
                wait(v, c, "ac", war)
                if sg is not None:
                    wait(v, c, sg[0], sg[1])
                if bias_ap is not None:
                    v.scalar_tensor_tensor(s1[:, off:off + w], c.ps[pbk][:, 0:w],
                                           scale, bias_ap, ALU.mult, ALU.add)\
                     .then_inc(sev.h, 1)
                else:
                    v.tensor_scalar(s1[:, off:off + w], c.ps[pbk][:, 0:w],
                                    scale, None, ALU.mult).then_inc(sev.h, 1)
            sp.on("vector", evac)
            seg_done = ev_v
        st["seg_evac_done"] = seg_done
        vch.n += 1
        nm_v = vch.n
        sd = seg_done

        def rmax(v, s1=s1, sb=sb, W=W, sd=sd):
            v.wait_ge(sev.h, sd)
            v.reduce_max(c.stat[:, sb:sb + 1], s1[:, 0:W], axis=AX.X,
                         negate=True).then_inc(vch.h, 1)
        sp.on("vector", rmax)
        sac.n += 1
        ex_v = sac.n
        p_war_h = st.get("p_war", 0)
        sum_war = st.get("sum_war", 0)

        def expf(a, s1=s1, p=st["p"], sb=sb, W=W, nm_v=nm_v, pw=p_war_h,
                 sw=sum_war):
            a.wait_ge(c.sem["vch"].h, max(nm_v, sw))
            wait(a, c, "mm", pw)
            a.activation(p[:, 0:W], s1[:, 0:W], AF.Exp,
                         bias=c.stat[:, sb:sb + 1],
                         accum_out=c.stat[:, sb + 1:sb + 2]).then_inc(sac.h, 1)
        sp.on("scalar", expf)
        st["exp_done"] = ex_v
        st["_sb"] = sb
    return strips


def finish_softmax(sp, c, strips):
    sev, sac, vch = c.S("ev"), c.S("ac"), c.S("vch")
    p_ready = []
    for st in strips:
        sb = st["_sb"]
        W = st["W"]
        ex_v = st["exp_done"]
        vch.n += 2
        rc_v = vch.n - 1
        sev.n += 1
        pn_v = sev.n
        st["recip_done"] = rc_v

        def norm(v, p=st["p"], sb=sb, W=W, ex_v=ex_v, rc_v=rc_v):
            v.wait_ge(sac.h, ex_v)
            v.reciprocal(c.stat[:, sb + 2:sb + 3],
                         c.stat[:, sb + 1:sb + 2]).then_inc(vch.h, 1)
            v.wait_ge(c.sem["vch"].h, rc_v)
            v.tensor_scalar_mul(p[:, 0:W], p[:, 0:W],
                                c.stat[:, sb + 2:sb + 3]).then_inc(vch.h, 1)
            v.engine_nop().then_inc(sev.h, 1)
        sp.on("vector", norm)
        p_ready.append(pn_v)
    return p_ready


def emit_pv2(sp, c, p_strips, p_ready, kc_list, v_tm, heads, a_banks, qw,
             pt_slots, slot_war):
    smm, sev = c.S("mm"), c.S("ev")
    nk = len(kc_list)
    af = [c.ps_free[a_banks[0]], c.ps_free[a_banks[1]]]
    for h in (0, 1):
        gi = 0
        for g0 in range(0, nk, 4):
            gn = min(4, nk - g0)
            pb = gi % 2
            gi += 1
            ptf = c.pt_free[pb]
            trv0 = smm.n
            for u in range(gn):
                j = g0 + u
                smm.n += 1

                def tr(te, h=h, j=j, u=u, pb=pb, pr=p_ready[h], ptf=ptf,
                       p=p_strips[h], first=(u == 0)):
                    if first:
                        te.wait_ge(sev.h, pr)
                        wait(te, c, ptf[0], ptf[1])
                    te.transpose(c.pt[pb][:, u * 128:(u + 1) * 128],
                                 p[:, j * 128:(j + 1) * 128],
                                 c.ident[:, :]).then_inc(smm.h, 1)
                sp.on("tensor", tr)
            tr_v = smm.n
            slot = pt_slots[h][(g0 // 4) % 2]
            sw = slot_war.get((h, (g0 // 4) % 2), 0)
            if gi % 2:
                sev2 = c.S("ev")
                sev2.n += 1
                pt_done = ("ev", sev2.n)
                c.pt_free[pb] = pt_done

                def ptevac(v, pb=pb, tr_v=tr_v, slot=slot, sw=sw, gn=gn):
                    v.wait_ge(smm.h, tr_v)
                    wait(v, c, "mm", sw)
                    v.tensor_copy(slot[:, 0:gn * 128],
                                  c.pt[pb][:, 0:gn * 128]).then_inc(c.sem["ev"].h, 1)
                sp.on("vector", ptevac)
            else:
                sac = c.S("ac")
                sac.n += 1
                pt_done = ("ac", sac.n)
                c.pt_free[pb] = pt_done

                def ptevac(a, pb=pb, tr_v=tr_v, slot=slot, sw=sw, gn=gn):
                    a.wait_ge(smm.h, tr_v)
                    wait(a, c, "mm", sw)
                    a.copy(slot[:, 0:gn * 128],
                           c.pt[pb][:, 0:gn * 128]).then_inc(sac.h, 1)
                sp.on("scalar", ptevac)
            pv_last = 0
            for u in range(gn):
                j = g0 + u
                kc = kc_list[j]
                smm.n += 1
                pv_last = smm.n
                afw = af[h] if j == 0 else None

                def pv(te, h=h, kc=kc, u=u, slot=slot, pt_done=pt_done, afw=afw,
                       first=(j == 0), last=(j == nk - 1)):
                    wait(te, c, pt_done[0], pt_done[1])
                    if afw is not None:
                        wait(te, c, afw[0], afw[1])
                    hh = heads[h]
                    te.matmul(c.ps[a_banks[h]][64 * h:64 * h + 64, 0:qw],
                              v_tm[:, kc * 1024 + 64 * hh: kc * 1024 + 64 * hh + 64],
                              slot[:, u * qw:(u + 1) * qw] if qw == 128 else slot[:, 0:qw],
                              start=first, stop=last, skip_group_check=True,
                              tile_position=(0, 64 * h)).then_inc(smm.h, 1)
                sp.on("tensor", pv)
            slot_war[(h, (g0 // 4) % 2)] = pv_last
    return smm.n


def a_evac2(sp, c, a_banks, qw, out_ap_fn, mm_v):
    sac = c.S("ac")
    for h in (0, 1):
        sac.n += 1

        def ev(a, h=h, mm_v=mm_v):
            a.wait_ge(c.sem["mm"].h, mm_v)
            a.copy(out_ap_fn(h),
                   c.ps[a_banks[h]][64 * h:64 * h + 64, 0:qw]).then_inc(sac.h, 1)
        sp.on("scalar", ev)
        c.ps_free[a_banks[h]] = ("ac", sac.n)
    return sac.n


def a_evac(sp, c, a_ps, qw, out_ap, mm_v):
    return emit_evac_copy(sp, c, a_ps, qw, out_ap, mm_v, engine="vector")


def emit_attn_l0a(sp, c, wqkv, bqkv, wout, bout, x_dram, xo_dram, ln_ready):
    nc = c.nc
    smm, sev, sac = c.S("mm"), c.S("ev"), c.S("ac")
    c.ab.push()
    c.af.push()
    v_tm = c.ab.alloc(TT * 1024)
    a_fm = c.ab.alloc(FC * S)
    q_fm = c.ab.alloc(S)
    k_fm = c.ab.alloc(S)
    p_str2 = [[c.ab.alloc(512) for _ in range(3)], [c.ab.alloc(512) for _ in range(3)]]
    pt_sl = [[c.ab.alloc(384), c.ab.alloc(384)], [c.ab.alloc(384), c.ab.alloc(384)]]
    c.ab.push()
    wv_b = c.ab.alloc(FC * 512)
    s1_str = [c.af.alloc(512), c.af.alloc(512)]
    mask_sb = c.af.alloc(384)

    bias_v = emit_bias_loads(sp, c, [(bqkv, 0, 3072), (bout, 3072, 1024)])
    mask_v = dma(sp, c, "sync", "aux", [(mask_sb, c.mask_l0a_dram)])

    v_done_tile = [0] * TT
    for e in range(2):
        wv_v = dma(sp, c, "gpsimd", "wd",
                   [(wv_b.rearrange("p (c m) -> p c m", c=FC),
                     wqkv[:, 2048 + e * 512: 2048 + (e + 1) * 512]
                     .rearrange("(c p) m -> p c m", p=128))],
                   gates=[("mm", c._wv_war)] if c._wv_war else [])
        last_mm = 0
        for kc in range(TT):
            pbk = kc % 4
            mm_v = emit_fm_group(
                sp, c, pbk,
                [c.ln_fm[:, cc * S + kc * 128: cc * S + (kc + 1) * 128]
                 for cc in range(FC)],
                [wv_b[:, cc * 512:(cc + 1) * 512] for cc in range(FC)],
                bias_lhs=c.ones_row[0:1, 0:128],
                bias_rhs=c.bias_row[0:1, 2048 + e * 512: 2048 + (e + 1) * 512],
                waits=[("wd", wv_v), ("ev", ln_ready), ("aux2", bias_v)])
            emit_evac_copy(sp, c, pbk, 512,
                           v_tm[:, kc * 1024 + e * 512: kc * 1024 + (e + 1) * 512],
                           mm_v)
            last_mm = mm_v
            v_done_tile[kc] = max(v_done_tile[kc], c.sem["ev"].n)
        c._wv_war = last_mm
    v_done = sev.n
    v_mm_done = c.S("mm").n
    c.ab.pop()
    wqk_b2 = [c.ab.alloc(FC * 256), c.ab.alloc(FC * 256)]
    wout_b = c.ab.alloc(FC * 1024)

    war = {}
    pt_slot_war = {}
    seq = 0

    def _l0a_fin(pend):
        blk, par, strips, kc_list = pend
        pr = finish_softmax(sp, c, strips)
        for h in (0, 1):
            war[(blk, h, "recip")] = strips[h]["recip_done"]
        return pr

    def _l0a_pv(pend, p_ready):
        blk, par, strips, kc_list = pend
        pp = blk // TT
        nb = blk % TT
        pv_v = emit_pv2(sp, c, [p_str2[0][par], p_str2[1][par]], p_ready,
                        kc_list, v_tm, (2 * pp, 2 * pp + 1), (4, 5), 128,
                        pt_sl, pt_slot_war)
        for h in (0, 1):
            war[(blk, h, "tr")] = pv_v
        a_evac2(sp, c, (4, 5), 128,
                lambda h, pp=pp, nb=nb: a_fm[64 * h:64 * h + 64,
                                             pp * S + nb * 128:pp * S + (nb + 1) * 128],
                pv_v)

    def _l0a_flush(pend):
        _l0a_pv(pend, _l0a_fin(pend))

    pendq = [None, None]
    for p in range(8):
        drain_bg(sp, c, 6)
        slot = p % 2
        wqk_b = wqk_b2[slot]
        prev_scores_done = c.S("mm").n
        wqk_v = dma(sp, c, "gpsimd", f"w{slot}", [
            (wqk_b[:, 0:FC * 128].rearrange("p (c m) -> p c m", c=FC),
             wqkv[:, 128 * p:128 * (p + 1)].rearrange("(c p2) m -> p2 c m", p2=128)),
            (wqk_b[:, FC * 128:FC * 256].rearrange("p (c m) -> p c m", c=FC),
             wqkv[:, 1024 + 128 * p: 1024 + 128 * (p + 1)]
             .rearrange("(c p2) m -> p2 c m", p2=128)),
        ], gates=[("mm", c._wqk_war.get(slot, v_mm_done))])
        qk_first = True
        for which, dst, boff in ((0, q_fm, 128 * p), (1, k_fm, 1024 + 128 * p)):
            for n in range(4):
                pbk = seq % 4
                seq += 1
                mm_v = emit_fm_group(
                    sp, c, pbk,
                    [wqk_b[:, which * FC * 128 + cc * 128: which * FC * 128 + (cc + 1) * 128]
                     for cc in range(FC)],
                    [c.ln_fm[:, cc * S + n * 512: cc * S + (n + 1) * 512]
                     for cc in range(FC)],
                    bias_lhs=c.bias_row[0:1, boff:boff + 128],
                    bias_rhs=c.ones_row[0:1, 0:512],
                    waits=[(f"w{slot}", wqk_v), ("ev", ln_ready), ("aux2", bias_v)])
                if qk_first:
                    sev.n += 1
                    vv = sev.n

                    def evq(v, mm_v=mm_v, dst=dst, n=n, psd=prev_scores_done, pbk=pbk):
                        v.wait_ge(c.sem["mm"].h, mm_v)
                        wait(v, c, "mm", psd)
                        v.tensor_copy(dst[:, n * 512:(n + 1) * 512],
                                      c.ps[pbk][:, :]).then_inc(sev.h, 1)
                    sp.on("vector", evq)
                    c.ps_free[pbk] = ("ev", vv)
                    qk_first = False
                else:
                    emit_evac_copy(sp, c, pbk, 512, dst[:, n * 512:(n + 1) * 512], mm_v)
        qk_done = sev.n
        c._wqk_war[slot] = c.S("mm").n

        for nb in range(TT):
            blk = p * TT + nb
            lo = max(0, (nb - 1) * 128)
            hi = min(S, (nb + 2) * 128)
            w = hi - lo
            moff = 128 if nb == 0 else 0
            par = blk % 3
            strips = []
            pend_pr = _l0a_fin(pendq[0]) if pendq[0] is not None else None
            for h in (0, 1):
                pbk = (nb * 2 + h) % 4
                smm.n += 1
                mm_v = smm.n
                bf = c.ps_free[pbk]

                def smat(te, h=h, nb=nb, lo=lo, hi=hi, w=w, pbk=pbk, bf=bf,
                         qd=qk_done, vd=v_done, mv=mask_v):
                    te.wait_ge(sev.h, max(qd, vd))
                    te.wait_ge(c.sem["aux"].h, mv)
                    wait(te, c, bf[0], bf[1])
                    te.matmul(c.ps[pbk][:, 0:w],
                              q_fm[64 * h:64 * h + 64, nb * 128:(nb + 1) * 128],
                              k_fm[64 * h:64 * h + 64, lo:hi],
                              start=True, stop=True,
                              tile_position=(64 * h, 0)).then_inc(smm.h, 1)
                sp.on("tensor", smat)
                strip = dict(
                    W=w, s1=s1_str[h], p=p_str2[h][par],
                    segments=[(pbk, 0, w, mm_v, mask_sb[:, moff:moff + w])],
                    s1_war=war.get((blk - 1, h, "exp"), 0),
                    p_war=war.get((blk - 3, h, "tr"), 0),
                    sum_war=war.get((blk - 3, h, "recip"), 0),
                )
                emit_softmax(sp, c, [strip], 1.0 / 32.0, par, stat_base=32 + 4 * h)
                strips.append(strip)
                war[(blk, h, "exp")] = strip["exp_done"]
            if pendq[0] is not None:
                _l0a_pv(pendq[0], pend_pr)
            pendq = [pendq[1], (blk, par, strips, list(range(lo // 128, hi // 128)))]
    for pe_ in pendq:
        if pe_ is not None:
            _l0a_flush(pe_)
    a_done_ac = c.S("ac").n

    wo_v = dma(sp, c, "gpsimd", "wp",
               [(wout_b.rearrange("p (c m) -> p c m", c=FC),
                 wout.rearrange("(c p) m -> p c m", p=128))],
               gates=[("mm", c.S("mm").n), ("ac", a_done_ac), ("ev", sev.n)])
    xor = emit_proj_residual(
        sp, c,
        lambda cc, t: a_fm[:, cc * S + t * 128: cc * S + (t + 1) * 128],
        lambda cc, e: wout_b[:, cc * 1024 + e * 512: cc * 1024 + (e + 1) * 512],
        3072, x_dram, xo_dram, wo_v,
        lambda t: ("ac", a_done_ac), ln_ready, bias_gate=("aux2", bias_v))
    c.ab.pop()
    c.af.pop()
    return xor



def emit_attn_l0b(sp, c, wqkvT, woutT, b_out, x_dram, xo_dram, ln_ready):
    nc = c.nc
    smm, sev, sac = c.S("mm"), c.S("ev"), c.S("ac")
    c.ab.push()
    c.af.push()
    v_tm = c.ab.alloc(TT * 1024)
    a_fm = c.ab.alloc(FC * S)
    q_fm = c.ab.alloc(S)
    k_fm = c.ab.alloc(S)
    p_str2 = [[c.ab.alloc(512) for _ in range(3)], [c.ab.alloc(512) for _ in range(3)]]
    pg_str2 = [[c.ab.alloc(16) for _ in range(3)], [c.ab.alloc(16) for _ in range(3)]]
    pt_sl = [[c.ab.alloc(256)], [c.ab.alloc(256)]]
    pgt_sl = [c.ab.alloc(128), c.ab.alloc(128)]
    ln_g = c.ab.alloc(FC * 16)
    vg_tm = c.ab.alloc(1024)
    kg_b = c.ab.alloc(16)
    s1_str = [c.af.alloc(512), c.af.alloc(512)]
    s1g_str = [c.af.alloc(16), c.af.alloc(16)]
    mask_sb = c.af.alloc(256)
    c.ab.push()
    wv_b = c.ab.alloc(FC * 512)

    bias_v = emit_bias_loads(sp, c, [(b_out, 0, 1024)])
    mask_v = dma(sp, c, "sync", "aux", [(mask_sb, c.mask_l0b_dram)])

    sev.n += 1
    lng_v = sev.n

    def lngf(v, lr=ln_ready):
        v.wait_ge(sev.h, lr)
        for cc in range(FC):
            src = AP(c.ln_fm, cc * S, [[FC * S, 128], [128, 16]])
            v.tensor_copy(ln_g[:, cc * 16:(cc + 1) * 16], src)
        v.engine_nop().then_inc(sev.h, 1)
    sp.on("vector", lngf)

    for e in range(2):
        wv_v = dma(sp, c, "gpsimd", "wd",
                   [(wv_b.rearrange("p (c m) -> p c m", c=FC),
                     wqkvT[:, 2048 + e * 512: 2048 + (e + 1) * 512]
                     .rearrange("(c p) m -> p c m", p=128))],
                   gates=[("mm", c._wv_war)] if c._wv_war else [])
        last_mm = 0
        for kc in range(TT):
            pbk = kc % 4
            mm_v = emit_fm_group(
                sp, c, pbk,
                [c.ln_fm[:, cc * S + kc * 128: cc * S + (kc + 1) * 128]
                 for cc in range(FC)],
                [wv_b[:, cc * 512:(cc + 1) * 512] for cc in range(FC)],
                waits=[("wd", wv_v), ("ev", ln_ready)])
            emit_evac_copy(sp, c, pbk, 512,
                           v_tm[:, kc * 1024 + e * 512: kc * 1024 + (e + 1) * 512],
                           mm_v)
            last_mm = mm_v
        pbk = 16 % 4
        mm_v = emit_fm_group(
            sp, c, pbk,
            [ln_g[:, cc * 16:(cc + 1) * 16] for cc in range(FC)],
            [wv_b[:, cc * 512:(cc + 1) * 512] for cc in range(FC)],
            waits=[("wd", wv_v), ("ev", lng_v)])
        sev.n += 1
        vv = sev.n

        def evg(v, mm_v=mm_v, e=e, pbk=pbk):
            v.wait_ge(c.sem["mm"].h, mm_v)
            v.tensor_copy(vg_tm[0:16, e * 512:(e + 1) * 512],
                          c.ps[pbk][0:16, :]).then_inc(sev.h, 1)
        sp.on("vector", evg)
        c.ps_free[pbk] = ("ev", vv)
        c._wv_war = c.S("mm").n
    v_done = sev.n
    v_mm_done = c.S("mm").n
    c.ab.pop()
    wqk_b2 = [c.ab.alloc(FC * 256), c.ab.alloc(FC * 256)]
    wout_b = c.ab.alloc(FC * 1024)

    war = {}
    pt_slot_war = {}
    pgt_slot_war = {}
    seq = 0

    def _l0b_fin(pend):
        blk, par, strips, stripsg, kc_list = pend
        pr = finish_softmax(sp, c, strips)
        prg = finish_softmax(sp, c, stripsg)
        for h in (0, 1):
            war[(blk, h, "recip")] = strips[h]["recip_done"]
            war[(blk, h, "grecip")] = stripsg[h]["recip_done"]
        return (pr, prg)

    def _l0b_pv(pend, prs):
        blk, par, strips, stripsg, kc_list = pend
        p_ready, pg_ready = prs
        pp = blk // TT
        nb = blk % TT
        pv_v = emit_pv_l0b(sp, c, [p_str2[0][par], p_str2[1][par]],
                           [pg_str2[0][par], pg_str2[1][par]], p_ready, pg_ready,
                           kc_list, v_tm, vg_tm, (2 * pp, 2 * pp + 1), (4, 5),
                           pt_sl, pgt_sl, pt_slot_war, pgt_slot_war)
        for h in (0, 1):
            war[(blk, h, "tr")] = pv_v
        a_evac2(sp, c, (4, 5), 128,
                lambda h, pp=pp, nb=nb: a_fm[64 * h:64 * h + 64,
                                             pp * S + nb * 128:pp * S + (nb + 1) * 128],
                pv_v)

    def _l0b_flush(pend):
        _l0b_pv(pend, _l0b_fin(pend))

    pendq = [None, None]
    for p in range(8):
        slot = p % 2
        wqk_b = wqk_b2[slot]
        prev_scores_done = c.S("mm").n
        wqk_v = dma(sp, c, "gpsimd", f"w{slot}", [
            (wqk_b[:, 0:FC * 128].rearrange("p (c m) -> p c m", c=FC),
             wqkvT[:, 128 * p:128 * (p + 1)].rearrange("(c p2) m -> p2 c m", p2=128)),
            (wqk_b[:, FC * 128:FC * 256].rearrange("p (c m) -> p c m", c=FC),
             wqkvT[:, 1024 + 128 * p: 1024 + 128 * (p + 1)]
             .rearrange("(c p2) m -> p2 c m", p2=128)),
        ], gates=[("mm", c._wqk_war.get(slot, v_mm_done))])
        qk_first = True
        for which, dst in ((0, q_fm), (1, k_fm)):
            for n in range(4):
                pbk = seq % 4
                seq += 1
                mm_v = emit_fm_group(
                    sp, c, pbk,
                    [wqk_b[:, which * FC * 128 + cc * 128: which * FC * 128 + (cc + 1) * 128]
                     for cc in range(FC)],
                    [c.ln_fm[:, cc * S + n * 512: cc * S + (n + 1) * 512]
                     for cc in range(FC)],
                    waits=[(f"w{slot}", wqk_v), ("ev", ln_ready)])
                if qk_first:
                    sev.n += 1
                    vv = sev.n

                    def evq(v, mm_v=mm_v, dst=dst, n=n, psd=prev_scores_done, pbk=pbk):
                        v.wait_ge(c.sem["mm"].h, mm_v)
                        wait(v, c, "mm", psd)
                        v.tensor_copy(dst[:, n * 512:(n + 1) * 512],
                                      c.ps[pbk][:, :]).then_inc(sev.h, 1)
                    sp.on("vector", evq)
                    c.ps_free[pbk] = ("ev", vv)
                    qk_first = False
                else:
                    emit_evac_copy(sp, c, pbk, 512, dst[:, n * 512:(n + 1) * 512], mm_v)
        pbk = seq % 4
        seq += 1
        mm_v = emit_fm_group(
            sp, c, pbk,
            [wqk_b[:, FC * 128 + cc * 128: FC * 128 + (cc + 1) * 128] for cc in range(FC)],
            [ln_g[:, cc * 16:(cc + 1) * 16] for cc in range(FC)],
            waits=[(f"w{slot}", wqk_v), ("ev", lng_v)])
        sev.n += 1
        kgv = sev.n

        def evkg(v, mm_v=mm_v, pbk=pbk, psd=prev_scores_done):
            v.wait_ge(c.sem["mm"].h, mm_v)
            wait(v, c, "mm", psd)
            v.tensor_copy(kg_b, c.ps[pbk][:, 0:16]).then_inc(sev.h, 1)
        sp.on("vector", evkg)
        c.ps_free[pbk] = ("ev", kgv)
        qk_done = sev.n
        c._wqk_war[slot] = c.S("mm").n

        for nb in range(TT):
            blk = p * TT + nb
            lo = max(0, (nb - 1) * 128)
            hi = (nb + 1) * 128
            w = hi - lo
            moff = 256 - w
            par = blk % 3
            strips = []
            stripsg = []
            pend_pr = _l0b_fin(pendq[0]) if pendq[0] is not None else None
            for h in (0, 1):
                pbk = (nb * 2 + h) % 4
                smm.n += 1
                mm_v = smm.n
                bf = c.ps_free[pbk]

                def smat(te, h=h, nb=nb, lo=lo, hi=hi, w=w, pbk=pbk, bf=bf,
                         qd=qk_done, vd=v_done, mv=mask_v):
                    te.wait_ge(sev.h, max(qd, vd))
                    te.wait_ge(c.sem["aux"].h, mv)
                    wait(te, c, bf[0], bf[1])
                    te.matmul(c.ps[pbk][:, 0:w],
                              q_fm[64 * h:64 * h + 64, nb * 128:(nb + 1) * 128],
                              k_fm[64 * h:64 * h + 64, lo:hi],
                              start=True, stop=True,
                              tile_position=(64 * h, 0)).then_inc(smm.h, 1)
                sp.on("tensor", smat)
                pbk2 = (nb * 2 + h + 2) % 4
                smm.n += 1
                mmg_v = smm.n
                bfg = c.ps_free[pbk2]

                def smatg(te, h=h, nb=nb, pbk2=pbk2, bfg=bfg, qd=qk_done):
                    te.wait_ge(sev.h, qd)
                    wait(te, c, bfg[0], bfg[1])
                    te.matmul(c.ps[pbk2][:, 0:16],
                              q_fm[64 * h:64 * h + 64, nb * 128:(nb + 1) * 128],
                              kg_b[64 * h:64 * h + 64, 0:16],
                              start=True, stop=True,
                              tile_position=(64 * h, 0)).then_inc(smm.h, 1)
                sp.on("tensor", smatg)
                strip = dict(
                    W=w, s1=s1_str[h], p=p_str2[h][par],
                    segments=[(pbk, 0, w, mm_v, mask_sb[:, moff:moff + w])],
                    s1_war=war.get((blk - 1, h, "exp"), 0),
                    p_war=war.get((blk - 3, h, "tr"), 0),
                    sum_war=war.get((blk - 3, h, "recip"), 0),
                )
                emit_softmax(sp, c, [strip], 1.0 / 8.0, par, stat_base=32 + 4 * h)
                strips.append(strip)
                war[(blk, h, "exp")] = strip["exp_done"]
                stripg = dict(
                    W=16, s1=s1g_str[h], p=pg_str2[h][par],
                    segments=[(pbk2, 0, 16, mmg_v, None)],
                    s1_war=war.get((blk - 1, h, "gexp"), 0),
                    p_war=war.get((blk - 3, h, "tr"), 0),
                    sum_war=war.get((blk - 3, h, "grecip"), 0),
                )
                emit_softmax(sp, c, [stripg], 1.0 / 8.0, par, stat_base=56 + 4 * h)
                stripsg.append(stripg)
                war[(blk, h, "gexp")] = stripg["exp_done"]
            if pendq[0] is not None:
                _l0b_pv(pendq[0], pend_pr)
            pendq = [pendq[1],
                     (blk, par, strips, stripsg, list(range(lo // 128, hi // 128)))]
    for pe_ in pendq:
        if pe_ is not None:
            _l0b_flush(pe_)
    a_done_ac = c.S("ac").n

    wo_v = dma(sp, c, "gpsimd", "wp",
               [(wout_b.rearrange("p (c m) -> p c m", c=FC),
                 woutT.rearrange("(c p) m -> p c m", p=128))],
               gates=[("mm", c.S("mm").n), ("ac", a_done_ac), ("ev", sev.n)])
    xor = emit_proj_residual(
        sp, c,
        lambda cc, t: a_fm[:, cc * S + t * 128: cc * S + (t + 1) * 128],
        lambda cc, e: wout_b[:, cc * 1024 + e * 512: cc * 1024 + (e + 1) * 512],
        0, x_dram, xo_dram, wo_v,
        lambda t: ("ac", a_done_ac), ln_ready, bias_gate=("aux2", bias_v))
    c.ab.pop()
    c.af.pop()
    return xor


def emit_pv_l0b(sp, c, p_str, pg_str, p_ready, pg_ready, kc_list, v_tm, vg_tm,
                heads, a_banks, pt_sl, pgt_sl, slot_war, gslot_war):
    smm, sev = c.S("mm"), c.S("ev")
    nk = len(kc_list)
    af = [c.ps_free[a_banks[0]], c.ps_free[a_banks[1]]]
    for h in (0, 1):
        pb = h % 2
        ptf = c.pt_free[pb]
        for u in range(nk):
            smm.n += 1

            def tr(te, h=h, u=u, pb=pb, pr=p_ready[h], ptf=ptf,
                   p=p_str[h], first=(u == 0)):
                if first:
                    te.wait_ge(sev.h, pr)
                    wait(te, c, ptf[0], ptf[1])
                te.transpose(c.pt[pb][:, u * 128:(u + 1) * 128],
                             p[:, u * 128:(u + 1) * 128],
                             c.ident[:, :]).then_inc(smm.h, 1)
            sp.on("tensor", tr)
        smm.n += 1
        tr_v = smm.n

        def trg(te, h=h, pb=pb, pgr=pg_ready[h], nk=nk, p=pg_str[h]):
            te.wait_ge(sev.h, pgr)
            te.transpose(c.pt[pb][0:16, nk * 128:(nk + 1) * 128],
                         p[:, 0:16], c.ident[:, :]).then_inc(smm.h, 1)
        sp.on("tensor", trg)
        slot = pt_sl[h][0]
        gslot = pgt_sl[h]
        sw = max(slot_war.get(h, 0), gslot_war.get(h, 0))
        sev.n += 1
        pt_v = sev.n
        c.pt_free[pb] = ("ev", pt_v)

        def ptevac(v, pb=pb, tr_v=tr_v, slot=slot, gslot=gslot, sw=sw, nk=nk):
            v.wait_ge(smm.h, tr_v)
            wait(v, c, "mm", sw)
            v.tensor_copy(slot[:, 0:nk * 128], c.pt[pb][:, 0:nk * 128])
            v.tensor_copy(gslot[0:16, 0:128],
                          c.pt[pb][0:16, nk * 128:(nk + 1) * 128]).then_inc(sev.h, 1)
        sp.on("vector", ptevac)
        pv_last = 0
        for u in range(nk):
            kc = kc_list[u]
            smm.n += 1
            pv_last = smm.n
            afw = af[h] if u == 0 else None

            def pv(te, h=h, kc=kc, u=u, slot=slot, pt_v=pt_v, afw=afw,
                   first=(u == 0)):
                te.wait_ge(sev.h, pt_v)
                if afw is not None:
                    wait(te, c, afw[0], afw[1])
                hh = heads[h]
                te.matmul(c.ps[a_banks[h]][64 * h:64 * h + 64, 0:128],
                          v_tm[:, kc * 1024 + 64 * hh: kc * 1024 + 64 * hh + 64],
                          slot[:, u * 128:(u + 1) * 128],
                          start=first, stop=False, skip_group_check=True,
                          tile_position=(0, 64 * h)).then_inc(smm.h, 1)
            sp.on("tensor", pv)
        smm.n += 1
        pv_last = smm.n
        slot_war[h] = pv_last
        gslot_war[h] = pv_last

        def pvg(te, h=h, gslot=gslot, pt_v=pt_v):
            te.wait_ge(sev.h, pt_v)
            hh = heads[h]
            te.matmul(c.ps[a_banks[h]][64 * h:64 * h + 64, 0:128],
                      vg_tm[0:16, 64 * hh:64 * hh + 64],
                      gslot[0:16, :],
                      start=False, stop=True, skip_group_check=True,
                      tile_position=(0, 64 * h)).then_inc(smm.h, 1)
        sp.on("tensor", pvg)
    return smm.n


def emit_attn_l1b(sp, c, wqT, bq, wkT, bk, wvT, bv, woT, bo, t_all,
                  x_dram, xo_dram, ln_ready):
    smm, sev, sac = c.S("mm"), c.S("ev"), c.S("ac")
    c.ab.push()
    c.af.push()
    v_tm = c.ab.alloc(TT * 1024)
    a_fm = c.ab.alloc(FC * S)
    q_fm = c.ab.alloc(S)
    k_fm = c.ab.alloc(S)
    _p3 = c.af.alloc(S).bitcast(BF16)
    p_str2 = [[c.ab.alloc(S), c.ab.alloc(S), _p3[:, 0:S]],
              [c.ab.alloc(S), c.ab.alloc(S), _p3[:, S:2 * S]]]
    pt_sl = [[c.ab.alloc(512), c.ab.alloc(512)], [c.ab.alloc(512), c.ab.alloc(512)]]
    s1_str = [c.af.alloc(S), c.af.alloc(S)]

    bias_v = emit_bias_loads(sp, c, [(bq, 0, 1024), (bk, 1024, 1024),
                                     (bv, 2048, 1024), (bo, 3072, 1024)])

    c.ab.push()
    wv_b = c.ab.alloc(FC * 512)
    for e in range(2):
        wv_v = dma(sp, c, "gpsimd", "wd",
                   [(wv_b.rearrange("p (c m) -> p c m", c=FC),
                     wvT[:, e * 512:(e + 1) * 512]
                     .rearrange("(c p) m -> p c m", p=128))],
                   gates=[("mm", c._wv_war)] if c._wv_war else [])
        for kc in range(TT):
            pbk = kc % 4
            mm_v = emit_fm_group(
                sp, c, pbk,
                [c.ln_fm[:, cc * S + kc * 128: cc * S + (kc + 1) * 128]
                 for cc in range(FC)],
                [wv_b[:, cc * 512:(cc + 1) * 512] for cc in range(FC)],
                bias_lhs=c.ones_row[0:1, 0:128],
                bias_rhs=c.bias_row[0:1, 2048 + e * 512: 2048 + (e + 1) * 512],
                waits=[("wd", wv_v), ("ev", ln_ready), ("aux2", bias_v)])
            emit_evac_copy(sp, c, pbk, 512,
                           v_tm[:, kc * 1024 + e * 512: kc * 1024 + (e + 1) * 512],
                           mm_v)
        c._wv_war = c.S("mm").n
    v_done = sev.n
    v_mm_done = c.S("mm").n
    c.ab.pop()
    c.ab.push()
    t_sb = [c.ab.alloc(3968), c.ab.alloc(3968)]
    wqk_b2 = [c.ab.alloc(FC * 256)]
    wqk_b2 = [wqk_b2[0], wqk_b2[0]]

    war = {}
    pt_slot_war = {}
    t_war = 0
    seq = 0

    def _l1b_fin(pend):
        blk, par, strips = pend
        pr = finish_softmax(sp, c, strips)
        for h in (0, 1):
            war[(blk, h, "recip")] = strips[h]["recip_done"]
        return pr

    def _l1b_pv(pend, p_ready):
        blk, par, strips = pend
        p = blk // TT
        nb = blk % TT
        pv_v = emit_pv2(sp, c, [p_str2[0][par], p_str2[1][par]], p_ready,
                        list(range(TT)), v_tm, (2 * p, 2 * p + 1), (4, 5), 128,
                        pt_sl, pt_slot_war)
        for h in (0, 1):
            war[(blk, h, "tr")] = pv_v
        a_evac2(sp, c, (4, 5), 128,
                lambda h: a_fm[64 * h:64 * h + 64,
                               p * S + nb * 128:p * S + (nb + 1) * 128], pv_v)

    def _l1b_flush_pv(pend):
        _l1b_pv(pend, _l1b_fin(pend))

    for p in range(8):
        slot = p % 2
        wqk_b = wqk_b2[slot]
        prev_scores_done = c.S("mm").n
        wqk_v = dma(sp, c, "gpsimd", f"w{slot}", [
            (wqk_b[:, 0:FC * 128].rearrange("p (c m) -> p c m", c=FC),
             wqT[:, 128 * p:128 * (p + 1)].rearrange("(c p2) m -> p2 c m", p2=128)),
            (wqk_b[:, FC * 128:FC * 256].rearrange("p (c m) -> p c m", c=FC),
             wkT[:, 128 * p:128 * (p + 1)].rearrange("(c p2) m -> p2 c m", p2=128)),
        ], gates=[("mm", c._wqk_war.get(0, v_mm_done))])
        t_v = dma(sp, c, "gpsimd", "tld",
                  [(t_sb[0], t_all[2 * p][:, 127:4095]),
                   (t_sb[1], t_all[2 * p + 1][:, 127:4095])],
                  gates=[("ev", t_war)] if t_war else [])
        qk_first = True
        for which, dst, boff in ((0, q_fm, 0), (1, k_fm, 1024)):
            for n in range(4):
                pbk = seq % 4
                seq += 1
                mm_v = emit_fm_group(
                    sp, c, pbk,
                    [wqk_b[:, which * FC * 128 + cc * 128: which * FC * 128 + (cc + 1) * 128]
                     for cc in range(FC)],
                    [c.ln_fm[:, cc * S + n * 512: cc * S + (n + 1) * 512]
                     for cc in range(FC)],
                    bias_lhs=c.bias_row[0:1, boff + 128 * p: boff + 128 * p + 128],
                    bias_rhs=c.ones_row[0:1, 0:512],
                    waits=[(f"w{slot}", wqk_v), ("ev", ln_ready), ("aux2", bias_v)])
                if qk_first:
                    sev.n += 1
                    vv = sev.n

                    def evq(v, mm_v=mm_v, dst=dst, n=n, psd=prev_scores_done, pbk=pbk):
                        v.wait_ge(c.sem["mm"].h, mm_v)
                        wait(v, c, "mm", psd)
                        v.tensor_copy(dst[:, n * 512:(n + 1) * 512],
                                      c.ps[pbk][:, :]).then_inc(sev.h, 1)
                    sp.on("vector", evq)
                    c.ps_free[pbk] = ("ev", vv)
                    qk_first = False
                else:
                    emit_evac_copy(sp, c, pbk, 512, dst[:, n * 512:(n + 1) * 512], mm_v)
        qk_done = sev.n
        c._wqk_war[0] = c.S("mm").n

        if p == 0:
            pendq = [None, None]
        for nb in range(TT):
            blk = p * TT + nb
            par = blk % 3
            strips = []
            pend_pr = _l1b_fin(pendq[0]) if pendq[0] is not None else None
            seg_done = {0: 0, 1: 0}
            for kt in range(4):
                mmvs = {}
                for h in (0, 1):
                    pbk = 2 * h + (kt % 2)
                    smm.n += 1
                    mm_v = smm.n
                    mmvs[h] = (pbk, mm_v)
                    bf = c.ps_free[pbk]

                    def smat(te, h=h, nb=nb, kt=kt, pbk=pbk, bf=bf,
                             qd=qk_done, vd=v_done, tv=t_v):
                        te.wait_ge(sev.h, max(qd, vd))
                        te.wait_ge(c.sem["tld"].h, tv)
                        wait(te, c, bf[0], bf[1])
                        te.matmul(c.ps[pbk][:, 0:512],
                                  q_fm[64 * h:64 * h + 64, nb * 128:(nb + 1) * 128],
                                  k_fm[64 * h:64 * h + 64, kt * 512:(kt + 1) * 512],
                                  start=True, stop=True,
                                  tile_position=(64 * h, 0)).then_inc(smm.h, 1)
                    sp.on("tensor", smat)
                for h in (0, 1):
                    pbk, mm_v = mmvs[h]
                    toff = 2047 - 127 - nb * 128 + kt * 512
                    bias_ap = t_sb[h][:, toff:toff + 512]
                    sev.n += 1
                    ev_v = sev.n
                    c.ps_free[pbk] = ("ev", ev_v)
                    s1w = war.get((blk - 1, h, "exp"), 0)

                    def evs(v, pbk=pbk, mm_v=mm_v, kt=kt, bias_ap=bias_ap,
                            h=h, s1w=s1w):
                        v.wait_ge(c.sem["mm"].h, mm_v)
                        wait(v, c, "ac", s1w)
                        v.scalar_tensor_tensor(
                            s1_str[h][:, kt * 512:(kt + 1) * 512],
                            c.ps[pbk][:, 0:512], 1.0 / 8.0, bias_ap,
                            ALU.mult, ALU.add).then_inc(sev.h, 1)
                    sp.on("vector", evs)
                    seg_done[h] = ev_v
            for h in (0, 1):
                strip = dict(
                    W=S, s1=s1_str[h], p=p_str2[h][par], segments=[],
                    pre_done=seg_done[h],
                    p_war=war.get((blk - 3, h, "tr"), 0),
                    sum_war=war.get((blk - 3, h, "recip"), 0),
                )
                emit_softmax(sp, c, [strip], 1.0 / 8.0, par, stat_base=32 + 4 * h)
                strips.append(strip)
                war[(blk, h, "exp")] = strip["exp_done"]
            if pendq[0] is not None:
                _l1b_pv(pendq[0], pend_pr)
            pendq = [pendq[1], (blk, par, strips)]
        if p == 7:
            for pe_ in pendq:
                if pe_ is not None:
                    _l1b_flush_pv(pe_)
        t_war = sev.n
    a_done_ac = c.S("ac").n

    c.ab.pop()
    wout_b = c.ab.alloc(FC * 1024)
    wo_v = dma(sp, c, "gpsimd", "wp",
               [(wout_b.rearrange("p (c m) -> p c m", c=FC),
                 woT.rearrange("(c p) m -> p c m", p=128))],
               gates=[("mm", c.S("mm").n), ("ac", a_done_ac), ("ev", sev.n)])
    xor = emit_proj_residual(
        sp, c,
        lambda cc, t: a_fm[:, cc * S + t * 128: cc * S + (t + 1) * 128],
        lambda cc, e: wout_b[:, cc * 1024 + e * 512: cc * 1024 + (e + 1) * 512],
        3072, x_dram, xo_dram, wo_v,
        lambda t: ("ac", a_done_ac), ln_ready, bias_gate=("aux2", bias_v))
    c.ab.pop()
    c.af.pop()
    return xor



def emit_attn_l1a(sp, c, wq, bq, wk, wv, wproj, bproj, wr, br, bnd_rev, e_scr,
                  x_dram, xo_dram, ln_ready):
    smm, sev, sac = c.S("mm"), c.S("ev"), c.S("ac")
    c.ab.push()
    c.af.push()
    v_tm = c.ab.alloc(TT * 1024)
    a_fm = c.ab.alloc(FC * S)
    c.ab.push()
    q_fm = c.ab.alloc(S)
    k_fm = c.ab.alloc(S)
    _p3 = c.af.alloc(1280).bitcast(BF16)
    p_str2 = [[c.ab.alloc(1280), c.ab.alloc(1280), _p3[:, 0:1280]],
              [c.ab.alloc(1280), c.ab.alloc(1280), _p3[:, 1280:2560]]]
    pt_sl = [[c.ab.alloc(512), c.ab.alloc(512)], [c.ab.alloc(512), c.ab.alloc(512)]]
    r_fm_a = c.ab.alloc(S)
    r_fm_b = c.ab.alloc(S)
    r_head = [c.ab.alloc(S), c.ab.alloc(S)]
    bnd_sb = c.ab.alloc(1024)
    s1_str = [c.af.alloc(1152), c.af.alloc(1152)]
    _es = c.af.alloc(2304).bitcast(BF16)
    es_sb2 = [[_es[:, 0:1152], _es[:, 1152:2304]],
              [_es[:, 2304:3456], _es[:, 3456:4608]]]
    e_sb = [c.af.alloc(1024)]
    e_sb = [e_sb[0], e_sb[0]]
    neg_sb = c.af.alloc(256)

    bias_v = emit_bias_loads(sp, c, [(bq, 0, 1024), (br, 1024, 160),
                                     (bproj, 2048, 1024)])
    bnd_v = dma(sp, c, "gpsimd", "aux2", [(bnd_sb[0:10, :], bnd_rev)])
    bias_v = bnd_v

    sev.n += 1
    neg_v = sev.n

    def negset(v):
        v.memset(neg_sb, NEG / 2.0).then_inc(sev.h, 1)
    sp.on("vector", negset)
    marg = []
    for b in range(4):
        marg.append((e_scr[b][:, 0:128], neg_sb[:, 0:128]))
        marg.append((e_scr[b][:, 1152:1408], neg_sb[:, 0:256]))
        marg.append((e_scr[b][:, 1408:1664], neg_sb[:, 0:256]))
    marg_v = dma(sp, c, "sync", "aux", marg, gates=[("ev", neg_v)])

    c.ab.push()
    wv_b = c.ab.alloc(FC * 256)
    for e in range(4):
        wv_v = dma(sp, c, "gpsimd", "wd",
                   [(wv_b.rearrange("p (c m) -> p c m", c=FC),
                     wv[:, e * 256:(e + 1) * 256]
                     .rearrange("(c p) m -> p c m", p=128))],
                   gates=[("mm", c._wv_war)] if c._wv_war else [])
        for kc in range(TT):
            pbk = kc % 4
            mm_v = emit_fm_group(
                sp, c, pbk,
                [c.ln_fm[:, cc * S + kc * 128: cc * S + (kc + 1) * 128]
                 for cc in range(FC)],
                [wv_b[:, cc * 256:(cc + 1) * 256] for cc in range(FC)],
                waits=[("wd", wv_v), ("ev", ln_ready)])
            emit_evac_copy(sp, c, pbk, 256,
                           v_tm[:, kc * 1024 + e * 256: kc * 1024 + (e + 1) * 256],
                           mm_v)
        c._wv_war = c.S("mm").n
    wr_v = dma(sp, c, "gpsimd", "wd",
               [(wv_b[:, 0:FC * 160].rearrange("p (c m) -> p c m", c=FC),
                 wr.rearrange("(c p) m -> p c m", p=128))],
               gates=[("mm", c._wv_war)])
    for which, dst, mw, boff in ((0, r_fm_a, 128, 1024), (1, r_fm_b, 32, 1152)):
        for n in range(4):
            pbk = n % 4
            mm_v = emit_fm_group(
                sp, c, pbk,
                [wv_b[:, cc * 160 + which * 128: cc * 160 + which * 128 + mw]
                 for cc in range(FC)],
                [c.ln_fm[:, cc * S + n * 512: cc * S + (n + 1) * 512]
                 for cc in range(FC)],
                bias_lhs=c.bias_row[0:1, boff:boff + mw],
                bias_rhs=c.ones_row[0:1, 0:512],
                waits=[("wd", wr_v), ("ev", ln_ready), ("aux2", max(bias_v, bnd_v))])
            sev.n += 1
            vv = sev.n

            def evr(v, mm_v=mm_v, dst=dst, n=n, pbk=pbk, mw=mw):
                v.wait_ge(c.sem["mm"].h, mm_v)
                v.tensor_copy(dst[0:mw, n * 512:(n + 1) * 512],
                              c.ps[pbk][0:mw, :]).then_inc(sev.h, 1)
            sp.on("vector", evr)
            c.ps_free[pbk] = ("ev", vv)
    r_done = sev.n
    v_done = sev.n
    v_mm_done = c.S("mm").n
    c.ab.pop()
    wqk_b2 = [c.ab.alloc(FC * 256), c.ab.alloc(FC * 256)]

    war = {}
    pt_slot_war = {}
    e_scr_war = [0, 0, 0, 0]
    seq = 0
    eb = 0

    def _l1a_fin(pend):
        blk, par, strips, kc_list = pend
        pr = finish_softmax(sp, c, strips)
        for h in (0, 1):
            war[(blk, h, "recip")] = strips[h]["recip_done"]
        return pr

    def _l1a_pv(pend, p_ready):
        blk, par, strips, kc_list = pend
        pp = blk // TT
        nb = blk % TT
        pv_v = emit_pv2(sp, c, [p_str2[0][par], p_str2[1][par]], p_ready,
                        kc_list, v_tm, (2 * pp, 2 * pp + 1), (4, 5), 128,
                        pt_sl, pt_slot_war)
        for h in (0, 1):
            war[(blk, h, "tr")] = pv_v
        a_evac2(sp, c, (4, 5), 128,
                lambda h, pp=pp, nb=nb: a_fm[64 * h:64 * h + 64,
                                             pp * S + nb * 128:pp * S + (nb + 1) * 128],
                pv_v)

    def _l1a_flush(pend):
        _l1a_pv(pend, _l1a_fin(pend))

    pendq = [None, None]
    for p in range(8):
        slot = p % 2
        wqk_b = wqk_b2[slot]
        prev_scores_done = c.S("mm").n
        wqk_v = dma(sp, c, "gpsimd", f"w{slot}", [
            (wqk_b[:, 0:FC * 128].rearrange("p (c m) -> p c m", c=FC),
             wq[:, 128 * p:128 * (p + 1)].rearrange("(c p2) m -> p2 c m", p2=128)),
            (wqk_b[:, FC * 128:FC * 256].rearrange("p (c m) -> p c m", c=FC),
             wk[:, 128 * p:128 * (p + 1)].rearrange("(c p2) m -> p2 c m", p2=128)),
        ], gates=[("mm", c._wqk_war.get(slot, v_mm_done))])
        rrep_pairs = []
        for h in (0, 1):
            hh = 2 * p + h
            lo = 10 * hh
            if lo + 10 <= 128:
                rrep_pairs.append((r_head[h][0:10, :], r_fm_a[lo:lo + 10, :]))
            elif lo < 128:
                k1 = 128 - lo
                rrep_pairs.append((r_head[h][0:k1, :], r_fm_a[lo:128, :]))
                rrep_pairs.append((r_head[h][k1:10, :], r_fm_b[0:10 - k1, :]))
            else:
                rrep_pairs.append((r_head[h][0:10, :], r_fm_b[lo - 128:lo - 118, :]))
        rr_v = dma(sp, c, "sync", "rrep", rrep_pairs,
                   gates=[("ev", r_done), ("mm", war.get(("rrep", p - 2), 0))])
        war[("rrep", p)] = 0
        qk_first = True
        for which, dst, boff in ((0, q_fm, 0), (1, k_fm, None)):
            for n in range(4):
                pbk = seq % 4
                seq += 1
                mm_v = emit_fm_group(
                    sp, c, pbk,
                    [wqk_b[:, which * FC * 128 + cc * 128: which * FC * 128 + (cc + 1) * 128]
                     for cc in range(FC)],
                    [c.ln_fm[:, cc * S + n * 512: cc * S + (n + 1) * 512]
                     for cc in range(FC)],
                    bias_lhs=(c.bias_row[0:1, boff + 128 * p: boff + 128 * p + 128]
                              if boff is not None else None),
                    bias_rhs=(c.ones_row[0:1, 0:512] if boff is not None else None),
                    waits=[(f"w{slot}", wqk_v), ("ev", ln_ready), ("aux2", bias_v)])
                if qk_first:
                    sev.n += 1
                    vv = sev.n

                    def evq(v, mm_v=mm_v, dst=dst, n=n, psd=prev_scores_done, pbk=pbk):
                        v.wait_ge(c.sem["mm"].h, mm_v)
                        wait(v, c, "mm", psd)
                        v.tensor_copy(dst[:, n * 512:(n + 1) * 512],
                                      c.ps[pbk][:, :]).then_inc(sev.h, 1)
                    sp.on("vector", evq)
                    c.ps_free[pbk] = ("ev", vv)
                    qk_first = False
                else:
                    emit_evac_copy(sp, c, pbk, 512, dst[:, n * 512:(n + 1) * 512], mm_v)
        qk_done = sev.n
        c._wqk_war[slot] = c.S("mm").n
        ew_prev = {0: 0, 1: 0}
        sk_prev = {}

        def emit_E(h, nb):
            nonlocal eb
            t0 = nb * 128
            p_start = max(0, t0 - 1024)
            blk = p * TT + nb
            ebuf = eb % 4
            eb += 1
            sk_war = e_scr_war[ebuf]
            for e2 in range(2):
                pbk = c._l1a_seq % 4
                c._l1a_seq += 1
                mm_v = emit_fm_group(
                    sp, c, pbk,
                    [r_head[h][0:10, t0:t0 + 128]],
                    [bnd_sb[0:10, e2 * 512:(e2 + 1) * 512]],
                    waits=[("rrep", rr_v), ("aux2", bnd_v)])
                sev.n += 1
                vv = sev.n
                ewp = ew_prev[h]

                def eve(v, mm_v=mm_v, e2=e2, pbk=pbk, h=h, ewp=ewp):
                    v.wait_ge(c.sem["mm"].h, mm_v)
                    wait(v, c, "ewr", ewp)
                    v.tensor_copy(e_sb[h][:, e2 * 512:(e2 + 1) * 512],
                                  c.ps[pbk][:, :]).then_inc(sev.h, 1)
                sp.on("vector", eve)
                c.ps_free[pbk] = ("ev", vv)
            e_ev = sev.n
            ew_v = dma(sp, c, "sync", "ewr",
                       [(e_scr[ebuf][:, 128:1152], e_sb[h])],
                       gates=[("ev", e_ev), ("sk", sk_war)])
            ew_prev[0] = ew_prev[1] = ew_v
            off0 = 128 + 1023 - t0 + p_start
            Wt = t0 + 128 - p_start
            skew_ap = AP(e_scr[ebuf].tensor,
                         e_scr[ebuf].offset + off0, [[1663, 128], [1, Wt]])
            es_war = war.get((blk - 2, h, "segev"), 0)
            sk_chain = c.S("sk").n
            sk_v = dma(sp, c, "gpsimd", "sk",
                       [(es_sb2[h][nb % 2][:, 0:Wt], skew_ap)],
                       gates=[("ewr", ew_v), ("aux", marg_v), ("ev", es_war),
                              ("sk", sk_chain)])
            e_scr_war[ebuf] = sk_v
            sk_prev[(h, nb)] = sk_v

        c._l1a_seq = seq
        for h in (0, 1):
            emit_E(h, 0)

        for nb in range(TT):
            blk = p * TT + nb
            t0 = nb * 128
            p_start = max(0, t0 - 1024)
            p_end = t0 + 128
            Wt = p_end - p_start
            par = blk % 3
            strips = []
            pend_pr = _l1a_fin(pendq[0]) if pendq[0] is not None else None
            for h in (0, 1):
                if nb < TT - 1:
                    emit_E(h, nb + 1)
                sk_v = sk_prev[(h, nb)]
                segs = []
                nseg = cdiv(Wt, 512)
                for si in range(nseg):
                    soff = si * 512
                    wseg = min(512, Wt - soff)
                    pbk = c._l1a_seq % 4
                    c._l1a_seq += 1
                    smm.n += 1
                    mm_v = smm.n
                    bf = c.ps_free[pbk]

                    def smat(te, h=h, nb=nb, soff=soff, wseg=wseg, pbk=pbk, bf=bf,
                             qd=qk_done, vd=v_done, ps0=p_start):
                        te.wait_ge(sev.h, max(qd, vd))
                        wait(te, c, bf[0], bf[1])
                        te.matmul(c.ps[pbk][:, 0:wseg],
                                  q_fm[64 * h:64 * h + 64, nb * 128:(nb + 1) * 128],
                                  k_fm[64 * h:64 * h + 64,
                                       ps0 + soff: ps0 + soff + wseg],
                                  start=True, stop=True,
                                  tile_position=(64 * h, 0)).then_inc(smm.h, 1)
                    sp.on("tensor", smat)
                    segs.append((pbk, soff, wseg, mm_v,
                                 es_sb2[h][nb % 2][:, soff:soff + wseg]))
                strip = dict(
                    W=Wt, s1=s1_str[h], p=p_str2[h][par], segments=segs,
                    s1_war=war.get((blk - 1, h, "exp"), 0),
                    p_war=war.get((blk - 3, h, "tr"), 0),
                    sum_war=war.get((blk - 3, h, "recip"), 0),
                    seg_gate=("sk", sk_v),
                )
                emit_softmax(sp, c, [strip], 1.0 / 8.0, par,
                             stat_base=32 + 4 * h)
                strips.append(strip)
                war[(blk, h, "segev")] = strip["seg_evac_done"]
            for h in (0, 1):
                war[(blk, h, "exp")] = strips[h]["exp_done"]
            if pendq[0] is not None:
                _l1a_pv(pendq[0], pend_pr)
            pendq = [pendq[1],
                     (blk, par, strips, list(range(p_start // 128, p_end // 128)))]
        war[("rrep", p)] = c.S("mm").n
    for pe_ in pendq:
        if pe_ is not None:
            _l1a_flush(pe_)
    a_done_ac = c.S("ac").n

    c.ab.pop()
    wout_b = c.ab.alloc(FC * 1024)
    wo_v = dma(sp, c, "gpsimd", "wp",
               [(wout_b.rearrange("p (c m) -> p c m", c=FC),
                 wproj.rearrange("(c p) m -> p c m", p=128))],
               gates=[("mm", c.S("mm").n), ("ac", a_done_ac), ("ev", sev.n)])
    xor = emit_proj_residual(
        sp, c,
        lambda cc, t: a_fm[:, cc * S + t * 128: cc * S + (t + 1) * 128],
        lambda cc, e: wout_b[:, cc * 1024 + e * 512: cc * 1024 + (e + 1) * 512],
        2048, x_dram, xo_dram, wo_v,
        lambda t: ("ac", a_done_ac), ln_ready, bias_gate=("aux2", bias_v))
    c.ab.pop()
    c.af.pop()
    return xor



def emit_ffn_l1(sp, c, w1T, w2T, h1T, h2T, w3T, s1d, s2d, h1sd, h2sd, w3sd,
                ident_dram, x_dram, xo_dram, ln_ready, prep_gate=None):
    drain_bg(sp, c, 10 ** 6)
    smm, sev, sac, vch = c.S("mm"), c.S("ev"), c.S("ac"), c.S("vch")
    HB = 32
    c.ab.push()
    c.af.push()
    su_str = c.ab.alloc(HB * 512)
    h_str = c.ab.alloc(HB * 512)
    wb1 = [c.ab.alloc(FC * 128) for _ in range(2)]
    wb2 = [c.ab.alloc(FC * 128) for _ in range(2)]
    hb = [c.ab.alloc(HB * 128) for _ in range(2)]
    w3b = c.ab.alloc(HB * 256)
    ident32 = c.af.alloc(128)
    s_cols = c.af.alloc(160)
    sc_rows = c.af.alloc(128)
    s3_bc = c.af.alloc(1024)
    s3row = c.af.alloc(1024)
    onef = c.af.alloc(128)
    y_blk = c.af.alloc(4096)

    id32_v = dma(sp, c, "gpsimd", "aux2", [(ident32, ident_dram)])
    last_tr = 0
    sc_all = 0
    for i, vec in enumerate((s1d, s2d, h1sd, h2sd)):
        ld_v = dma(sp, c, "sync", "aux",
                   [(sc_rows[0:32, :], vec.rearrange("(m p) -> m p", p=128))],
                   gates=[("mm", last_tr)] if last_tr else [])
        bf = c.ps_free[5]
        smm.n += 1
        tr_v = smm.n
        last_tr = tr_v

        def trsc(te, ld_v=ld_v, id32_v=id32_v, bf=bf):
            te.wait_ge(c.sem["aux"].h, ld_v)
            te.wait_ge(c.sem["aux2"].h, id32_v)
            wait(te, c, bf[0], bf[1])
            te.transpose(c.ps[5][:, 0:32], sc_rows[0:32, :],
                         ident32[0:32, 0:32]).then_inc(smm.h, 1)
        sp.on("tensor", trsc)
        sev.n += 1
        ev_v = sev.n

        def evsc(v, tr_v=tr_v, i=i):
            v.wait_ge(smm.h, tr_v)
            v.tensor_copy(s_cols[:, i * 32:(i + 1) * 32],
                          c.ps[5][:, 0:32]).then_inc(sev.h, 1)
        sp.on("vector", evsc)
        c.ps_free[5] = ("ev", ev_v)
        sc_all = ev_v
    s3_v = dma(sp, c, "sync", "aux", [(s3row[0:1, :], w3sd[None, :])],
               gates=[("mm", last_tr)])
    sev.n += 1
    onef_v = sev.n

    def onesf(v):
        v.memset(onef[0:1, 0:128], 1.0).then_inc(sev.h, 1)
    sp.on("vector", onesf)
    s3bc_done = 0
    for e in range(2):
        bf = c.ps_free[4]
        smm.n += 1
        mm_v = smm.n

        def s3mm(te, e=e, s3_v=s3_v, onef_v=onef_v, bf=bf):
            te.wait_ge(c.sem["aux"].h, s3_v)
            te.wait_ge(sev.h, onef_v)
            wait(te, c, bf[0], bf[1])
            te.matmul(c.ps[4][0:128, 0:512], onef[0:1, 0:128],
                      s3row[0:1, e * 512:(e + 1) * 512],
                      start=True, stop=True).then_inc(smm.h, 1)
        sp.on("tensor", s3mm)
        sev.n += 1
        vv = sev.n

        def s3ev(v, mm_v=mm_v, e=e):
            v.wait_ge(smm.h, mm_v)
            v.tensor_copy(s3_bc[:, e * 512:(e + 1) * 512],
                          c.ps[4][:, 0:512]).then_inc(sev.h, 1)
        sp.on("vector", s3ev)
        c.ps_free[4] = ("ev", vv)
        s3bc_done = vv
        prep_done = max(sc_all, s3bc_done)

    war = {}
    mm_slot = {"w1": [0, 0], "h": [0, 0]}
    x_out_ready = [None] * TT
    for n in range(4):
        tok = slice(n * 512, (n + 1) * 512)
        for m in range(HB):
            dbl = m % 2
            g0 = [("mm", mm_slot["w1"][dbl])] if mm_slot["w1"][dbl] else []
            if prep_gate and n == 0 and m == dbl:
                g0 = g0 + [prep_gate]
            lw = dma(sp, c, "sync", f"sw{dbl}", [
                (wb1[dbl].rearrange("p (c m) -> p c m", c=FC),
                 w1T[:, m * 128:(m + 1) * 128].rearrange("(c p) m -> p c m", p=128)),
                (wb2[dbl].rearrange("p (c m) -> p c m", c=FC),
                 w2T[:, m * 128:(m + 1) * 128].rearrange("(c p) m -> p c m", p=128)),
            ], gates=g0)
            pg, pu = 2 * dbl, 2 * dbl + 1
            gf, uf = c.ps_free[pg], c.ps_free[pu]
            smm.n += 2
            mm_v = smm.n
            mm_slot["w1"][dbl] = mm_v

            def mmu(te, dbl=dbl, lw=lw, tok=tok, pg=pg, pu=pu, gf=gf, uf=uf,
                    lr=ln_ready, pd=prep_done):
                te.wait_ge(c.sem[f"sw{dbl}"].h, lw)
                te.wait_ge(sev.h, max(lr, pd))
                wait(te, c, gf[0], gf[1])
                wait(te, c, uf[0], uf[1])
                for cc in range(FC):
                    ins = te.matmul(c.ps[pg][:, :], wb1[dbl][:, cc * 128:(cc + 1) * 128],
                                    c.ln_fm[:, cc * S:(cc + 1) * S][:, tok],
                                    start=(cc == 0), stop=(cc == FC - 1))
                    if cc == FC - 1:
                        ins.then_inc(smm.h, 1)
                for cc in range(FC):
                    ins = te.matmul(c.ps[pu][:, :], wb2[dbl][:, cc * 128:(cc + 1) * 128],
                                    c.ln_fm[:, cc * S:(cc + 1) * S][:, tok],
                                    start=(cc == 0), stop=(cc == FC - 1))
                    if cc == FC - 1:
                        ins.then_inc(smm.h, 1)
            sp.on("tensor", mmu)

            sac.n += 1
            sa = sac.n
            su_war = war.get(("su", n - 1), 0) if m < 2 else 0

            def silu(a, pg=pg, mm_v=mm_v, m=m, sw=su_war):
                a.wait_ge(smm.h, mm_v - 1)
                wait(a, c, "ev", sw)
                a.activation(su_str[:, m * 512:(m + 1) * 512], c.ps[pg][:, :],
                             AF.Silu, scale=s_cols[:, m:m + 1]).then_inc(sac.h, 1)
            sp.on("scalar", silu)
            c.ps_free[pg] = ("ac", sa)

            sev.n += 1
            hv = sev.n
            h_war = war.get(("h", n - 1), 0) if m < 2 else 0

            def hmul(v, m=m, pu=pu, sa=sa, mm_v=mm_v, hw=h_war):
                v.wait_ge(sac.h, sa)
                v.wait_ge(smm.h, mm_v)
                wait(v, c, "mm", hw)
                v.scalar_tensor_tensor(h_str[:, m * 512:(m + 1) * 512],
                                       c.ps[pu][:, :], s_cols[:, 32 + m:33 + m],
                                       su_str[:, m * 512:(m + 1) * 512],
                                       ALU.mult, ALU.mult).then_inc(sev.h, 1)
            sp.on("vector", hmul)
            c.ps_free[pu] = ("ev", hv)
        h_done = sev.n
        war[("su", n)] = sev.n

        def hidden_layer(src, dst, wT, scol_off, h_ready, wslot_key, wait_extra):
            g_done_local = 0
            for m2 in range(HB):
                dbl = m2 % 2
                lw = dma(sp, c, "sync", f"sh{dbl}", [
                    (hb[dbl].rearrange("p (c m) -> p c m", c=HB),
                     wT[:, m2 * 128:(m2 + 1) * 128]
                     .rearrange("(c p) m -> p c m", p=128)),
                ], gates=[("mm", mm_slot["h"][dbl])] if mm_slot["h"][dbl] else [])
                pbk = 2 * dbl
                bf = c.ps_free[pbk]
                smm.n += 1
                mm_v = smm.n
                mm_slot["h"][dbl] = mm_v

                def mmh(te, dbl=dbl, lw=lw, pbk=pbk, bf=bf, hr=h_ready, src=src,
                        we=wait_extra):
                    te.wait_ge(c.sem[f"sh{dbl}"].h, lw)
                    te.wait_ge(sev.h, hr)
                    wait(te, c, "ac", we)
                    wait(te, c, bf[0], bf[1])
                    for k in range(HB):
                        ins = te.matmul(c.ps[pbk][:, :],
                                        hb[dbl][:, k * 128:(k + 1) * 128],
                                        src[:, k * 512:(k + 1) * 512],
                                        start=(k == 0), stop=(k == HB - 1))
                    ins.then_inc(smm.h, 1)
                sp.on("tensor", mmh)

                sac.n += 1
                sa = sac.n

                def gelu(a, pbk=pbk, mm_v=mm_v, m2=m2, dst=dst, so=scol_off):
                    a.wait_ge(smm.h, mm_v)
                    a.activation(dst[:, m2 * 512:(m2 + 1) * 512], c.ps[pbk][:, :],
                                 AF.Gelu, scale=s_cols[:, so + m2:so + m2 + 1])\
                     .then_inc(sac.h, 1)
                sp.on("scalar", gelu)
                c.ps_free[pbk] = ("ac", sa)
                g_done_local = sa
            return g_done_local

        g1_done = hidden_layer(h_str, su_str, h1T, 64, h_done, "h", 0)
        sev.n += 0
        g2_done = hidden_layer(su_str, h_str, h2T, 96, h_done, "h", g1_done)
        war[("h", n)] = smm.n

        yev_done = 0
        for e4 in range(4):
            w3_v = dma(sp, c, "sync", "sw3", [
                (w3b.rearrange("p (c m) -> p c m", c=HB),
                 w3T[:, e4 * 256:(e4 + 1) * 256].rearrange("(c p) m -> p c m", p=128)),
            ], gates=[("mm", war.get(("w3", n, e4 - 1), 0))] if e4 else
                     ([("mm", war.get(("w3", n - 1, 3), 0))] if n else []))
            for tt_i in range(4):
                pbk = 4 + tt_i % 2
                bf = c.ps_free[pbk]
                smm.n += 1
                mm_v = smm.n

                def mmy(te, tt_i=tt_i, pbk=pbk, bf=bf, w3_v=w3_v, g2d=g2_done):
                    te.wait_ge(c.sem["sw3"].h, w3_v)
                    te.wait_ge(sac.h, g2d)
                    wait(te, c, bf[0], bf[1])
                    for k in range(HB):
                        ins = te.matmul(c.ps[pbk][:, 0:256],
                                        h_str[:, k * 512 + tt_i * 128: k * 512 + (tt_i + 1) * 128],
                                        w3b[:, k * 256:(k + 1) * 256],
                                        start=(k == 0), stop=(k == HB - 1))
                    ins.then_inc(smm.h, 1)
                sp.on("tensor", mmy)
                war[("w3", n, e4)] = smm.n

                sev.n += 1
                yv = sev.n
                ybw = war.get(("yblk", n - 1), 0)

                def yev(v, tt_i=tt_i, e4=e4, pbk=pbk, mm_v=mm_v, ybw=ybw):
                    v.wait_ge(smm.h, mm_v)
                    wait(v, c, "ev", ybw)
                    v.scalar_tensor_tensor(
                        y_blk[:, tt_i * 1024 + e4 * 256: tt_i * 1024 + (e4 + 1) * 256],
                        c.ps[pbk][:, 0:256], 1.0,
                        s3_bc[:, e4 * 256:(e4 + 1) * 256],
                        ALU.mult, ALU.mult).then_inc(sev.h, 1)
                sp.on("vector", yev)
                c.ps_free[pbk] = ("ev", yv)
                yev_done = yv

        for tt_i in range(4):
            t = n * 4 + tt_i
            buf = t % 2
            xsl = c.xbuf[:, buf * D:(buf + 1) * D]
            xnsl = c.xnew[:, buf * D:(buf + 1) * D]
            gates = [("ev", ln_ready)]
            if t >= 2:
                gates.append(x_out_ready[t - 2])
            xv = dma(sp, c, "sync", f"xl{buf}",
                     [(xsl, x_dram[t * 128:(t + 1) * 128, :])], gates=gates)
            sev.n += 1
            rv = sev.n

            def resid(v, tt_i=tt_i, xv=xv, xsl=xsl, xnsl=xnsl, buf=buf, yd=yev_done):
                v.wait_ge(c.sem[f"xl{buf}"].h, xv)
                v.wait_ge(sev.h, yd)
                v.tensor_add(xnsl, xsl,
                             y_blk[:, tt_i * 1024:(tt_i + 1) * 1024]).then_inc(sev.h, 1)
            sp.on("vector", resid)
            x_out_ready[t] = emit_store_tile(sp, c, xo_dram, t, xnsl, rv)
        war[("yblk", n)] = sev.n
    c.ab.pop()
    c.af.pop()
    return x_out_ready


def finalize(sp, c, x_out_ready):
    tot = {}
    for g in x_out_ready:
        if g is not None:
            tot[g[0]] = max(tot.get(g[0], 0), g[1])

    def fin(g):
        for k, v in tot.items():
            g.wait_ge(c.sem[k].h, v)
    sp.on("gpsimd", fin)


def build_block(nc, sp):
    with nc.Block() as block:
        @block.sync
        def _(eng):
            for fn in sp.streams["sync"]:
                fn(eng)

        @block.tensor
        def _(eng):
            for fn in sp.streams["tensor"]:
                fn(eng)

        @block.vector
        def _(eng):
            for fn in sp.streams["vector"]:
                fn(eng)

        @block.scalar
        def _(eng):
            for fn in sp.streams["scalar"]:
                fn(eng)

        @block.gpsimd
        def _(eng):
            for fn in sp.streams["gpsimd"]:
                fn(eng)


def build_ffn_l0_program():
    nc = bass.Bass("TRN2", target_bir_lowering=False)
    x_in = nc.dram_tensor("x_in", [S, D], F32, kind="ExternalInput").ap()
    x_out = nc.dram_tensor("x_out", [S, D], F32, kind="ExternalOutput").ap()
    wgT = nc.dram_tensor("wgT", [D, 3072], I32, kind="ExternalInput").ap()
    wuT = nc.dram_tensor("wuT", [D, 3072], I32, kind="ExternalInput").ap()
    wdT = nc.dram_tensor("wdT", [3072, D], I32, kind="ExternalInput").ap()
    ident = nc.dram_tensor("ident", [128, 128], BF16, kind="ExternalInput").ap()
    exits = contextlib.ExitStack()
    with exits:
        c = make_ctx(nc, exits)
        sp = SP()
        emit_const_init(sp, c, ident)
        ln_ready = emit_ln(sp, c, x_in)
        xor = emit_ffn_l0(sp, c, wgT, wuT, wdT, x_in, x_out, ln_ready)
        finalize(sp, c, xor)
        build_block(nc, sp)
    return nc


def build_attn_l0a_program():
    nc = bass.Bass("TRN2", target_bir_lowering=False)
    x_in = nc.dram_tensor("x_in", [S, D], F32, kind="ExternalInput").ap()
    x_out = nc.dram_tensor("x_out", [S, D], F32, kind="ExternalOutput").ap()
    wqkv = nc.dram_tensor("wqkv", [D, 3072], F32, kind="ExternalInput").ap()
    bqkv = nc.dram_tensor("bqkv", [3072], F32, kind="ExternalInput").ap()
    wout = nc.dram_tensor("wout", [D, D], F32, kind="ExternalInput").ap()
    bout = nc.dram_tensor("bout", [D], F32, kind="ExternalInput").ap()
    ident = nc.dram_tensor("ident", [128, 128], BF16, kind="ExternalInput").ap()
    mask = nc.dram_tensor("mask_l0a", [128, 384], F32, kind="ExternalInput").ap()
    exits = contextlib.ExitStack()
    with exits:
        c = make_ctx(nc, exits)
        c.mask_l0a_dram = mask
        c._wv_war = 0
        c._wqk_war = {}
        sp = SP()
        emit_const_init(sp, c, ident)
        ln_ready = emit_ln(sp, c, x_in)
        xor = emit_attn_l0a(sp, c, wqkv, bqkv, wout, bout, x_in, x_out, ln_ready)
        finalize(sp, c, xor)
        build_block(nc, sp)
    return nc


def build_attn_l0b_program():
    nc = bass.Bass("TRN2", target_bir_lowering=False)
    x_in = nc.dram_tensor("x_in", [S, D], F32, kind="ExternalInput").ap()
    x_out = nc.dram_tensor("x_out", [S, D], F32, kind="ExternalOutput").ap()
    wqkvT = nc.dram_tensor("wqkvT", [D, 3072], F32, kind="ExternalInput").ap()
    woutT = nc.dram_tensor("woutT", [D, D], F32, kind="ExternalInput").ap()
    b_out = nc.dram_tensor("b_out", [D], F32, kind="ExternalInput").ap()
    ident = nc.dram_tensor("ident", [128, 128], BF16, kind="ExternalInput").ap()
    mask = nc.dram_tensor("mask_l0b", [128, 256], F32, kind="ExternalInput").ap()
    exits = contextlib.ExitStack()
    with exits:
        c = make_ctx(nc, exits)
        c.mask_l0b_dram = mask
        c._wv_war = 0
        c._wqk_war = {}
        sp = SP()
        emit_const_init(sp, c, ident)
        ln_ready = emit_ln(sp, c, x_in)
        xor = emit_attn_l0b(sp, c, wqkvT, woutT, b_out, x_in, x_out, ln_ready)
        finalize(sp, c, xor)
        build_block(nc, sp)
    return nc


def build_ffn_l1_program():
    nc = bass.Bass("TRN2", target_bir_lowering=False)
    x_in = nc.dram_tensor("x_in", [S, D], F32, kind="ExternalInput").ap()
    x_out = nc.dram_tensor("x_out", [S, D], F32, kind="ExternalOutput").ap()
    w1T = nc.dram_tensor("w1T", [D, 4096], F32, kind="ExternalInput").ap()
    w2T = nc.dram_tensor("w2T", [D, 4096], F32, kind="ExternalInput").ap()
    h1T = nc.dram_tensor("h1T", [4096, 4096], F32, kind="ExternalInput").ap()
    h2T = nc.dram_tensor("h2T", [4096, 4096], F32, kind="ExternalInput").ap()
    w3T = nc.dram_tensor("w3T", [4096, D], F32, kind="ExternalInput").ap()
    s1d = nc.dram_tensor("s1", [4096], F32, kind="ExternalInput").ap()
    s2d = nc.dram_tensor("s2", [4096], F32, kind="ExternalInput").ap()
    h1sd = nc.dram_tensor("h1s", [4096], F32, kind="ExternalInput").ap()
    h2sd = nc.dram_tensor("h2s", [4096], F32, kind="ExternalInput").ap()
    w3sd = nc.dram_tensor("w3s", [D], F32, kind="ExternalInput").ap()
    ident = nc.dram_tensor("ident", [128, 128], BF16, kind="ExternalInput").ap()
    exits = contextlib.ExitStack()
    with exits:
        c = make_ctx(nc, exits)
        sp = SP()
        emit_const_init(sp, c, ident)
        ln_ready = emit_ln(sp, c, x_in)
        xor = emit_ffn_l1(sp, c, w1T, w2T, h1T, h2T, w3T, s1d, s2d, h1sd, h2sd,
                          w3sd, ident, x_in, x_out, ln_ready)
        finalize(sp, c, xor)
        build_block(nc, sp)
    return nc


def build_attn_l1a_program():
    nc = bass.Bass("TRN2", target_bir_lowering=False)
    x_in = nc.dram_tensor("x_in", [S, D], F32, kind="ExternalInput").ap()
    x_out = nc.dram_tensor("x_out", [S, D], F32, kind="ExternalOutput").ap()
    wq = nc.dram_tensor("wq", [D, D], F32, kind="ExternalInput").ap()
    wk = nc.dram_tensor("wk", [D, D], F32, kind="ExternalInput").ap()
    wv = nc.dram_tensor("wv", [D, D], F32, kind="ExternalInput").ap()
    wproj = nc.dram_tensor("wproj", [D, D], F32, kind="ExternalInput").ap()
    bq = nc.dram_tensor("bq", [D], F32, kind="ExternalInput").ap()
    bproj = nc.dram_tensor("bproj", [D], F32, kind="ExternalInput").ap()
    wr = nc.dram_tensor("wr", [D, 160], F32, kind="ExternalInput").ap()
    br = nc.dram_tensor("br", [160], F32, kind="ExternalInput").ap()
    bnd_rev = nc.dram_tensor("bnd_rev", [10, 1024], F32, kind="ExternalInput").ap()
    e_scr = nc.dram_tensor("e_scr", [4, 128, 1664], F32, kind="Internal").ap()
    ident = nc.dram_tensor("ident", [128, 128], BF16, kind="ExternalInput").ap()
    exits = contextlib.ExitStack()
    with exits:
        c = make_ctx(nc, exits)
        c._wv_war = 0
        c._wqk_war = {}
        sp = SP()
        emit_const_init(sp, c, ident)
        ln_ready = emit_ln(sp, c, x_in)
        xor = emit_attn_l1a(sp, c, wq, bq, wk, wv, wproj, bproj, wr, br, bnd_rev,
                            e_scr, x_in, x_out, ln_ready)
        finalize(sp, c, xor)
        build_block(nc, sp)
    return nc


def build_attn_l1b_program():
    nc = bass.Bass("TRN2", target_bir_lowering=False)
    x_in = nc.dram_tensor("x_in", [S, D], F32, kind="ExternalInput").ap()
    x_out = nc.dram_tensor("x_out", [S, D], F32, kind="ExternalOutput").ap()
    wqT = nc.dram_tensor("wqT", [D, D], F32, kind="ExternalInput").ap()
    wkT = nc.dram_tensor("wkT", [D, D], F32, kind="ExternalInput").ap()
    wvT = nc.dram_tensor("wvT", [D, D], F32, kind="ExternalInput").ap()
    woT = nc.dram_tensor("woT", [D, D], F32, kind="ExternalInput").ap()
    bq = nc.dram_tensor("bq", [D], F32, kind="ExternalInput").ap()
    bk = nc.dram_tensor("bk", [D], F32, kind="ExternalInput").ap()
    bv = nc.dram_tensor("bv", [D], F32, kind="ExternalInput").ap()
    bo = nc.dram_tensor("bo", [D], F32, kind="ExternalInput").ap()
    t_all = nc.dram_tensor("t_all", [16, 128, 4224], F32, kind="ExternalInput").ap()
    ident = nc.dram_tensor("ident", [128, 128], BF16, kind="ExternalInput").ap()
    exits = contextlib.ExitStack()
    with exits:
        c = make_ctx(nc, exits)
        c._wv_war = 0
        c._wqk_war = {}
        sp = SP()
        emit_const_init(sp, c, ident)
        ln_ready = emit_ln(sp, c, x_in)
        xor = emit_attn_l1b(sp, c, wqT, bq, wkT, bk, wvT, bv, woT, bo, t_all,
                            x_in, x_out, ln_ready)
        finalize(sp, c, xor)
        build_block(nc, sp)
    return nc


def t_all_np(pos_bias):
    H = pos_bias.shape[0]
    r = np.arange(128)[:, None]
    cc = np.arange(4224)[None, :]
    idx = 4094 - cc + r
    sel = np.clip(idx, 0, 4094)
    t = pos_bias[:, sel]
    t[:, (idx < 0) | (idx > 4094)] = 0.0
    return np.ascontiguousarray(t.astype(np.float32))


def mask_l0b_np():
    m = np.zeros((128, 256), np.float32)
    i = np.arange(128)[:, None]
    j = np.arange(256)[None, :]
    d = i - (j - 128)
    valid = (d >= 0) & (d <= 128)
    m[~valid] = NEG
    return m


def mask_l0a_np():
    m = np.zeros((128, 384), np.float32)
    i = np.arange(128)[:, None]
    j = np.arange(384)[None, :]
    kq = (j - 128) - i
    m[np.abs(kq) > 128] = NEG
    return m


def _ident_bf16():
    return np.eye(128, dtype=ml_dtypes.bfloat16)


def build_fused_program():
    nc = bass.Bass("TRN2", target_bir_lowering=False)
    I = lambda n, sh, dt=F32: nc.dram_tensor(n, sh, dt, kind="ExternalInput").ap()
    x_in = I("x_in", [S, D])
    x_out = nc.dram_tensor("x_out", [S, D], F32, kind="ExternalOutput").ap()
    xa = nc.dram_tensor("xa", [S, D], F32, kind="Internal").ap()
    xb = nc.dram_tensor("xb", [S, D], F32, kind="Internal").ap()
    e_scr = nc.dram_tensor("e_scr", [4, 128, 1664], F32, kind="Internal").ap()
    ident = I("ident", [128, 128], BF16)
    l0a = dict(wqkv=I("l0a_wqkv", [D, 3072]), bqkv=I("l0a_bqkv", [3072]),
               wout=I("l0a_wout", [D, D]), bout=I("l0a_bout", [D]),
               mask=I("mask_l0a", [128, 384]))
    f0 = dict(wgT=I("f0_wgT", [D, 3072], I32), wuT=I("f0_wuT", [D, 3072], I32),
              wdT=I("f0_wdT", [3072, D], I32))
    l0b = dict(wqkvT=I("l0b_wqkvT", [D, 3072]), woutT=I("l0b_woutT", [D, D]),
               b_out=I("l0b_bout", [D]), mask=I("mask_l0b", [128, 256]))
    l1a = dict(wq=I("l1a_wq", [D, D]), wk=I("l1a_wk", [D, D]), wv=I("l1a_wv", [D, D]),
               wproj=I("l1a_wproj", [D, D]), bq=I("l1a_bq", [D]),
               bproj=I("l1a_bproj", [D]), wr=I("l1a_wr", [D, 160]),
               br=I("l1a_br", [160]), bnd_rev=I("l1a_bnd_rev", [10, 1024]))
    f1 = dict(w1T=I("f1_w1T", [D, 4096]), w2T=I("f1_w2T", [D, 4096]),
              h1T=I("f1_h1T", [4096, 4096]), h2T=I("f1_h2T", [4096, 4096]),
              w3T=I("f1_w3T", [4096, D]), s1=I("f1_s1", [4096]), s2=I("f1_s2", [4096]),
              h1s=I("f1_h1s", [4096]), h2s=I("f1_h2s", [4096]), w3s=I("f1_w3s", [D]))
    l1b = dict(wqT=I("l1b_wqT", [D, D]), wkT=I("l1b_wkT", [D, D]),
               wvT=I("l1b_wvT", [D, D]), woT=I("l1b_woT", [D, D]),
               bq=I("l1b_bq", [D]), bk=I("l1b_bk", [D]), bv=I("l1b_bv", [D]),
               bo=I("l1b_bo", [D]), t_all=I("t_all", [16, 128, 4224]))
    f0c = dict(wgT=nc.dram_tensor("c_f0g", [D, 3072], BF16, kind="Internal").ap(),
               wuT=nc.dram_tensor("c_f0u", [D, 3072], BF16, kind="Internal").ap(),
               wdT=nc.dram_tensor("c_f0d", [3072, D], BF16, kind="Internal").ap())
    f1c = dict(w1T=nc.dram_tensor("c_f1w1", [D, 4096], BF16, kind="Internal").ap(),
               w2T=nc.dram_tensor("c_f1w2", [D, 4096], BF16, kind="Internal").ap(),
               h1T=nc.dram_tensor("c_f1h1", [4096, 4096], BF16, kind="Internal").ap(),
               h2T=nc.dram_tensor("c_f1h2", [4096, 4096], BF16, kind="Internal").ap(),
               w3T=nc.dram_tensor("c_f1w3", [4096, D], BF16, kind="Internal").ap())
    exits = contextlib.ExitStack()
    with exits:
        c = make_ctx(nc, exits)
        c.mask_l0a_dram = l0a["mask"]
        c.mask_l0b_dram = l0b["mask"]
        sp = SP()
        emit_const_init(sp, c, ident)
        prep0 = emit_prepass(sp, c, [(f0c["wgT"], f0["wgT"]), (f0c["wuT"], f0["wuT"]),
                                     (f0c["wdT"], f0["wdT"])])
        chain = [x_in, xa, xb, xa, xb, xa, xb, xa, x_out]
        state = None
        prep1 = None

        def boundary():
            c.entry_gates = [("mm", c.S("mm").n), ("ev", c.S("ev").n),
                             ("ac", c.S("ac").n)]
            c._wv_war = 0
            c._wqk_war = {}

        for i, sub in enumerate(["l0a", "ffn0", "l0b", "ffn0",
                                 "l1a", "ffn1", "l1b", "ffn1"]):
            src, dst = chain[i], chain[i + 1]
            boundary()
            lr = emit_ln(sp, c, src, x_tile_ready=state)
            if sub == "l0a":
                state = emit_attn_l0a(sp, c, l0a["wqkv"], l0a["bqkv"], l0a["wout"],
                                      l0a["bout"], src, dst, lr)
                prep1 = emit_prepass(sp, c, [(f1c[k], f1[k]) for k in
                                             ("w1T", "w2T", "h1T", "h2T", "w3T")],
                                     sem="prep1", defer=True)
            elif sub == "ffn0":
                state = emit_ffn_l0(sp, c, f0c["wgT"], f0c["wuT"], f0c["wdT"],
                                    src, dst, lr, prep_gate=prep0)
            elif sub == "l0b":
                state = emit_attn_l0b(sp, c, l0b["wqkvT"], l0b["woutT"],
                                      l0b["b_out"], src, dst, lr)
            elif sub == "l1a":
                state = emit_attn_l1a(sp, c, l1a["wq"], l1a["bq"], l1a["wk"],
                                      l1a["wv"], l1a["wproj"], l1a["bproj"],
                                      l1a["wr"], l1a["br"], l1a["bnd_rev"],
                                      e_scr, src, dst, lr)
            elif sub == "ffn1":
                state = emit_ffn_l1(sp, c, f1c["w1T"], f1c["w2T"], f1c["h1T"],
                                    f1c["h2T"], f1c["w3T"], f1["s1"], f1["s2"],
                                    f1["h1s"], f1["h2s"], f1["w3s"], ident,
                                    src, dst, lr, prep_gate=prep1)
            elif sub == "l1b":
                state = emit_attn_l1b(sp, c, l1b["wqT"], l1b["bq"], l1b["wkT"],
                                      l1b["bk"], l1b["wvT"], l1b["bv"],
                                      l1b["woT"], l1b["bo"], l1b["t_all"],
                                      src, dst, lr)
        finalize(sp, c, state)
        build_block(nc, sp)
    return nc


def _fused_in_map(inputs):
    T = np.ascontiguousarray
    m = dict(ident=_ident_bf16(), mask_l0a=mask_l0a_np(), mask_l0b=mask_l0b_np())
    m.update(l0a_wqkv=inputs["L0_attn_a_Wqkv"], l0a_bqkv=inputs["L0_attn_a_bqkv"],
             l0a_wout=inputs["L0_attn_a_Wout"], l0a_bout=inputs["L0_attn_a_bout"])
    m.update(f0_wgT=T(inputs["L0_ffn_w_gate"].T), f0_wuT=T(inputs["L0_ffn_w_up"].T),
             f0_wdT=T(inputs["L0_ffn_w_down"].T))
    m.update(l0b_wqkvT=T(inputs["L0_attn_b_w_qkv"].T),
             l0b_woutT=T(inputs["L0_attn_b_w_out"].T),
             l0b_bout=inputs["L0_attn_b_b_out"])
    m.update(l1a_wq=inputs["L1_attn_a_Wq"], l1a_wk=inputs["L1_attn_a_Wk"],
             l1a_wv=inputs["L1_attn_a_Wv"], l1a_wproj=inputs["L1_attn_a_Wproj"],
             l1a_bq=inputs["L1_attn_a_bq"], l1a_bproj=inputs["L1_attn_a_bproj"],
             l1a_wr=inputs["L1_attn_a_Wr"], l1a_br=inputs["L1_attn_a_br"],
             l1a_bnd_rev=T(inputs["L1_attn_a_b_nd"][:, ::-1]))
    m.update(f1_w1T=T(inputs["L1_ffn_w1_w"].T), f1_w2T=T(inputs["L1_ffn_w2_w"].T),
             f1_h1T=T(inputs["L1_ffn_h1_w"].T), f1_h2T=T(inputs["L1_ffn_h2_w"].T),
             f1_w3T=T(inputs["L1_ffn_w3_w"].T),
             f1_s1=inputs["L1_ffn_w1_s"], f1_s2=inputs["L1_ffn_w2_s"],
             f1_h1s=inputs["L1_ffn_h1_s"], f1_h2s=inputs["L1_ffn_h2_s"],
             f1_w3s=inputs["L1_ffn_w3_s"])
    m.update(l1b_wqT=T(inputs["L1_attn_b_Wq"].T), l1b_wkT=T(inputs["L1_attn_b_Wk"].T),
             l1b_wvT=T(inputs["L1_attn_b_Wv"].T), l1b_woT=T(inputs["L1_attn_b_Wo"].T),
             l1b_bq=inputs["L1_attn_b_bq"], l1b_bk=inputs["L1_attn_b_bk"],
             l1b_bv=inputs["L1_attn_b_bv"], l1b_bo=inputs["L1_attn_b_bo"],
             t_all=t_all_np(inputs["L1_attn_b_pos_bias"]))
    return m


def kernel_fused(**inputs):
    B = 8
    x = np.ascontiguousarray(inputs["L0_attn_a_x"].astype(np.float32, copy=False))
    nc = _get_program("fused", build_fused_program)
    common = _fused_in_map(inputs)
    in_maps = [dict(common, x_in=x[b]) for b in range(B)]
    res = run_bass_kernel_spmd(nc, in_maps, core_ids=list(range(B)))
    return np.stack([np.asarray(res.results[b]["x_out"]) for b in range(B)]
                    ).astype(np.float32)


_PROGRAMS = {}


def _get_program(name, builder):
    if name not in _PROGRAMS:
        _PROGRAMS[name] = builder()
    return _PROGRAMS[name]


def _prep_host(inputs):
    T = np.ascontiguousarray
    h = {}
    h["ident"] = _ident_bf16()
    h["l0a"] = dict(
        wqkv=inputs["L0_attn_a_Wqkv"], bqkv=inputs["L0_attn_a_bqkv"],
        wout=inputs["L0_attn_a_Wout"], bout=inputs["L0_attn_a_bout"],
        mask_l0a=mask_l0a_np(), ident=h["ident"])
    h["ffn0"] = dict(
        wgT=T(inputs["L0_ffn_w_gate"].T), wuT=T(inputs["L0_ffn_w_up"].T),
        wdT=T(inputs["L0_ffn_w_down"].T), ident=h["ident"])
    h["l0b"] = dict(
        wqkvT=T(inputs["L0_attn_b_w_qkv"].T), woutT=T(inputs["L0_attn_b_w_out"].T),
        b_out=inputs["L0_attn_b_b_out"], mask_l0b=mask_l0b_np(), ident=h["ident"])
    h["l1a"] = dict(
        wq=inputs["L1_attn_a_Wq"], wk=inputs["L1_attn_a_Wk"],
        wv=inputs["L1_attn_a_Wv"], wproj=inputs["L1_attn_a_Wproj"],
        bq=inputs["L1_attn_a_bq"], bproj=inputs["L1_attn_a_bproj"],
        wr=inputs["L1_attn_a_Wr"], br=inputs["L1_attn_a_br"],
        bnd_rev=T(inputs["L1_attn_a_b_nd"][:, ::-1]), ident=h["ident"])
    h["ffn1"] = dict(
        w1T=T(inputs["L1_ffn_w1_w"].T), w2T=T(inputs["L1_ffn_w2_w"].T),
        h1T=T(inputs["L1_ffn_h1_w"].T), h2T=T(inputs["L1_ffn_h2_w"].T),
        w3T=T(inputs["L1_ffn_w3_w"].T),
        s1=inputs["L1_ffn_w1_s"], s2=inputs["L1_ffn_w2_s"],
        h1s=inputs["L1_ffn_h1_s"], h2s=inputs["L1_ffn_h2_s"],
        w3s=inputs["L1_ffn_w3_s"], ident=h["ident"])
    h["l1b"] = dict(
        wqT=T(inputs["L1_attn_b_Wq"].T), wkT=T(inputs["L1_attn_b_Wk"].T),
        wvT=T(inputs["L1_attn_b_Wv"].T), woT=T(inputs["L1_attn_b_Wo"].T),
        bq=inputs["L1_attn_b_bq"], bk=inputs["L1_attn_b_bk"],
        bv=inputs["L1_attn_b_bv"], bo=inputs["L1_attn_b_bo"],
        t_all=t_all_np(inputs["L1_attn_b_pos_bias"]), ident=h["ident"])
    return h


_BUILDERS = dict(
    l0a=("l0a", lambda: build_attn_l0a_program()),
    ffn0=("ffn0", lambda: build_ffn_l0_program()),
    l0b=("l0b", lambda: build_attn_l0b_program()),
    l1a=("l1a", lambda: build_attn_l1a_program()),
    ffn1=("ffn1", lambda: build_ffn_l1_program()),
    l1b=("l1b", lambda: build_attn_l1b_program()),
)


def kernel(**inputs):
    return kernel_fused(**inputs)
